# version 12
# baseline (speedup 1.0000x reference)
import numpy as np
import ml_dtypes
from contextlib import ExitStack

import concourse.bass as bass
import concourse.mybir as mybir
from concourse.bass_utils import run_bass_kernel_spmd

F32 = mybir.dt.float32
BF16 = mybir.dt.bfloat16
I32 = mybir.dt.int32
OP = mybir.AluOpType
AF = mybir.ActivationFunctionType
AX = mybir.AxisListType

D = 2048
TC = 1024
NH = 512
NQ = 256
NB = NQ // 128
NCC = 16
SQ2048 = float(np.sqrt(2048.0))
NW = 8192.0 * 1024.0

DBG = []

ENGS = ("tensor", "vector", "scalar", "sync", "gpsimd")


class St:
    def __init__(self):
        self.c = {k: 0 for k in ("t", "v", "s", "wg", "ws")}
        self.ops = None
        self.sems = None
        self.bank = {k: [] for k in (0, 1, 2, 3, "s1", "s2", "mi", "ha")}

    def begin(self):
        self.ops = {e: [] for e in ENGS}

    def inc(self, k, n=1):
        self.c[k] += n
        return self.c[k]

    def val(self, k):
        return self.c[k]

    def bw(self, *keys):
        out = []
        for k in keys:
            out.extend(e for e in self.bank[k] if e[1] > 0)
        return out

    def bset(self, key, *edges):
        self.bank[key] = list(edges)


def flush(nc, st):
    ops = st.ops
    st.ops = None
    with nc.Block() as blk:
        for eng in ENGS:
            lst = ops[eng]
            if not lst:
                continue

            def mk(lst):
                def body(e):
                    for f in lst:
                        f(e)
                return body

            getattr(blk, eng)(mk(lst))


def op(st, eng, meth, *args, waits=(), inc=None, incn=1, **kw):
    sems = st.sems
    val = st.inc(inc, incn) if inc else None
    waits = [w for w in waits if w is not None and w[1] > 0]

    def f(e):
        for sname, v in waits:
            e.wait_ge(sems[sname], v)
        inst = getattr(e, meth)(*args, **kw)
        if inc:
            inst.then_inc(sems[inc], incn)

    st.ops[eng].append(f)
    return val


def opn(st, eng, meth, *args, waits=(), **kw):
    return op(st, eng, meth, *args,
              waits=list(waits) + [("v", st.val("v"))], **kw)


def wt(st, eng, sname, v):
    sems = st.sems

    def f(e):
        e.wait_ge(sems[sname], v)

    st.ops[eng].append(f)


def build(debug=(), stop=None):
    nc = bass.Bass("TRN2", target_bir_lowering=False,
                   detect_race_conditions=False)
    st = St()

    def _stopnow():
        st.begin()
        wt(st, "sync", "wsc", st.val("wsc"))
        flush(nc, st)


    d_x = nc.dram_tensor("x_t", [D, TC], F32, kind="ExternalInput").ap()
    d_halo = nc.dram_tensor("halo_p", [128, 48], F32, kind="ExternalInput").ap()
    d_win = nc.dram_tensor("win_p", [64, 128, 2048], F32, kind="ExternalInput").ap()
    d_wout = nc.dram_tensor("wout_p", [16, 128, 4096], F32, kind="ExternalInput").ap()
    d_wm1 = nc.dram_tensor("wm1_p", [64, 128, 2048], F32, kind="ExternalInput").ap()
    d_wm2 = nc.dram_tensor("wm2_p", [16, 128, 8192], F32, kind="ExternalInput").ap()
    d_wup = nc.dram_tensor("wup_p", [32, 128, 2048], F32, kind="ExternalInput").ap()
    d_wdn = nc.dram_tensor("wdn_p", [16, 128, 4096], F32, kind="ExternalInput").ap()
    d_h128 = nc.dram_tensor("c_h128", [128, 128], BF16, kind="ExternalInput").ap()
    d_idf = nc.dram_tensor("c_identf", [128, 128], F32, kind="ExternalInput").ap()
    d_onf = nc.dram_tensor("c_onesf", [128, 128], F32, kind="ExternalInput").ap()
    d_onb = nc.dram_tensor("c_onesb", [128, 8], BF16, kind="ExternalInput").ap()
    d_cw = nc.dram_tensor("c_convw", [128, 128], F32, kind="ExternalInput").ap()
    d_cb = nc.dram_tensor("c_convb", [128, 32], F32, kind="ExternalInput").ap()
    d_out = nc.dram_tensor("out", [D, TC], F32, kind="ExternalOutput").ap()
    d_su = nc.dram_tensor("sp_u", [128, NCC * TC], F32, kind="Internal").ap()
    d_sxm = nc.dram_tensor("sp_xmid", [128, NCC * TC], F32, kind="Internal").ap()
    d_sxp = nc.dram_tensor("sp_xpre", [128, NCC * TC], F32, kind="Internal").ap()
    d_sx1 = nc.dram_tensor("sp_x1", [128, NCC * TC], BF16, kind="Internal").ap()
    FP8 = mybir.dt.float8e4
    d_stup = nc.dram_tensor("sp_tup", [128, 32 * 2048], FP8, kind="Internal").ap()
    d_stdn = nc.dram_tensor("sp_tdn", [128, 16 * 4096], FP8, kind="Internal").ap()

    dbg_outs = {}
    for nm, shape, dt in [
        ("h", [128, NCC * TC], BF16), ("y", [128, 32 * TC], BF16),
        ("xmid", [128, NCC * TC], F32), ("u", [128, NCC * TC], F32),
        ("h2", [128, NCC * TC], BF16), ("xpre", [128, NCC * TC], F32),
        ("x1", [128, NCC * TC], BF16), ("v1", [128, NCC * NQ], F32),
        ("q1", [128, NCC * NQ], BF16), ("g", [128, 32 * NQ], F32),
        ("q2", [128, 32 * NQ], BF16), ("x2pre", [128, NCC * NQ], F32),
        ("gam", [1, 8], F32), ("c1r", [1, NQ], F32), ("alr", [1, NQ], F32),
        ("colz", [128, 24], F32),
    ]:
        if nm in debug:
            dbg_outs[nm] = nc.dram_tensor("dbg_" + nm, shape, dt,
                                          kind="ExternalOutput").ap()

    es = ExitStack()
    with es:
        semnames = ("t", "v", "s", "wsc", "wx0", "wx1", "wg0", "wg1",
                    "wm0", "wm1", "wu0", "wu1", "wgg0", "wgg1", "wp0", "wp1",
                    "wq", "wf0", "wf1", "wt0", "wt1", "wo0", "wo1")
        st.sems = {k: es.enter_context(nc.semaphore("sem_" + k))
                   for k in semnames}
        st.c = {k: 0 for k in semnames}

        sbL = lambda n, sh, dt: es.enter_context(nc.sbuf_tensor(n, sh, dt, side="left"))
        c_h128 = sbL("s_h128", [128, 128], BF16)
        c_idf = sbL("c_idf", [128, 128], F32)
        c_onf = sbL("c_onf", [128, 128], F32)
        c_onb = sbL("c_onb", [128, 8], BF16)
        c_cw = sbL("c_cw", [128, 128], F32)
        c_cb = sbL("c_cb", [128, 32], F32)
        halo_x = sbL("halo_x", [128, 48], F32)
        halo_h = sbL("halo_h", [128, 48], BF16)
        halo_sq = sbL("halo_sq", [128, 48], BF16)
        bc = sbL("bc", [128, TC], F32)
        bc2 = sbL("bc2", [128, TC], F32)
        rows = sbL("rows", [1, TC], F32)
        rows2 = sbL("rows2", [1, TC], F32)
        rows3 = sbL("rows3", [1, TC], F32)
        cols = sbL("cols", [128, 24], F32)
        gcols = sbL("gcols", [128, 96], F32)
        gtmp = sbL("gtmp", [128, 2], F32)
        grow = sbL("grow", [1, 8], F32)
        gcol = sbL("gcol", [128, 8], F32)
        sqs = [sbL(f"sqs{i}", [128, TC], BF16) for i in range(2)]

        ps = [es.enter_context(nc.psum_tensor(f"p{i}", [128, NH], F32))
              for i in range(4)]
        pstat1 = es.enter_context(nc.psum_tensor("pstat1", [128, NH], F32))
        pstat2 = es.enter_context(nc.psum_tensor("pstat2", [128, NH], F32))
        pmisc = es.enter_context(nc.psum_tensor("pmisc", [128, NH], F32))
        phalo = es.enter_context(nc.psum_tensor("phalo", [128, NH], F32))

        arY = nc.sbuf_tensor("arY", [128, 32 * TC], BF16, side="right")
        Y = arY.__enter__()
        arH = nc.sbuf_tensor("arH", [128, NCC * TC], BF16, side="right")
        H = arH.__enter__()
        ph1 = ExitStack()
        xrb = [ph1.enter_context(nc.sbuf_tensor(f"xrb{i}", [128, TC], F32,
                                                side="left")) for i in range(2)]

        st.begin()
        for dst, src in [(c_h128, d_h128), (c_idf, d_idf), (c_onf, d_onf),
                         (c_onb, d_onb), (c_cw, d_cw), (c_cb, d_cb),
                         (halo_x, d_halo)]:
            consts_done = op(st, "sync", "dma_start", dst[:, :], src,
                             inc="wsc", incn=16)
        sq_t = []
        sq_s = []
        for cc in range(NCC):
            xld = op(st, "sync", "dma_start", xrb[cc % 2][:, :],
                     d_x[cc * 128:(cc + 1) * 128, :],
                     waits=[("s", sq_s[cc - 2])] if cc >= 2 else [],
                     inc=f"wx{cc % 2}", incn=16)
            sv = op(st, "scalar", "activation", sqs[cc % 2][:, :],
                    xrb[cc % 2][:, :], AF.Square,
                    waits=[(f"wx{cc % 2}", xld)] +
                          ([("t", sq_t[cc - 2])] if cc >= 2 else []),
                    inc="s")
            sq_s.append(sv)
            op(st, "tensor", "matmul", pstat1[0:1, :], c_onb[:, 0:1],
               sqs[cc % 2][:, 0:NH], start=(cc == 0), stop=(cc == NCC - 1),
               waits=[("s", sv), ("wsc", consts_done)])
            tv = op(st, "tensor", "matmul", pstat2[0:1, :], c_onb[:, 0:1],
                    sqs[cc % 2][:, NH:TC], start=(cc == 0), stop=(cc == NCC - 1),
                    inc="t")
            sq_t.append(tv)
        svh = op(st, "scalar", "activation", halo_sq[:, :], halo_x[:, :],
                 AF.Square, waits=[("wsc", consts_done)], inc="s")
        for cc in range(NCC):
            op(st, "tensor", "matmul", pmisc[0:1, 0:3], c_onb[:, 0:1],
               halo_sq[:, cc * 3:(cc + 1) * 3], start=(cc == 0),
               stop=(cc == NCC - 1), waits=[("s", svh)] if cc == 0 else [],
               inc="t" if cc == NCC - 1 else None)

        v1 = opn(st, "vector", "tensor_scalar", rows[0:1, 0:NH], pstat1[0:1, :],
                1e-30, None, op0=OP.max, waits=[("t", st.val("t"))], inc="v")
        v1 = opn(st, "vector", "tensor_scalar", rows[0:1, NH:TC], pstat2[0:1, :],
                1e-30, None, op0=OP.max, inc="v")
        v1 = opn(st, "vector", "tensor_scalar", rows3[0:1, 0:3], pmisc[0:1, 0:3],
                1e-30, None, op0=OP.max, inc="v")
        v1 = opn(st, "vector", "reciprocal", rows2[0:1, 0:TC], rows[0:1, 0:TC],
                inc="v")
        v1 = opn(st, "vector", "reciprocal", rows3[0:1, 4:7], rows3[0:1, 0:3],
                inc="v")
        s1 = op(st, "scalar", "activation", rows[0:1, 0:TC], rows2[0:1, 0:TC],
                AF.Sqrt, scale=2048.0, waits=[("v", v1)], inc="s")
        s1 = op(st, "scalar", "activation", rows3[0:1, 0:3], rows3[0:1, 4:7],
                AF.Sqrt, scale=2048.0, inc="s")
        op(st, "tensor", "matmul", pstat1[:, :], c_onf[0:1, :], rows[0:1, 0:NH],
           start=True, stop=True, waits=[("s", s1)])
        op(st, "tensor", "matmul", pstat2[:, :], c_onf[0:1, :], rows[0:1, NH:TC],
           start=True, stop=True)
        tb = op(st, "tensor", "matmul", pmisc[:, 0:3], c_onf[0:1, :],
                rows3[0:1, 0:3], start=True, stop=True, inc="t")
        sb = op(st, "scalar", "copy", bc[:, 0:NH], pstat1[:, :],
                waits=[("t", tb)], inc="s")
        sb = op(st, "scalar", "copy", bc[:, NH:TC], pstat2[:, :], inc="s")
        sb = op(st, "scalar", "copy", bc2[:, 0:3], pmisc[:, 0:3], inc="s")
        st.bset("s1", ("s", sb)); st.bset("s2", ("s", sb)); st.bset("mi", ("s", sb))
        h_v = 0
        for cc in range(NCC):
            rl = op(st, "sync", "dma_start", xrb[cc % 2][:, :],
                    d_x[cc * 128:(cc + 1) * 128, :],
                    waits=([("s", sq_s[NCC - 1])] if cc < 2 else []) +
                          ([("v", h_v)] if cc >= 2 else []),
                    inc=f"wx{cc % 2}", incn=16)
            h_v = op(st, "vector", "tensor_mul", H[:, cc * TC:(cc + 1) * TC],
                     xrb[cc % 2][:, :], bc[:, :],
                     waits=[(f"wx{cc % 2}", rl)] + ([("s", sb)] if cc == 0 else []),
                     inc="v")
        for cc in range(NCC):
            h_v = op(st, "vector", "tensor_mul", halo_h[:, cc * 3:(cc + 1) * 3],
                     halo_x[:, cc * 3:(cc + 1) * 3], bc2[:, 0:3], inc="v")
        h_done = h_v
        if "h" in debug:
            op(st, "sync", "dma_start", dbg_outs["h"], H[:, :],
               waits=[("v", h_done)], inc="wsc", incn=16)
        flush(nc, st)
        ph1.close()
        if stop == 1:
            _stopnow(); return nc

        phGam = ExitStack()
        gwb5 = [phGam.enter_context(nc.sbuf_tensor(f"gwb5_{i}", [128, 1024], F32,
                                                   side="left")) for i in range(2)]
        gab5 = phGam.enter_context(nc.sbuf_tensor("gab5", [128, 1024], F32,
                                                  side="left"))
        gpieces = []
        for jj in range(64):
            gpieces.append((d_wup[jj // 2, :, (jj % 2) * 1024:(jj % 2 + 1) * 1024],
                            jj))
        for jj in range(32):
            gpieces.append((d_wdn[jj // 2, :, (jj % 2) * 2048:(jj % 2) * 2048 + 1024],
                            64 + jj))
            gpieces.append((d_wdn[jj // 2, :, (jj % 2) * 2048 + 1024:(jj % 2 + 1) * 2048],
                            64 + jj))
        gp_idx = [0]
        g_abs = {}
        g_last_add = [0]

        def emit_gamma_piece():
            i = gp_idx[0]
            if i >= len(gpieces):
                return
            gp_idx[0] += 1
            src_, col = gpieces[i]
            two_rounds = (i >= 64 and (i - 64) % 2 == 1)
            gv = op(st, "sync", "dma_start", gwb5[i % 2][:, :], src_,
                    waits=[g_abs.get(i - 2)], inc=f"wgg{i % 2}", incn=16)
            if not two_rounds:
                g_abs[i] = ("s", op(st, "scalar", "activation", gab5[:, :],
                                    gwb5[i % 2][:, :], AF.Abs,
                                    accum_out=gcols[:, col:col + 1],
                                    waits=[(f"wgg{i % 2}", gv)], inc="s"))
            else:
                sa = op(st, "scalar", "activation", gab5[:, :],
                        gwb5[i % 2][:, :], AF.Abs,
                        accum_out=gtmp[:, 1:2],
                        waits=[(f"wgg{i % 2}", gv), ("v", g_last_add[0])],
                        inc="s")
                g_abs[i] = ("s", sa)
                g_last_add[0] = op(st, "vector", "tensor_add",
                                   gcols[:, col:col + 1], gcols[:, col:col + 1],
                                   gtmp[:, 1:2], waits=[("s", sa)], inc="v")

        ph2 = ExitStack()
        sb2 = lambda n, sh, dt: ph2.enter_context(nc.sbuf_tensor(n, sh, dt, side="left"))
        pan1 = [sb2(f"pan1_{i}", [128, 2048], BF16) for i in range(2)]
        xpb = sb2("xpb", [128, TC + 3], F32)
        szb = sb2("szb", [128, TC], F32)
        xcb = sb2("xcb", [128, TC], F32)
        sxcb = sb2("sxcb", [128, TC], F32)

        st.begin()
        pan_free = {0: ("t", 0), 1: ("t", 0)}
        y_k = {}
        y_v = 0
        for j in range(64):
            k = j // 2
            is_xp = (j % 2 == 0)
            wgv = op(st, "gpsimd", "dma_start", pan1[j % 2][:, :], d_win[j, :, :],
                     waits=[pan_free[j % 2]], inc=f"wg{j % 2}", incn=16)
            pr = (0, 1) if is_xp else (2, 3)
            w0 = [(f"wg{j % 2}", wgv)] + st.bw(*pr)
            if j == 0:
                w0.append(("v", h_done))
            if is_xp:
                w0 += st.bw("ha")
            for c in range(NCC):
                lhs = pan1[j % 2][:, c * 128:(c + 1) * 128]
                last = (c == NCC - 1)
                op(st, "tensor", "matmul", ps[pr[0]][:, :], lhs,
                   H[:, c * TC:c * TC + NH], start=(c == 0), stop=last,
                   waits=w0 if c == 0 else [])
                if is_xp:
                    op(st, "tensor", "matmul", phalo[:, 0:3], lhs,
                       halo_h[:, c * 3:(c + 1) * 3], start=(c == 0), stop=last)
                tv = op(st, "tensor", "matmul", ps[pr[1]][:, :], lhs,
                        H[:, c * TC + NH:(c + 1) * TC], start=(c == 0), stop=last,
                        inc="t" if last else None)
            pan_free[j % 2] = ("t", tv)
            if is_xp:
                sv = op(st, "scalar", "copy", xpb[:, 3:3 + NH], ps[pr[0]][:, :],
                        waits=[("t", tv)] + ([("v", y_k[k - 1])] if k >= 1 else []),
                        inc="s")
                sv = op(st, "scalar", "copy", xpb[:, 3 + NH:3 + TC],
                        ps[pr[1]][:, :], inc="s")
                hvv = op(st, "vector", "tensor_copy", xpb[:, 0:3], phalo[:, 0:3],
                         waits=[("t", tv)] + ([("v", y_k[k - 1])] if k >= 1 else []),
                         inc="v")
                st.bset(pr[0], ("s", sv)); st.bset(pr[1], ("s", sv))
                st.bset("ha", ("v", hvv))
                cv = op(st, "vector", "tensor_scalar", xcb[:, :], xpb[:, 0:TC],
                        c_cw[:, k:k + 1], c_cb[:, k:k + 1],
                        op0=OP.mult, op1=OP.add,
                        waits=[("s", sv), ("v", hvv)], inc="v")
                for tap in range(1, 4):
                    cv = op(st, "vector", "scalar_tensor_tensor", xcb[:, :],
                            xpb[:, tap:tap + TC],
                            c_cw[:, tap * 32 + k:tap * 32 + k + 1],
                            xcb[:, :], op0=OP.mult, op1=OP.add, inc="v")
                sxv = op(st, "scalar", "activation", sxcb[:, :], xcb[:, :],
                         AF.Silu, waits=[("v", cv)], inc="s")
            else:
                zv = op(st, "scalar", "activation", szb[:, 0:NH], ps[pr[0]][:, :],
                        AF.Silu, waits=[("t", tv)], inc="s")
                zv = op(st, "scalar", "activation", szb[:, NH:TC], ps[pr[1]][:, :],
                        AF.Silu, inc="s")
                st.bset(pr[0], ("s", zv)); st.bset(pr[1], ("s", zv))
                y_v = op(st, "vector", "tensor_mul", Y[:, k * TC:(k + 1) * TC],
                         sxcb[:, :], szb[:, :], waits=[("s", zv)], inc="v")
                y_k[k] = y_v
            emit_gamma_piece()
        y_done = y_v
        if "y" in debug:
            op(st, "sync", "dma_start", dbg_outs["y"], Y[:, :],
               waits=[("v", y_done)], inc="wsc", incn=16)
        flush(nc, st)
        ph2.close()
        arH.__exit__(None, None, None)
        if stop == 2:
            _stopnow(); return nc

        arXM = nc.sbuf_tensor("arXM", [128, NCC * TC], F32, side="right")
        XM = arXM.__enter__()
        ph3 = ExitStack()
        sb3 = lambda n, sh, dt: ph3.enter_context(nc.sbuf_tensor(n, sh, dt, side="left"))
        pan2 = [sb3(f"pan2_{i}", [128, 2048], BF16) for i in range(2)]
        ubf = [sb3(f"ubf{i}", [128, TC], F32) for i in range(2)]
        xr3 = [sb3(f"xr3_{i}", [128, TC], F32) for i in range(2)]

        st.begin()
        pan_free = {0: ("t", 0), 1: ("t", 0)}
        u_sp = {}
        rm2_t = []
        xu_v = []
        for m in range(16):
            xld = op(st, "sync", "dma_start", xr3[m % 2][:, :],
                     d_x[m * 128:(m + 1) * 128, :],
                     waits=[("v", xu_v[m - 2])] if m >= 2 else [],
                     inc=f"wx{m % 2}", incn=16)
            tv = None
            pr = (0, 1) if (m % 2 == 0) else (2, 3)
            for hp in range(2):
                wgv = op(st, "gpsimd", "dma_start", pan2[hp][:, :],
                         d_wout[m, :, hp * 2048:(hp + 1) * 2048],
                         waits=[pan_free[hp]], inc=f"wg{hp}", incn=16)
                w0 = [(f"wg{hp}", wgv)] + (st.bw(*pr) if hp == 0 else [])
                if m == 0 and hp == 0:
                    w0.append(("v", y_done))
                for ci in range(16):
                    c = hp * 16 + ci
                    lhs = pan2[hp][:, ci * 128:(ci + 1) * 128]
                    first = (c == 0)
                    last = (c == 31)
                    op(st, "tensor", "matmul", ps[pr[0]][:, :], lhs,
                       Y[:, c * TC:c * TC + NH], start=first, stop=last,
                       waits=w0 if ci == 0 else [])
                    tv = op(st, "tensor", "matmul", ps[pr[1]][:, :], lhs,
                            Y[:, c * TC + NH:(c + 1) * TC], start=first,
                            stop=last, inc="t" if ci == 15 else None)
                pan_free[hp] = ("t", tv)
            wv = [("t", tv), (f"wx{m % 2}", xld)]
            if m >= 2:
                wv += [u_sp[m - 2]]
            vv = op(st, "vector", "tensor_add", XM[:, m * TC:m * TC + NH],
                    xr3[m % 2][:, 0:NH], ps[pr[0]][:, :], waits=wv, inc="v")
            vv = op(st, "vector", "tensor_add", XM[:, m * TC + NH:(m + 1) * TC],
                    xr3[m % 2][:, NH:TC], ps[pr[1]][:, :], inc="v")
            vv = op(st, "vector", "scalar_tensor_tensor", ubf[m % 2][:, 0:NH],
                    xr3[m % 2][:, 0:NH], 2.0, ps[pr[0]][:, :],
                    op0=OP.mult, op1=OP.add, inc="v")
            vv = op(st, "vector", "scalar_tensor_tensor", ubf[m % 2][:, NH:TC],
                    xr3[m % 2][:, NH:TC], 2.0, ps[pr[1]][:, :],
                    op0=OP.mult, op1=OP.add, inc="v")
            xu_v.append(vv)
            st.bset(pr[0], ("v", vv)); st.bset(pr[1], ("v", vv))
            sv = op(st, "scalar", "activation", sqs[m % 2][:, :],
                    XM[:, m * TC:(m + 1) * TC], AF.Square,
                    waits=[("v", vv)] + ([("t", rm2_t[m - 2])] if m >= 2 else []),
                    inc="s")
            op(st, "tensor", "matmul", pstat1[0:1, :], c_onb[:, 0:1],
               sqs[m % 2][:, 0:NH], start=(m == 0), stop=(m == 15),
               waits=[("s", sv)] + (st.bw("s1", "s2") if m == 0 else []))
            tv2 = op(st, "tensor", "matmul", pstat2[0:1, :], c_onb[:, 0:1],
                     sqs[m % 2][:, NH:TC], start=(m == 0), stop=(m == 15),
                     inc="t")
            rm2_t.append(tv2)
            u_sp[m] = (f"wu{m % 2}",
                       op(st, "sync", "dma_start", d_su[:, m * TC:(m + 1) * TC],
                          ubf[m % 2][:, :], waits=[("v", vv)],
                          inc=f"wu{m % 2}", incn=16))
            emit_gamma_piece()
            emit_gamma_piece()
        mm2_t = st.val("t")
        u_sp_all = [u_sp[14], u_sp[15]]
        if "xmid" in debug:
            op(st, "sync", "dma_start", dbg_outs["xmid"], XM[:, :],
               waits=[("v", st.val("v"))], inc="wsc", incn=16)
        if "u" in debug:
            op(st, "sync", "dma_start", dbg_outs["u"], d_su,
               waits=u_sp_all, inc="wsc", incn=16)
        flush(nc, st)

        ph3.close()
        if stop == 3:
            _stopnow(); return nc

        st.begin()
        while gp_idx[0] < len(gpieces):
            emit_gamma_piece()
        gv1 = opn(st, "vector", "reduce_sum", gtmp[:, 0:1], gcols[:, 0:64],
                  axis=AX.X,
                  waits=[g_abs[len(gpieces) - 1], ("v", g_last_add[0])], inc="v")
        gv1 = opn(st, "vector", "reduce_sum", gtmp[:, 1:2], gcols[:, 64:96],
                  axis=AX.X, inc="v")
        gt1 = op(st, "tensor", "matmul", pmisc[0:1, 0:2], c_onf[:, 0:1],
                 gtmp[:, 0:2], start=True, stop=True,
                 waits=[("v", gv1)] + st.bw("mi"), inc="t")
        gv2 = opn(st, "vector", "tensor_scalar", grow[0:1, 0:2], pmisc[0:1, 0:2],
                  1.0 / NW, 1e-5, op0=OP.mult, op1=OP.max,
                  waits=[("t", gt1)], inc="v")
        gv2 = opn(st, "vector", "tensor_scalar", grow[0:1, 2:3], grow[0:1, 0:1],
                  0.5, None, op0=OP.mult, inc="v")
        gv2 = opn(st, "vector", "tensor_scalar", grow[0:1, 3:4], grow[0:1, 1:2],
                  0.5, None, op0=OP.mult, inc="v")
        gv2 = opn(st, "vector", "tensor_scalar", grow[0:1, 4:5], grow[0:1, 2:3],
                  -1.0, None, op0=OP.mult, inc="v")
        gv2 = opn(st, "vector", "tensor_scalar", grow[0:1, 5:6], grow[0:1, 3:4],
                  -1.0, None, op0=OP.mult, inc="v")
        gv2 = opn(st, "vector", "tensor_scalar", grow[0:1, 6:7], grow[0:1, 0:1],
                  1.0 / (127.0 * SQ2048), None, op0=OP.mult, inc="v")
        gv2 = opn(st, "vector", "tensor_scalar", grow[0:1, 7:8], grow[0:1, 1:2],
                  1.0 / (127.0 * 64.0), None, op0=OP.mult, inc="v")
        gt2 = op(st, "tensor", "matmul", pmisc[:, 0:8], c_onf[0:1, :],
                 grow[0:1, 0:8], start=True, stop=True, waits=[("v", gv2)],
                 inc="t")
        gs2 = op(st, "scalar", "copy", gcol[:, 0:8], pmisc[:, 0:8],
                 waits=[("t", gt2)], inc="s")
        gamma_rdy = ("s", gs2)
        st.bset("mi", ("s", gs2))
        if "gam" in debug:
            op(st, "sync", "dma_start", dbg_outs["gam"], grow[0:1, :],
               waits=[("v", gv2)], inc="wsc", incn=16)
        v1 = opn(st, "vector", "tensor_scalar", rows[0:1, 0:NH], pstat1[0:1, :],
                1e-30, None, op0=OP.max, waits=[("t", mm2_t)], inc="v")
        v1 = opn(st, "vector", "tensor_scalar", rows[0:1, NH:TC], pstat2[0:1, :],
                1e-30, None, op0=OP.max, inc="v")
        v1 = opn(st, "vector", "reciprocal", rows2[0:1, 0:TC], rows[0:1, 0:TC],
                inc="v")
        s1 = op(st, "scalar", "activation", rows[0:1, 0:TC], rows2[0:1, 0:TC],
                AF.Sqrt, scale=2048.0, waits=[("v", v1)], inc="s")
        op(st, "tensor", "matmul", pstat1[:, :], c_onf[0:1, :], rows[0:1, 0:NH],
           start=True, stop=True, waits=[("s", s1)])
        tb = op(st, "tensor", "matmul", pstat2[:, :], c_onf[0:1, :],
                rows[0:1, NH:TC], start=True, stop=True, inc="t")
        sb = op(st, "scalar", "copy", bc[:, 0:NH], pstat1[:, :],
                waits=[("t", tb)], inc="s")
        sb = op(st, "scalar", "copy", bc[:, NH:TC], pstat2[:, :], inc="s")
        st.bset("s1", ("s", sb)); st.bset("s2", ("s", sb))
        flush(nc, st)
        phGam.close()

        phT = ExitStack()
        TDEP = 4
        twb = [phT.enter_context(nc.sbuf_tensor(f"twb{i}", [128, 512], F32,
                                                side="left"))
               for i in range(TDEP)]
        tgb = phT.enter_context(nc.sbuf_tensor("tgb", [128, 512], BF16,
                                               side="left"))
        tob = [phT.enter_context(nc.sbuf_tensor(f"tob{i}", [128, 512], BF16,
                                                side="left"))
               for i in range(TDEP)]
        TP = 512
        tpieces = []
        for m in range(32):
            for j in range(4):
                tpieces.append((d_wup[m, :, j * TP:(j + 1) * TP],
                                d_stup[:, m * 2048 + j * TP:m * 2048 + (j + 1) * TP],
                                True))
        for m in range(16):
            for j in range(8):
                tpieces.append((d_wdn[m, :, j * TP:(j + 1) * TP],
                                d_stdn[:, m * 4096 + j * TP:m * 4096 + (j + 1) * TP],
                                False))
        tp_idx = [0]
        tp_sp = {}
        tern_up_done = []
        tern_dn_done = []

        def emit_tern_piece():
            i = tp_idx[0]
            if i >= len(tpieces):
                return
            tp_idx[0] += 1
            src_, dst_, is_up = tpieces[i]
            b = i % TDEP
            sb_ = i % 2
            ld = op(st, "sync", "dma_start", twb[b][:, :], src_,
                    waits=[tp_sp.get(i - TDEP)], inc=f"wgg{sb_}", incn=16)
            wv = [(f"wgg{sb_}", ld)]
            if i == 0:
                wv.append(gamma_rdy)
            ng, pg = (4, 2) if is_up else (5, 3)
            op(st, "vector", "tensor_scalar", tgb[:, :], twb[b][:, :],
               gcol[:, ng:ng + 1], None, op0=OP.is_lt, waits=wv, inc="v")
            tv = op(st, "vector", "scalar_tensor_tensor", tob[b][:, :],
                    twb[b][:, :], gcol[:, pg:pg + 1], tgb[:, :],
                    op0=OP.is_gt, op1=OP.subtract,
                    waits=[tp_sp.get(i - TDEP)], inc="v")
            tp_sp[i] = (f"wt{sb_}",
                        op(st, "gpsimd", "dma_start", dst_, tob[b][:, :],
                           waits=[("v", tv)], inc=f"wt{sb_}", incn=16))
            if i == 127:
                tern_up_done.extend([tp_sp[126], tp_sp[127]])
            if i == 255:
                tern_dn_done.extend([tp_sp[254], tp_sp[255]])

        phB = ExitStack()
        B2 = phB.enter_context(nc.sbuf_tensor("arB2", [128, NCC * TC], BF16,
                                              side="left"))
        st.begin()
        h2_v = 0
        for cc in range(NCC):
            h2_v = op(st, "vector", "tensor_mul", B2[:, cc * TC:(cc + 1) * TC],
                      XM[:, cc * TC:(cc + 1) * TC], bc[:, :],
                      waits=[("s", sb)] if cc == 0 else [], inc="v")
        if "h2" in debug:
            op(st, "sync", "dma_start", dbg_outs["h2"], B2[:, :],
               waits=[("v", h2_v)], inc="wsc", incn=16)
        flush(nc, st)
        arXM.__exit__(None, None, None)
        arY.__exit__(None, None, None)
        if stop == 4:
            _stopnow(); phB.close(); return nc

        hid_cm = nc.sbuf_tensor("hidden", [128, 64 * TC], BF16, side="right")
        HID = hid_cm.__enter__()
        ph5 = ExitStack()
        sb5 = lambda n, sh, dt: ph5.enter_context(nc.sbuf_tensor(n, sh, dt, side="left"))
        pan5 = [sb5(f"pan5_{i}", [128, 1024], BF16) for i in range(2)]

        st.begin()
        pan_free = {0: ("t", 0), 1: ("t", 0)}
        hid_s = 0
        for m in range(64):
            tv = None
            pr = (0, 1) if (m % 2 == 0) else (2, 3)
            for hp in range(2):
                wgv = op(st, "gpsimd", "dma_start", pan5[hp][:, :],
                         d_wm1[m, :, hp * 1024:(hp + 1) * 1024],
                         waits=[pan_free[hp]], inc=f"wg{hp}", incn=16)
                w0 = [(f"wg{hp}", wgv)] + (st.bw(*pr) if hp == 0 else [])
                if m == 0 and hp == 0:
                    w0 += [("v", h2_v)]
                for ci in range(8):
                    c = hp * 8 + ci
                    lhs = pan5[hp][:, ci * 128:(ci + 1) * 128]
                    first = (c == 0)
                    last = (c == NCC - 1)
                    op(st, "tensor", "matmul", ps[pr[0]][:, :], lhs,
                       B2[:, c * TC:c * TC + NH], start=first, stop=last,
                       waits=w0 if ci == 0 else [])
                    tv = op(st, "tensor", "matmul", ps[pr[1]][:, :], lhs,
                            B2[:, c * TC + NH:(c + 1) * TC], start=first,
                            stop=last, inc="t" if ci == 7 else None)
                pan_free[hp] = ("t", tv)
            sv = op(st, "scalar", "activation", HID[:, m * TC:m * TC + NH],
                    ps[pr[0]][:, :], AF.Relu, waits=[("t", tv)], inc="s")
            sv = op(st, "scalar", "activation", HID[:, m * TC + NH:(m + 1) * TC],
                    ps[pr[1]][:, :], AF.Relu, inc="s")
            hid_s = sv
            st.bset(pr[0], ("s", sv)); st.bset(pr[1], ("s", sv))
            for _ in range(3):
                emit_tern_piece()
        flush(nc, st)
        ph5.close()
        phB.close()
        if stop == 5:
            _stopnow(); hid_cm.__exit__(None, None, None); return nc

        ph6 = ExitStack()
        sb6 = lambda n, sh, dt: ph6.enter_context(nc.sbuf_tensor(n, sh, dt, side="left"))
        pan6 = [sb6(f"pan6_{i}", [128, 4096], BF16) for i in range(2)]
        xq6 = [sb6(f"xq6_{i}", [128, TC], F32) for i in range(2)]
        ub6 = [sb6(f"ub6_{i}", [128, TC], F32) for i in range(2)]

        st.begin()
        pan_free = {0: ("t", 0), 1: ("t", 0)}
        ln1_t = []
        xp_sp = {}
        xpre_vs = []
        for m in range(16):
            url = op(st, "sync", "dma_start", ub6[m % 2][:, :],
                     d_su[:, m * TC:(m + 1) * TC],
                     waits=u_sp_all if m < 2 else
                     [("v", xpre_vs[m - 2])], inc=f"wu{m % 2}", incn=16)
            tv = None
            pr = (0, 1) if (m % 2 == 0) else (2, 3)
            for hp in range(2):
                wgv = op(st, "gpsimd", "dma_start", pan6[hp][:, :],
                         d_wm2[m, :, hp * 4096:(hp + 1) * 4096],
                         waits=[pan_free[hp]], inc=f"wg{hp}", incn=16)
                w0 = [(f"wg{hp}", wgv)] + (st.bw(*pr) if hp == 0 else [])
                if m == 0 and hp == 0:
                    w0 += [("s", hid_s)]
                for ci in range(32):
                    c = hp * 32 + ci
                    lhs = pan6[hp][:, ci * 128:(ci + 1) * 128]
                    first = (c == 0)
                    last = (c == 63)
                    op(st, "tensor", "matmul", ps[pr[0]][:, :], lhs,
                       HID[:, c * TC:c * TC + NH], start=first, stop=last,
                       waits=w0 if ci == 0 else [])
                    tv = op(st, "tensor", "matmul", ps[pr[1]][:, :], lhs,
                            HID[:, c * TC + NH:(c + 1) * TC], start=first,
                            stop=last, inc="t" if ci == 31 else None)
                pan_free[hp] = ("t", tv)
            wv = [("t", tv), (f"wu{m % 2}", url)]
            if m >= 2:
                wv.append(xp_sp[m - 2])
            vv = op(st, "vector", "tensor_add", xq6[m % 2][:, 0:NH],
                    ub6[m % 2][:, 0:NH], ps[pr[0]][:, :], waits=wv, inc="v")
            vv = op(st, "vector", "tensor_add", xq6[m % 2][:, NH:TC],
                    ub6[m % 2][:, NH:TC], ps[pr[1]][:, :], inc="v")
            xpre_vs.append(vv)
            st.bset(pr[0], ("v", vv)); st.bset(pr[1], ("v", vv))
            sv = op(st, "scalar", "copy", sqs[0][:, :], xq6[m % 2][:, :],
                    waits=[("v", vv)] + ([("t", ln1_t[m - 1])] if m >= 1 else []),
                    inc="s")
            sv = op(st, "scalar", "activation", sqs[1][:, :], xq6[m % 2][:, :],
                    AF.Square, inc="s")
            wstat = [("s", sv)]
            if m == 0:
                wstat += st.bw("s1", "s2", "mi", "ha")
            op(st, "tensor", "matmul", pstat1[0:1, :], c_onb[:, 0:1],
               sqs[0][:, 0:NH], start=(m == 0), stop=(m == 15), waits=wstat)
            op(st, "tensor", "matmul", pstat2[0:1, :], c_onb[:, 0:1],
               sqs[0][:, NH:TC], start=(m == 0), stop=(m == 15))
            op(st, "tensor", "matmul", pmisc[0:1, :], c_onb[:, 0:1],
               sqs[1][:, 0:NH], start=(m == 0), stop=(m == 15))
            tv2 = op(st, "tensor", "matmul", phalo[0:1, :], c_onb[:, 0:1],
                     sqs[1][:, NH:TC], start=(m == 0), stop=(m == 15), inc="t")
            ln1_t.append(tv2)
            xp_sp[m] = (f"wp{m % 2}",
                        op(st, "sync", "dma_start", d_sxp[:, m * TC:(m + 1) * TC],
                           xq6[m % 2][:, :], waits=[("v", vv)],
                           inc=f"wp{m % 2}", incn=16))
            for _ in range(4):
                emit_tern_piece()
        mm4_t = st.val("t")
        xp_sp_all = [xp_sp[14], xp_sp[15]]
        if "xpre" in debug:
            op(st, "sync", "dma_start", dbg_outs["xpre"], d_sxp,
               waits=xp_sp_all, inc="wsc", incn=16)
        flush(nc, st)
        hid_cm.__exit__(None, None, None)
        if stop == 6:
            _stopnow(); ph6.close(); return nc

        st.begin()
        v1 = opn(st, "vector", "tensor_scalar", rows3[0:1, 0:NH], pstat1[0:1, :],
                1.0 / 2048.0, None, op0=OP.mult, waits=[("t", mm4_t)], inc="v")
        v1 = opn(st, "vector", "tensor_scalar", rows3[0:1, NH:TC], pstat2[0:1, :],
                1.0 / 2048.0, None, op0=OP.mult, inc="v")
        v1 = opn(st, "vector", "tensor_scalar", rows[0:1, 0:NH], pmisc[0:1, :],
                1.0 / 2048.0, None, op0=OP.mult, inc="v")
        v1 = opn(st, "vector", "tensor_scalar", rows[0:1, NH:TC], phalo[0:1, :],
                1.0 / 2048.0, None, op0=OP.mult, inc="v")
        v1 = opn(st, "vector", "tensor_mul", rows2[0:1, 0:TC], rows3[0:1, 0:TC],
                rows3[0:1, 0:TC], inc="v")
        v1 = opn(st, "vector", "scalar_tensor_tensor", rows[0:1, 0:TC],
                rows[0:1, 0:TC], 1e-5, rows2[0:1, 0:TC],
                op0=OP.add, op1=OP.subtract, inc="v")
        v1 = opn(st, "vector", "reciprocal", rows2[0:1, 0:TC], rows[0:1, 0:TC],
                inc="v")
        s1 = op(st, "scalar", "activation", rows[0:1, 0:TC], rows2[0:1, 0:TC],
                AF.Sqrt, waits=[("v", v1)], inc="s")
        v1 = opn(st, "vector", "tensor_mul", rows2[0:1, 0:TC], rows3[0:1, 0:TC],
                rows[0:1, 0:TC], waits=[("s", s1)], inc="v")
        op(st, "tensor", "matmul", pstat1[:, :], c_onf[0:1, :], rows[0:1, 0:NH],
           start=True, stop=True, waits=[("v", v1)])
        op(st, "tensor", "matmul", pstat2[:, :], c_onf[0:1, :], rows[0:1, NH:TC],
           start=True, stop=True)
        op(st, "tensor", "matmul", pmisc[:, :], c_onf[0:1, :], rows2[0:1, 0:NH],
           start=True, stop=True)
        tb = op(st, "tensor", "matmul", phalo[:, :], c_onf[0:1, :],
                rows2[0:1, NH:TC], start=True, stop=True, inc="t")
        sb = op(st, "scalar", "copy", bc[:, 0:NH], pstat1[:, :], waits=[("t", tb)],
                inc="s")
        sb = op(st, "scalar", "copy", bc[:, NH:TC], pstat2[:, :], inc="s")
        sb = op(st, "scalar", "copy", bc2[:, 0:NH], pmisc[:, :], inc="s")
        sb = op(st, "scalar", "copy", bc2[:, NH:TC], phalo[:, :], inc="s")
        for _k in ("s1", "s2", "mi", "ha"):
            st.bset(_k, ("s", sb))
        x1_v = 0
        x1_sp = {}
        for cc in range(NCC):
            rl = op(st, "sync", "dma_start", xq6[cc % 2][:, :],
                    d_sxp[:, cc * TC:(cc + 1) * TC],
                    waits=(xp_sp_all if cc < 2 else []) +
                          ([("v", x1_v)] if cc >= 2 else []),
                    inc=f"wp{cc % 2}", incn=16)
            vv = op(st, "vector", "tensor_mul", ub6[cc % 2][:, :],
                    xq6[cc % 2][:, :], bc[:, :],
                    waits=[(f"wp{cc % 2}", rl)] + ([("s", sb)] if cc == 0 else []) +
                          ([x1_sp[cc - 2]] if cc >= 2 else []),
                    inc="v")
            x1_v = op(st, "vector", "tensor_sub", ub6[cc % 2][:, :],
                      ub6[cc % 2][:, :], bc2[:, :], inc="v")
            x1_sp[cc] = (f"wu{cc % 2}",
                         op(st, "gpsimd", "dma_start",
                            d_sx1[:, cc * TC:(cc + 1) * TC], ub6[cc % 2][:, :],
                            waits=[("v", x1_v)], inc=f"wu{cc % 2}", incn=16))
        x1_sp_all = [x1_sp[NCC - 2], x1_sp[NCC - 1]]
        if "x1" in debug:
            op(st, "sync", "dma_start", dbg_outs["x1"], d_sx1,
               waits=x1_sp_all, inc="wsc", incn=16)
        flush(nc, st)
        ph6.close()
        phT.close()
        if stop == 7:
            _stopnow(); return nc

        ph8 = ExitStack()
        sb8 = lambda n, sh, dt: ph8.enter_context(nc.sbuf_tensor(n, sh, dt, side="left"))
        QA = sb8("qarena", [128, 64 * NQ], BF16)
        X1Q = [sb8(f"x1q{i}", [128, NCC * NQ], BF16) for i in range(3)]
        C2 = [sb8(f"c2_{i}", [128, 32 * NQ], F32) for i in range(2)]
        V1B = sb8("v1b", [128, NCC * NQ], F32)
        QG = 4
        CHB = sb8("chb", [128, 4 * NQ], BF16)
        CLB = sb8("clb", [128, 4 * NQ], BF16)
        SCR1 = sb8("scr1", [128, 8 * NQ], F32)
        QI = sb8("qi8", [128, QG * NQ], I32)
        pant = [sb8(f"pant{i}", [128, 4096], BF16) for i in range(2)]
        rel8 = [sb8("rel8_0", [128, NQ], F32)] * 2

        def rep_ap(buf, off, n):
            rs = buf[:, :].ap[0][0]
            return bass.AP(buf[:, 0:NQ].tensor, off, [[rs, 128], [0, n], [1, NQ]])

        def sview(buf, off, nblk, blkstride, width):
            rs = buf[:, :].ap[0][0]
            return bass.AP(buf[:, 0:NQ].tensor, off,
                           [[rs, 128], [blkstride, nblk], [1, width]])

        scr1_t = [0]
        ld_q = {}
        h1_s = {}
        h3_s = {}
        t1_v = {}
        t3_v = {}
        x1_sb = {}
        vq1_d = {}
        vq2_d = {}
        m5_ev = {}
        mm5_t_d = {}
        mm6_t_d = {}
        m6_s = {}
        xa_v = {}
        out_edge = {}
        pan_free8 = {0: ("t", 0), 1: ("t", 0)}
        ln2_t = []
        d_sx1_r = d_sx1.rearrange("p (c t) -> p c t", c=NCC)
        MS5 = list(range(16, 32)) + list(range(16))

        def butterflies_slab(buf, nch, wait0):
            vlast = 0
            s = nch // 2
            first = True
            while s >= 1:
                nblk = nch // (2 * s)
                width = s * NQ
                av = sview(buf, 0, nblk, 2 * s * NQ, width)
                bv = sview(buf, width, nblk, 2 * s * NQ, width)
                vlast = op(st, "vector", "tensor_add", av, av, bv,
                           waits=list(wait0) if first else [], inc="v")
                vlast = op(st, "vector", "scalar_tensor_tensor", bv, bv,
                           -2.0, av, op0=OP.mult, op1=OP.add, inc="v")
                first = False
                s //= 2
            return vlast

        def max_to_col(wait_v, col_idx):
            tt = 0
            for b_ in range(NB):
                tt = op(st, "tensor", "transpose",
                        pmisc[:, b_ * 128:(b_ + 1) * 128],
                        SCR1[:, b_ * 128:(b_ + 1) * 128], c_idf[:, :],
                        waits=([("v", wait_v)] + st.bw("mi")) if b_ == 0 else [],
                        inc="t" if b_ == NB - 1 else None)
            scr1_t[0] = tt
            vv = 0
            for b_ in range(NB):
                vv = op(st, "vector", "reduce_max",
                        cols[:, col_idx + b_:col_idx + b_ + 1],
                        pmisc[:, b_ * 128:(b_ + 1) * 128], axis=AX.X,
                        waits=[("t", tt)] if b_ == 0 else [], inc="v")
            st.bset("mi", ("v", vv))
            return vv

        def col_to_bcast(col_idx, wait_v, dstbuf, dstoff, rtag=None):
            tt = 0
            for b_ in range(NB):
                tt = op(st, "tensor", "transpose",
                        pmisc[0:1, b_ * 128:(b_ + 1) * 128],
                        cols[:, col_idx + b_:col_idx + b_ + 1], c_idf[:, :],
                        waits=([("v", wait_v)] + st.bw("mi")) if b_ == 0 else [],
                        inc="t" if b_ == NB - 1 else None)
            vv = opn(st, "vector", "tensor_copy", rows[0:1, 0:NQ],
                     pmisc[0:1, 0:NQ], waits=[("t", tt)], inc="v")
            if rtag is not None and rtag in debug:
                op(st, "sync", "dma_start", dbg_outs[rtag], rows[0:1, 0:NQ],
                   waits=[("v", vv)], inc="wsc", incn=16)
            tb_ = op(st, "tensor", "matmul", pmisc[:, 0:NQ], c_onf[0:1, :],
                     rows[0:1, 0:NQ], start=True, stop=True, waits=[("v", vv)],
                     inc="t")
            sb_ = op(st, "scalar", "copy", dstbuf[:, dstoff:dstoff + NQ],
                     pmisc[:, 0:NQ], waits=[("t", tb_)], inc="s")
            st.bset("mi", ("s", sb_))
            return sb_

        def quant(buf, nch, q0col, wait_s, bcbuf, bcoff, extra=()):
            vq = 0
            W = QG * NQ
            bcrep = rep_ap(bcbuf, bcoff, QG)
            for g in range(nch // QG):
                vsl = buf[:, g * W:(g + 1) * W].rearrange(
                    "p (c t) -> p c t", c=QG)
                w = ([("s", wait_s)] + [e for e in extra if e and e[1] > 0]) \
                    if g == 0 else []
                op(st, "vector", "tensor_mul",
                   SCR1[:, 0:W].rearrange("p (c t) -> p c t", c=QG),
                   vsl, bcrep, waits=w, inc="v")
                op(st, "vector", "tensor_scalar", QI[:, 0:W], SCR1[:, 0:W],
                   0.5, None, op0=OP.add, inc="v")
                op(st, "vector", "scalar_tensor_tensor", SCR1[:, 0:W],
                   SCR1[:, 0:W], 0.5, QI[:, 0:W], op0=OP.add, op1=OP.is_lt,
                   inc="v")
                vq = op(st, "vector", "scalar_tensor_tensor",
                        QA[:, q0col + g * W:q0col + (g + 1) * W],
                        SCR1[:, 0:W], -1.0, QI[:, 0:W], op0=OP.mult,
                        op1=OP.add, inc="v")
            return vq

        def emit_LOAD(q):
            w = list(x1_sp_all)
            if q >= 3:
                w.append(("v", xa_v[q - 3]))
            ld_q[q] = op(st, "sync", "dma_start",
                         X1Q[q % 3].rearrange("p (c t) -> p c t", c=NCC),
                         d_sx1_r[:, :, q * NQ:(q + 1) * NQ],
                         waits=w, inc="wq", incn=16)

        def emit_H1(q):
            ev = 0
            for cc in range(NCC):
                bk = cc % 4
                w = st.bw(bk)
                if cc == 0:
                    w = w + [("wq", ld_q[q])]
                tv = op(st, "tensor", "matmul", ps[bk][:, 0:NQ], c_h128[:, :],
                        X1Q[q % 3][:, cc * NQ:(cc + 1) * NQ], start=True, stop=True,
                        waits=w, inc="t")
                sw = [("t", tv)]
                if cc == 0 and q >= 1:
                    sw.append(("v", vq1_d[q - 1]))
                ev = op(st, "scalar", "copy", V1B[:, cc * NQ:(cc + 1) * NQ],
                        ps[bk][:, 0:NQ], waits=sw, inc="s")
                st.bset(bk, ("s", ev))
            h1_s[q] = ev

        def emit_B1(q):
            butterflies_slab(V1B, NCC, [("s", h1_s[q])])

        def emit_T1(q):
            t1_v[q] = op(st, "vector", "tensor_reduce",
                         SCR1[:, 0:NQ].rearrange("p t -> p t ()"),
                         V1B[:, :].rearrange("p (c t) -> p t c", c=NCC),
                         AX.X, OP.max, apply_absolute_value=True,
                         waits=[("t", scr1_t[0])], inc="v")
            if "v1" in debug and q == 0:
                op(st, "sync", "dma_start", dbg_outs["v1"], V1B[:, :],
                   waits=[("v", t1_v[q])], inc="wsc", incn=16)

        def emit_X1(q):
            p = q % 2
            vmx = max_to_col(t1_v[q], 2 * p)
            vc = opn(st, "vector", "tensor_scalar", cols[:, 2 * p:2 * p + NB],
                     cols[:, 2 * p:2 * p + NB], SQ2048 * 1e-5, None,
                     op0=OP.max, waits=[("v", vmx)], inc="v")
            vc = opn(st, "vector", "reciprocal", cols[:, 8:8 + NB],
                     cols[:, 2 * p:2 * p + NB], inc="v")
            vc = opn(st, "vector", "tensor_scalar", cols[:, 8:8 + NB],
                     cols[:, 8:8 + NB], 127.0, None, op0=OP.mult, inc="v")
            x1_sb[q] = col_to_bcast(8, vc, bc, p * NQ,
                                    rtag="c1r" if q == 0 else None)

        def emit_Q1(q):
            p = q % 2
            vq1_d[q] = quant(V1B, NCC, (32 + p * 16) * NQ, x1_sb[q],
                             bc, p * NQ,
                             extra=[("t", mm5_t_d.get(q - 2, 0))])
            if "q1" in debug and q == 0:
                op(st, "sync", "dma_start", dbg_outs["q1"],
                   QA[:, (32 + p * 16) * NQ:(48 + p * 16) * NQ],
                   waits=[("v", vq1_d[q])], inc="wsc", incn=16)

        ha_prev = [("s", 0)]
        hch_e = {i: ("t", 0) for i in range(4)}

        def emit_M5(q, filler=(), inserts=None):
            p = q % 2
            C = C2[p]
            q1off = (32 + p * 16) * NQ
            inserts = inserts or {}
            fl = list(filler)
            fpos = [0]

            def emit_fill(n):
                while n > 0 and fpos[0] < len(fl):
                    fl[fpos[0]]()
                    fpos[0] += 1
                    n -= 1

            cl_v = {}
            LAG = 3
            h3_last = 0
            for idx in range(32 + LAG):
                if idx in inserts:
                    inserts[idx]()
                if idx < 32:
                    m = MS5[idx]
                    emit_fill(2 if idx < 8 else (1 if idx % 3 == 0 else 0))
                    b = idx % 2
                    wld = [pan_free8[b]]
                    if q == 0 and idx < 2:
                        wld += tern_up_done
                    dv = op(st, "gpsimd", "dma_start", pant[b][:, 0:2048],
                            d_stup[:, m * 2048:(m + 1) * 2048],
                            waits=wld, inc=f"wt{b}", incn=16)
                    bk = idx % 4
                    w0 = [(f"wt{b}", dv)] + st.bw(bk)
                    if idx == 0:
                        w0.append(("v", vq1_d[q]))
                    for c in range(NCC):
                        lhs = pant[b][:, c * 128:(c + 1) * 128]
                        tv = op(st, "tensor", "matmul", ps[bk][:, 0:NQ], lhs,
                                QA[:, q1off + c * NQ:q1off + (c + 1) * NQ],
                                start=(c == 0), stop=(c == NCC - 1),
                                waits=w0 if c == 0 else [],
                                inc="t" if c == NCC - 1 else None)
                    pan_free8[b] = ("t", tv)
                    sw = [("t", tv)]
                    if m == 0:
                        sw.append(out_edge.get(q - 2))
                    op(st, "scalar", "activation", rel8[idx % 2][:, :],
                       ps[bk][:, 0:NQ], AF.Relu, waits=sw, inc="s")
                    g2 = op(st, "scalar", "activation",
                            C[:, m * NQ:(m + 1) * NQ],
                            rel8[idx % 2][:, :], AF.Square, inc="s")
                    st.bset(bk, ("s", g2))
                    sl = idx % 4
                    ch = op(st, "scalar", "copy",
                            CHB[:, sl * NQ:(sl + 1) * NQ],
                            C[:, m * NQ:(m + 1) * NQ],
                            waits=[hch_e[sl]], inc="s")
                    cl_v[idx] = op(st, "vector", "tensor_sub",
                                   CLB[:, sl * NQ:(sl + 1) * NQ],
                                   C[:, m * NQ:(m + 1) * NQ],
                                   CHB[:, sl * NQ:(sl + 1) * NQ],
                                   waits=[("s", ch), hch_e[sl]], inc="v")
                if idx >= LAG:
                    jj = idx - LAG
                    mm = MS5[jj]
                    sl = jj % 4
                    hf = jj % 2
                    wtm = [("v", cl_v[jj])]
                    if ha_prev[0][1] > 0:
                        wtm.append(ha_prev[0])
                    op(st, "tensor", "matmul",
                       phalo[:, hf * NQ:(hf + 1) * NQ], c_h128[:, :],
                       CHB[:, sl * NQ:(sl + 1) * NQ], start=True, stop=False,
                       waits=wtm)
                    th = op(st, "tensor", "matmul",
                            phalo[:, hf * NQ:(hf + 1) * NQ], c_h128[:, :],
                            CLB[:, sl * NQ:(sl + 1) * NQ], start=False,
                            stop=True, inc="t")
                    hch_e[sl] = ("t", th)
                    h3_last = op(st, "scalar", "copy",
                                 C[:, mm * NQ:(mm + 1) * NQ],
                                 phalo[:, hf * NQ:(hf + 1) * NQ],
                                 waits=[("t", th)], inc="s")
                    ha_prev[0] = ("s", h3_last)
            emit_fill(len(fl))
            h3_s[q] = h3_last
            mm5_t_d[q] = st.val("t")

        def b3t3_thunks(q):
            p = q % 2
            buf = C2[p]
            thunks = []
            s = 16
            first = True
            while s >= 1:
                nblk = 32 // (2 * s)
                width = s * NQ
                av = sview(buf, 0, nblk, 2 * s * NQ, width)
                bv = sview(buf, width, nblk, 2 * s * NQ, width)
                def t1(av=av, bv=bv, first=first, q=q):
                    w0 = [("s", h3_s[q])] if first else []
                    op(st, "vector", "tensor_add", av, av, bv, waits=w0,
                       inc="v")

                def t2(av=av, bv=bv):
                    op(st, "vector", "scalar_tensor_tensor", bv, bv, -2.0, av,
                       op0=OP.mult, op1=OP.add, inc="v")

                thunks += [t1, t2]
                first = False
                s //= 2

            def t3(q=q, p=p):
                t3_v[q] = op(st, "vector", "tensor_reduce",
                             SCR1[:, 0:NQ].rearrange("p t -> p t ()"),
                             C2[p][:, :].rearrange("p (c t) -> p t c", c=32),
                             AX.X, OP.max, apply_absolute_value=True,
                             waits=[("t", scr1_t[0])], inc="v")

            thunks.append(t3)
            return thunks

        def emit_X3(q):
            p = q % 2
            vmx = max_to_col(t3_v[q], 12)
            vc = opn(st, "vector", "tensor_scalar", cols[:, 16:16 + NB],
                     cols[:, 2 * p:2 * p + NB], gcol[:, 6:7], None, op0=OP.mult,
                     waits=[("v", vmx)], inc="v")
            vc = opn(st, "vector", "tensor_mul", cols[:, 18:18 + NB],
                     cols[:, 16:16 + NB], cols[:, 16:16 + NB], inc="v")
            vc = opn(st, "vector", "reciprocal", cols[:, 20:20 + NB],
                     cols[:, 18:18 + NB], inc="v")
            vc = opn(st, "vector", "tensor_scalar", cols[:, 20:20 + NB],
                     cols[:, 20:20 + NB], 6.4e-4, None, op0=OP.mult, inc="v")
            vc = opn(st, "vector", "tensor_max", cols[:, 12:12 + NB],
                     cols[:, 12:12 + NB], cols[:, 20:20 + NB], inc="v")
            vc = opn(st, "vector", "reciprocal", cols[:, 20:20 + NB],
                     cols[:, 12:12 + NB], inc="v")
            vc2 = opn(st, "vector", "tensor_scalar", cols[:, 20:20 + NB],
                      cols[:, 20:20 + NB], 127.0, None, op0=OP.mult, inc="v")
            sb_c2 = col_to_bcast(20, vc2, bc2, 0)
            va = opn(st, "vector", "tensor_mul", cols[:, 16:16 + NB],
                     cols[:, 18:18 + NB], cols[:, 12:12 + NB], inc="v")
            va = opn(st, "vector", "tensor_scalar", cols[:, 16:16 + NB],
                     cols[:, 16:16 + NB], gcol[:, 7:8], None, op0=OP.mult,
                     inc="v")
            col_to_bcast(16, va, bc2, NQ, rtag="alr" if q == 0 else None)
            return sb_c2

        def emit_Q2(q, sb_c2):
            p = q % 2
            vq2_d[q] = quant(C2[p], 32, 0, sb_c2, bc2, 0,
                             extra=[("t", mm6_t_d.get(q - 1, 0))])
            if "q2" in debug and q == 0:
                op(st, "sync", "dma_start", dbg_outs["q2"], QA[:, 0:32 * NQ],
                   waits=[("v", vq2_d[q])], inc="wsc", incn=16)

        def emit_M6(q, inserts=None):
            p = q % 2
            C = C2[p]
            inserts = inserts or {}
            ev = 0
            for m in range(16):
                if m in inserts:
                    inserts[m]()
                b = m % 2
                wld = [pan_free8[b]]
                if q == 0 and m < 2:
                    wld += tern_dn_done
                dv = op(st, "gpsimd", "dma_start", pant[b][:, :],
                        d_stdn[:, m * 4096:(m + 1) * 4096],
                        waits=wld, inc=f"wt{b}", incn=16)
                bk = m % 4
                w0 = [(f"wt{b}", dv)] + st.bw(bk)
                if m == 0:
                    w0.append(("v", vq2_d[q]))
                for c in range(32):
                    lhs = pant[b][:, c * 128:(c + 1) * 128]
                    tv = op(st, "tensor", "matmul", ps[bk][:, 0:NQ], lhs,
                            QA[:, c * NQ:(c + 1) * NQ], start=(c == 0),
                            stop=(c == 31), waits=w0 if c == 0 else [],
                            inc="t" if c == 31 else None)
                pan_free8[b] = ("t", tv)
                sw = [("t", tv)]
                if m == 0:
                    sw.append(("v", vq2_d[q]))
                ev = op(st, "scalar", "copy", C[:, m * NQ:(m + 1) * NQ],
                        ps[bk][:, 0:NQ], waits=sw, inc="s")
                st.bset(bk, ("s", ev))
                op(st, "vector", "tensor_mul", C[:, m * NQ:(m + 1) * NQ],
                   C[:, m * NQ:(m + 1) * NQ], bc2[:, NQ:2 * NQ],
                   waits=[("s", ev)], inc="v")
                vv2 = op(st, "vector", "tensor_add",
                         C[:, m * NQ:(m + 1) * NQ],
                         C[:, m * NQ:(m + 1) * NQ],
                         X1Q[q % 3][:, m * NQ:(m + 1) * NQ], inc="v")
                if m % 2 == 1:
                    gsl = C[:, (m - 1) * NQ:(m + 1) * NQ]
                    sv = op(st, "scalar", "copy", sqs[0][:, 0:2 * NQ], gsl,
                            waits=[("v", vv2)] +
                                  ([("t", ln2_t[-1])] if ln2_t else []),
                            inc="s")
                    sv = op(st, "scalar", "activation", sqs[1][:, 0:2 * NQ],
                            gsl, AF.Square, inc="s")
                    wstat = [("s", sv)]
                    if m == 1:
                        wstat += st.bw("s1", "s2")
                    for j in range(2):
                        op(st, "tensor", "matmul", pstat1[0:1, 0:NQ],
                           c_onb[:, 0:1], sqs[0][:, j * NQ:(j + 1) * NQ],
                           start=(m == 1 and j == 0), stop=(m == 15 and j == 1),
                           waits=wstat if j == 0 else [])
                        tv2 = op(st, "tensor", "matmul", pstat2[0:1, 0:NQ],
                                 c_onb[:, 0:1], sqs[1][:, j * NQ:(j + 1) * NQ],
                                 start=(m == 1 and j == 0),
                                 stop=(m == 15 and j == 1),
                                 inc="t" if j == 1 else None)
                    ln2_t.append(tv2)
            mm6_t_d[q] = st.val("t")
            m6_s[q] = ev
            xa_v[q] = vv2
            if "x2pre" in debug and q == 0:
                op(st, "sync", "dma_start", dbg_outs["x2pre"],
                   C[:, 0:NCC * NQ], waits=[("v", vv2)], inc="wsc", incn=16)

        def emit_L2(q):
            p = q % 2
            C = C2[p]
            c3 = C[:, 0:NCC * NQ].rearrange("p (c t) -> p c t", c=NCC)
            v1 = opn(st, "vector", "tensor_scalar", rows3[0:1, 0:NQ],
                     pstat1[0:1, 0:NQ], 1.0 / 2048.0, None, op0=OP.mult,
                     waits=[("t", ln2_t[-1])], inc="v")
            v1 = opn(st, "vector", "tensor_scalar", rows[0:1, 0:NQ],
                     pstat2[0:1, 0:NQ], 1.0 / 2048.0, None, op0=OP.mult,
                     inc="v")
            v1 = opn(st, "vector", "tensor_mul", rows2[0:1, 0:NQ],
                     rows3[0:1, 0:NQ], rows3[0:1, 0:NQ], inc="v")
            v1 = opn(st, "vector", "scalar_tensor_tensor", rows[0:1, 0:NQ],
                     rows[0:1, 0:NQ], 1e-5, rows2[0:1, 0:NQ],
                     op0=OP.add, op1=OP.subtract, inc="v")
            v1 = opn(st, "vector", "reciprocal", rows2[0:1, 0:NQ],
                     rows[0:1, 0:NQ], inc="v")
            s1 = op(st, "scalar", "activation", rows[0:1, 0:NQ],
                    rows2[0:1, 0:NQ], AF.Sqrt, waits=[("v", v1)], inc="s")
            v1 = opn(st, "vector", "tensor_mul", rows2[0:1, 0:NQ],
                     rows3[0:1, 0:NQ], rows[0:1, 0:NQ], waits=[("s", s1)],
                     inc="v")
            op(st, "tensor", "matmul", pstat1[:, 0:NQ], c_onf[0:1, :],
               rows[0:1, 0:NQ], start=True, stop=True, waits=[("v", v1)])
            tb = op(st, "tensor", "matmul", pstat2[:, 0:NQ], c_onf[0:1, :],
                    rows2[0:1, 0:NQ], start=True, stop=True, inc="t")
            sb_ = op(st, "scalar", "copy", bc2[:, 2 * NQ:3 * NQ],
                     pstat1[:, 0:NQ], waits=[("t", tb)], inc="s")
            sb_ = op(st, "scalar", "copy", bc2[:, 3 * NQ:4 * NQ],
                     pstat2[:, 0:NQ], inc="s")
            st.bset("s1", ("s", sb_)); st.bset("s2", ("s", sb_))
            invrep = rep_ap(bc2, 2 * NQ, NCC)
            muirep = rep_ap(bc2, 3 * NQ, NCC)
            lv = op(st, "vector", "tensor_mul", c3, c3, invrep,
                    waits=[("s", sb_)], inc="v")
            lv = op(st, "vector", "tensor_tensor", c3, c3, muirep,
                    op=OP.subtract, inc="v")
            od = op(st, "sync", "dma_start",
                    d_out.rearrange("(c p) t -> p c t", p=128)[:, :, q * NQ:(q + 1) * NQ],
                    C[:, 0:NCC * NQ].rearrange("p (c t) -> p c t", c=NCC),
                    waits=[("v", lv)], inc="wo0", incn=16)
            out_edge[q] = ("wo0", od)

        st.begin()
        emit_LOAD(0); emit_H1(0); emit_B1(0); emit_T1(0); emit_X1(0); emit_Q1(0)
        sbc2_d = {}
        for k in range(5):
            A = k if k <= 3 else None
            B = k - 1 if k >= 1 else None
            if A is not None:
                ins = {}
                if A < 3:
                    def _lh(A=A):
                        emit_LOAD(A + 1)
                        emit_H1(A + 1)
                    ins[8] = _lh
                if B is not None:
                    def _x3(B=B):
                        sbc2_d[B] = emit_X3(B)
                    ins[22] = _x3
                emit_M5(A, filler=b3t3_thunks(B) if B is not None else (),
                        inserts=ins)
            elif B is not None:
                for th in b3t3_thunks(B):
                    th()
                sbc2_d[B] = emit_X3(B)
            if B is not None:
                emit_Q2(B, sbc2_d[B])
                ins6 = {}
                if A is not None and A < 3:
                    def _bt(A=A):
                        emit_B1(A + 1)
                        emit_T1(A + 1)
                    def _x1(A=A):
                        emit_X1(A + 1)
                    ins6[2] = _bt
                    ins6[8] = _x1
                emit_M6(B, inserts=ins6)
                if A is not None and A < 3:
                    emit_Q1(A + 1)
                emit_L2(B)
            elif A is not None and A < 3:
                emit_B1(A + 1)
                emit_T1(A + 1)
                emit_X1(A + 1)
                emit_Q1(A + 1)
        flush(nc, st)

        st.begin()
        wt(st, "sync", "wo0", out_edge[3][1])
        if debug:
            wt(st, "sync", "wsc", st.val("wsc"))
        flush(nc, st)

        ph8.close()

    return nc



def _hadamard128():
    h = np.ones((1, 1), dtype=np.float32)
    while h.shape[0] < 128:
        h = np.block([[h, h], [h, -h]])
    return h


def _prep_weights(W_in, conv_w, conv_b, W_out, W_mlp1, W_mlp2, w_up, w_down):
    c = np.ascontiguousarray

    def panels(WT, ncc, nmc):
        return c(WT.reshape(ncc, 128, nmc, 128).transpose(2, 1, 0, 3)
                 .reshape(nmc, 128, ncc * 128))

    win = panels(W_in.T, 16, 64)
    order = []
    for k in range(32):
        order += [k, 32 + k]
    win = c(win[order])
    wout = panels(W_out.T, 32, 16)
    wm1 = panels(W_mlp1.T, 16, 64)
    wm2 = panels(W_mlp2.T, 64, 16)
    wup = panels(w_up.T, 16, 32)
    wdn = panels(w_down.T, 32, 16)
    cw = c(conv_w[:, 0, :].reshape(32, 128, 4).transpose(1, 2, 0).reshape(128, 128))
    cb = c(conv_b.reshape(32, 128).T)
    h128 = _hadamard128().astype(ml_dtypes.bfloat16)
    identf = np.eye(128, dtype=np.float32)
    onesf = np.ones((128, 128), dtype=np.float32)
    onesb = np.ones((128, 8), dtype=ml_dtypes.bfloat16)
    return dict(win_p=win, wout_p=wout, wm1_p=wm1, wm2_p=wm2, wup_p=wup,
                wdn_p=wdn, c_convw=cw, c_convb=cb, c_h128=h128,
                c_identf=identf, c_onesf=onesf, c_onesb=onesb)


def _prep_core(x, b, h):
    c = np.ascontiguousarray
    xt = c(x[b, h * TC:(h + 1) * TC, :].T)
    if h == 0:
        halo = np.zeros((128, 48), dtype=np.float32)
    else:
        hs = x[b, h * TC - 3:h * TC, :]
        halo = c(hs.T.reshape(16, 128, 3).transpose(1, 0, 2).reshape(128, 48))
    return xt, halo


def kernel(L0_attn_x, L0_attn_W_in, L0_attn_conv_w, L0_attn_conv_b,
           L0_attn_W_out, L0_attn_W_mlp1, L0_attn_W_mlp2,
           L0_ffn_w_up, L0_ffn_w_down):
    x = np.ascontiguousarray(L0_attn_x, dtype=np.float32)
    shared = _prep_weights(
        np.asarray(L0_attn_W_in, np.float32), np.asarray(L0_attn_conv_w, np.float32),
        np.asarray(L0_attn_conv_b, np.float32), np.asarray(L0_attn_W_out, np.float32),
        np.asarray(L0_attn_W_mlp1, np.float32), np.asarray(L0_attn_W_mlp2, np.float32),
        np.asarray(L0_ffn_w_up, np.float32), np.asarray(L0_ffn_w_down, np.float32))

    nc = build(debug=tuple(DBG))
    in_maps = []
    for i in range(8):
        b, hh = i // 2, i % 2
        xt, halo = _prep_core(x, b, hh)
        m = dict(shared)
        m["x_t"] = xt
        m["halo_p"] = halo
        in_maps.append(m)

    res = run_bass_kernel_spmd(nc, in_maps, core_ids=list(range(8)))
    out = np.empty((4, 2048, 2048), dtype=np.float32)
    for i in range(8):
        b, hh = i // 2, i % 2
        out[b, hh * TC:(hh + 1) * TC, :] = res.results[i]["out"].T
    return out


# version 13
# speedup vs baseline: 1.0285x; 1.0285x over previous
import numpy as np
import ml_dtypes
from contextlib import ExitStack

import concourse.bass as bass
import concourse.mybir as mybir
from concourse.bass_utils import run_bass_kernel_spmd

F32 = mybir.dt.float32
BF16 = mybir.dt.bfloat16
I32 = mybir.dt.int32
OP = mybir.AluOpType
AF = mybir.ActivationFunctionType
AX = mybir.AxisListType

D = 2048
TC = 1024
NH = 512
NQ = 256
NB = NQ // 128
NCC = 16
SQ2048 = float(np.sqrt(2048.0))
NW = 8192.0 * 1024.0

DBG = []

ENGS = ("tensor", "vector", "scalar", "sync", "gpsimd")


class St:
    def __init__(self):
        self.c = {k: 0 for k in ("t", "v", "s", "wg", "ws")}
        self.ops = None
        self.sems = None
        self.bank = {k: [] for k in (0, 1, 2, 3, "s1", "s2", "mi", "ha")}

    def begin(self):
        self.ops = {e: [] for e in ENGS}

    def inc(self, k, n=1):
        self.c[k] += n
        return self.c[k]

    def val(self, k):
        return self.c[k]

    def bw(self, *keys):
        out = []
        for k in keys:
            out.extend(e for e in self.bank[k] if e[1] > 0)
        return out

    def bset(self, key, *edges):
        self.bank[key] = list(edges)


def flush(nc, st):
    ops = st.ops
    st.ops = None
    with nc.Block() as blk:
        for eng in ENGS:
            lst = ops[eng]
            if not lst:
                continue

            def mk(lst):
                def body(e):
                    for f in lst:
                        f(e)
                return body

            getattr(blk, eng)(mk(lst))


def op(st, eng, meth, *args, waits=(), inc=None, incn=1, **kw):
    sems = st.sems
    val = st.inc(inc, incn) if inc else None
    waits = [w for w in waits if w is not None and w[1] > 0]

    def f(e):
        for sname, v in waits:
            e.wait_ge(sems[sname], v)
        inst = getattr(e, meth)(*args, **kw)
        if inc:
            inst.then_inc(sems[inc], incn)

    st.ops[eng].append(f)
    return val


def opn(st, eng, meth, *args, waits=(), **kw):
    return op(st, eng, meth, *args,
              waits=list(waits) + [("v", st.val("v"))], **kw)


def wt(st, eng, sname, v):
    sems = st.sems

    def f(e):
        e.wait_ge(sems[sname], v)

    st.ops[eng].append(f)


def build(debug=(), stop=None):
    nc = bass.Bass("TRN2", target_bir_lowering=False,
                   detect_race_conditions=False)
    st = St()

    def _stopnow():
        st.begin()
        wt(st, "sync", "wsc", st.val("wsc"))
        flush(nc, st)


    d_x = nc.dram_tensor("x_t", [D, TC], F32, kind="ExternalInput").ap()
    d_halo = nc.dram_tensor("halo_p", [128, 48], F32, kind="ExternalInput").ap()
    d_win = nc.dram_tensor("win_p", [64, 128, 2048], F32, kind="ExternalInput").ap()
    d_wout = nc.dram_tensor("wout_p", [16, 128, 4096], F32, kind="ExternalInput").ap()
    d_wm1 = nc.dram_tensor("wm1_p", [64, 128, 2048], F32, kind="ExternalInput").ap()
    d_wm2 = nc.dram_tensor("wm2_p", [16, 128, 8192], F32, kind="ExternalInput").ap()
    d_wup = nc.dram_tensor("wup_p", [32, 128, 2048], F32, kind="ExternalInput").ap()
    d_wdn = nc.dram_tensor("wdn_p", [16, 128, 4096], F32, kind="ExternalInput").ap()
    d_h128 = nc.dram_tensor("c_h128", [128, 128], BF16, kind="ExternalInput").ap()
    d_idf = nc.dram_tensor("c_identf", [128, 128], F32, kind="ExternalInput").ap()
    d_onf = nc.dram_tensor("c_onesf", [128, 128], F32, kind="ExternalInput").ap()
    d_onb = nc.dram_tensor("c_onesb", [128, 8], BF16, kind="ExternalInput").ap()
    d_cw = nc.dram_tensor("c_convw", [128, 128], F32, kind="ExternalInput").ap()
    d_cb = nc.dram_tensor("c_convb", [128, 32], F32, kind="ExternalInput").ap()
    d_out = nc.dram_tensor("out", [D, TC], F32, kind="ExternalOutput").ap()
    d_su = nc.dram_tensor("sp_u", [128, NCC * TC], F32, kind="Internal").ap()
    d_sxm = nc.dram_tensor("sp_xmid", [128, NCC * TC], F32, kind="Internal").ap()
    d_sxp = nc.dram_tensor("sp_xpre", [128, NCC * TC], F32, kind="Internal").ap()
    d_sx1 = nc.dram_tensor("sp_x1", [128, NCC * TC], BF16, kind="Internal").ap()
    FP8 = mybir.dt.float8e4
    d_stup = nc.dram_tensor("sp_tup", [128, 32 * 2048], FP8, kind="Internal").ap()
    d_stdn = nc.dram_tensor("sp_tdn", [128, 16 * 4096], FP8, kind="Internal").ap()

    dbg_outs = {}
    for nm, shape, dt in [
        ("h", [128, NCC * TC], BF16), ("y", [128, 32 * TC], BF16),
        ("xmid", [128, NCC * TC], F32), ("u", [128, NCC * TC], F32),
        ("h2", [128, NCC * TC], BF16), ("xpre", [128, NCC * TC], F32),
        ("x1", [128, NCC * TC], BF16), ("v1", [128, NCC * NQ], F32),
        ("q1", [128, NCC * NQ], BF16), ("g", [128, 32 * NQ], F32),
        ("q2", [128, 32 * NQ], BF16), ("x2pre", [128, NCC * NQ], F32),
        ("gam", [1, 8], F32), ("c1r", [1, NQ], F32), ("alr", [1, NQ], F32),
        ("colz", [128, 24], F32),
    ]:
        if nm in debug:
            dbg_outs[nm] = nc.dram_tensor("dbg_" + nm, shape, dt,
                                          kind="ExternalOutput").ap()

    es = ExitStack()
    with es:
        semnames = ("t", "v", "s", "wsc", "wx0", "wx1", "wg0", "wg1",
                    "wm0", "wm1", "wu0", "wu1", "wgg0", "wgg1", "wp0", "wp1",
                    "wq", "wf0", "wf1", "wt0", "wt1", "wo0", "wo1")
        st.sems = {k: es.enter_context(nc.semaphore("sem_" + k))
                   for k in semnames}
        st.c = {k: 0 for k in semnames}

        sbL = lambda n, sh, dt: es.enter_context(nc.sbuf_tensor(n, sh, dt, side="left"))
        c_h128 = sbL("s_h128", [128, 128], BF16)
        c_idf = sbL("c_idf", [128, 128], F32)
        c_onf = sbL("c_onf", [128, 128], F32)
        c_onb = sbL("c_onb", [128, 8], BF16)
        c_cw = sbL("c_cw", [128, 128], F32)
        c_cb = sbL("c_cb", [128, 32], F32)
        halo_x = sbL("halo_x", [128, 48], F32)
        halo_h = sbL("halo_h", [128, 48], BF16)
        halo_sq = sbL("halo_sq", [128, 48], BF16)
        bc = sbL("bc", [128, TC], F32)
        bc2 = sbL("bc2", [128, TC], F32)
        rows = sbL("rows", [1, TC], F32)
        rows2 = sbL("rows2", [1, TC], F32)
        rows3 = sbL("rows3", [1, TC], F32)
        cols = sbL("cols", [128, 24], F32)
        gcols = sbL("gcols", [128, 96], F32)
        gtmp = sbL("gtmp", [128, 2], F32)
        grow = sbL("grow", [1, 8], F32)
        gcol = sbL("gcol", [128, 8], F32)
        sqs = [sbL(f"sqs{i}", [128, TC], BF16) for i in range(2)]

        ps = [es.enter_context(nc.psum_tensor(f"p{i}", [128, NH], F32))
              for i in range(4)]
        pstat1 = es.enter_context(nc.psum_tensor("pstat1", [128, NH], F32))
        pstat2 = es.enter_context(nc.psum_tensor("pstat2", [128, NH], F32))
        pmisc = es.enter_context(nc.psum_tensor("pmisc", [128, NH], F32))
        phalo = es.enter_context(nc.psum_tensor("phalo", [128, NH], F32))

        arY = nc.sbuf_tensor("arY", [128, 32 * TC], BF16, side="right")
        Y = arY.__enter__()
        arH = nc.sbuf_tensor("arH", [128, NCC * TC], BF16, side="right")
        H = arH.__enter__()
        ph1 = ExitStack()
        xrb = [ph1.enter_context(nc.sbuf_tensor(f"xrb{i}", [128, TC], F32,
                                                side="left")) for i in range(2)]

        st.begin()
        for dst, src in [(c_h128, d_h128), (c_idf, d_idf), (c_onf, d_onf),
                         (c_onb, d_onb), (c_cw, d_cw), (c_cb, d_cb),
                         (halo_x, d_halo)]:
            consts_done = op(st, "sync", "dma_start", dst[:, :], src,
                             inc="wsc", incn=16)
        sq_t = []
        sq_s = []
        for cc in range(NCC):
            xld = op(st, "sync", "dma_start", xrb[cc % 2][:, :],
                     d_x[cc * 128:(cc + 1) * 128, :],
                     waits=[("s", sq_s[cc - 2])] if cc >= 2 else [],
                     inc=f"wx{cc % 2}", incn=16)
            sv = op(st, "scalar", "activation", sqs[cc % 2][:, :],
                    xrb[cc % 2][:, :], AF.Square,
                    waits=[(f"wx{cc % 2}", xld)] +
                          ([("t", sq_t[cc - 2])] if cc >= 2 else []),
                    inc="s")
            sq_s.append(sv)
            op(st, "tensor", "matmul", pstat1[0:1, :], c_onb[:, 0:1],
               sqs[cc % 2][:, 0:NH], start=(cc == 0), stop=(cc == NCC - 1),
               waits=[("s", sv), ("wsc", consts_done)])
            tv = op(st, "tensor", "matmul", pstat2[0:1, :], c_onb[:, 0:1],
                    sqs[cc % 2][:, NH:TC], start=(cc == 0), stop=(cc == NCC - 1),
                    inc="t")
            sq_t.append(tv)
        svh = op(st, "scalar", "activation", halo_sq[:, :], halo_x[:, :],
                 AF.Square, waits=[("wsc", consts_done)], inc="s")
        for cc in range(NCC):
            op(st, "tensor", "matmul", pmisc[0:1, 0:3], c_onb[:, 0:1],
               halo_sq[:, cc * 3:(cc + 1) * 3], start=(cc == 0),
               stop=(cc == NCC - 1), waits=[("s", svh)] if cc == 0 else [],
               inc="t" if cc == NCC - 1 else None)

        v1 = opn(st, "vector", "tensor_scalar", rows[0:1, 0:NH], pstat1[0:1, :],
                1e-30, None, op0=OP.max, waits=[("t", st.val("t"))], inc="v")
        v1 = opn(st, "vector", "tensor_scalar", rows[0:1, NH:TC], pstat2[0:1, :],
                1e-30, None, op0=OP.max, inc="v")
        v1 = opn(st, "vector", "tensor_scalar", rows3[0:1, 0:3], pmisc[0:1, 0:3],
                1e-30, None, op0=OP.max, inc="v")
        v1 = opn(st, "vector", "reciprocal", rows2[0:1, 0:TC], rows[0:1, 0:TC],
                inc="v")
        v1 = opn(st, "vector", "reciprocal", rows3[0:1, 4:7], rows3[0:1, 0:3],
                inc="v")
        s1 = op(st, "scalar", "activation", rows[0:1, 0:TC], rows2[0:1, 0:TC],
                AF.Sqrt, scale=2048.0, waits=[("v", v1)], inc="s")
        s1 = op(st, "scalar", "activation", rows3[0:1, 0:3], rows3[0:1, 4:7],
                AF.Sqrt, scale=2048.0, inc="s")
        op(st, "tensor", "matmul", pstat1[:, :], c_onf[0:1, :], rows[0:1, 0:NH],
           start=True, stop=True, waits=[("s", s1)])
        op(st, "tensor", "matmul", pstat2[:, :], c_onf[0:1, :], rows[0:1, NH:TC],
           start=True, stop=True)
        tb = op(st, "tensor", "matmul", pmisc[:, 0:3], c_onf[0:1, :],
                rows3[0:1, 0:3], start=True, stop=True, inc="t")
        sb = op(st, "scalar", "copy", bc[:, 0:NH], pstat1[:, :],
                waits=[("t", tb)], inc="s")
        sb = op(st, "scalar", "copy", bc[:, NH:TC], pstat2[:, :], inc="s")
        sb = op(st, "scalar", "copy", bc2[:, 0:3], pmisc[:, 0:3], inc="s")
        st.bset("s1", ("s", sb)); st.bset("s2", ("s", sb)); st.bset("mi", ("s", sb))
        h_v = 0
        for cc in range(NCC):
            rl = op(st, "sync", "dma_start", xrb[cc % 2][:, :],
                    d_x[cc * 128:(cc + 1) * 128, :],
                    waits=([("s", sq_s[NCC - 1])] if cc < 2 else []) +
                          ([("v", h_v)] if cc >= 2 else []),
                    inc=f"wx{cc % 2}", incn=16)
            h_v = op(st, "vector", "tensor_mul", H[:, cc * TC:(cc + 1) * TC],
                     xrb[cc % 2][:, :], bc[:, :],
                     waits=[(f"wx{cc % 2}", rl)] + ([("s", sb)] if cc == 0 else []),
                     inc="v")
        for cc in range(NCC):
            h_v = op(st, "vector", "tensor_mul", halo_h[:, cc * 3:(cc + 1) * 3],
                     halo_x[:, cc * 3:(cc + 1) * 3], bc2[:, 0:3], inc="v")
        h_done = h_v
        if "h" in debug:
            op(st, "sync", "dma_start", dbg_outs["h"], H[:, :],
               waits=[("v", h_done)], inc="wsc", incn=16)
        flush(nc, st)
        ph1.close()
        if stop == 1:
            _stopnow(); return nc

        phGam = ExitStack()
        gwb5 = [phGam.enter_context(nc.sbuf_tensor(f"gwb5_{i}", [128, 1024], F32,
                                                   side="left")) for i in range(2)]
        gab5 = phGam.enter_context(nc.sbuf_tensor("gab5", [128, 1024], F32,
                                                  side="left"))
        gpieces = []
        for jj in range(64):
            gpieces.append((d_wup[jj // 2, :, (jj % 2) * 1024:(jj % 2 + 1) * 1024],
                            jj))
        for jj in range(32):
            gpieces.append((d_wdn[jj // 2, :, (jj % 2) * 2048:(jj % 2) * 2048 + 1024],
                            64 + jj))
            gpieces.append((d_wdn[jj // 2, :, (jj % 2) * 2048 + 1024:(jj % 2 + 1) * 2048],
                            64 + jj))
        gp_idx = [0]
        g_abs = {}
        g_last_add = [0]

        def emit_gamma_piece():
            i = gp_idx[0]
            if i >= len(gpieces):
                return
            gp_idx[0] += 1
            src_, col = gpieces[i]
            two_rounds = (i >= 64 and (i - 64) % 2 == 1)
            gv = op(st, "sync", "dma_start", gwb5[i % 2][:, :], src_,
                    waits=[g_abs.get(i - 2)], inc=f"wgg{i % 2}", incn=16)
            if not two_rounds:
                g_abs[i] = ("s", op(st, "scalar", "activation", gab5[:, :],
                                    gwb5[i % 2][:, :], AF.Abs,
                                    accum_out=gcols[:, col:col + 1],
                                    waits=[(f"wgg{i % 2}", gv)], inc="s"))
            else:
                sa = op(st, "scalar", "activation", gab5[:, :],
                        gwb5[i % 2][:, :], AF.Abs,
                        accum_out=gtmp[:, 1:2],
                        waits=[(f"wgg{i % 2}", gv), ("v", g_last_add[0])],
                        inc="s")
                g_abs[i] = ("s", sa)
                g_last_add[0] = op(st, "vector", "tensor_add",
                                   gcols[:, col:col + 1], gcols[:, col:col + 1],
                                   gtmp[:, 1:2], waits=[("s", sa)], inc="v")

        ph2 = ExitStack()
        sb2 = lambda n, sh, dt: ph2.enter_context(nc.sbuf_tensor(n, sh, dt, side="left"))
        pan1 = [sb2(f"pan1_{i}", [128, 2048], BF16) for i in range(2)]
        xpb = sb2("xpb", [128, TC + 3], F32)
        szb = sb2("szb", [128, TC], F32)
        xcb = sb2("xcb", [128, TC], F32)
        sxcb = sb2("sxcb", [128, TC], F32)

        st.begin()
        pan_free = {0: ("t", 0), 1: ("t", 0)}
        y_k = {}
        y_v = 0
        for j in range(64):
            k = j // 2
            is_xp = (j % 2 == 0)
            wgv = op(st, "gpsimd", "dma_start", pan1[j % 2][:, :], d_win[j, :, :],
                     waits=[pan_free[j % 2]], inc=f"wg{j % 2}", incn=16)
            pr = (0, 1) if is_xp else (2, 3)
            w0 = [(f"wg{j % 2}", wgv)] + st.bw(*pr)
            if j == 0:
                w0.append(("v", h_done))
            if is_xp:
                w0 += st.bw("ha")
            for c in range(NCC):
                lhs = pan1[j % 2][:, c * 128:(c + 1) * 128]
                last = (c == NCC - 1)
                op(st, "tensor", "matmul", ps[pr[0]][:, :], lhs,
                   H[:, c * TC:c * TC + NH], start=(c == 0), stop=last,
                   waits=w0 if c == 0 else [])
                if is_xp:
                    op(st, "tensor", "matmul", phalo[:, 0:3], lhs,
                       halo_h[:, c * 3:(c + 1) * 3], start=(c == 0), stop=last)
                tv = op(st, "tensor", "matmul", ps[pr[1]][:, :], lhs,
                        H[:, c * TC + NH:(c + 1) * TC], start=(c == 0), stop=last,
                        inc="t" if last else None)
            pan_free[j % 2] = ("t", tv)
            if is_xp:
                sv = op(st, "scalar", "copy", xpb[:, 3:3 + NH], ps[pr[0]][:, :],
                        waits=[("t", tv)] + ([("v", y_k[k - 1])] if k >= 1 else []),
                        inc="s")
                sv = op(st, "scalar", "copy", xpb[:, 3 + NH:3 + TC],
                        ps[pr[1]][:, :], inc="s")
                hvv = op(st, "vector", "tensor_copy", xpb[:, 0:3], phalo[:, 0:3],
                         waits=[("t", tv)] + ([("v", y_k[k - 1])] if k >= 1 else []),
                         inc="v")
                st.bset(pr[0], ("s", sv)); st.bset(pr[1], ("s", sv))
                st.bset("ha", ("v", hvv))
                cv = op(st, "vector", "tensor_scalar", xcb[:, :], xpb[:, 0:TC],
                        c_cw[:, k:k + 1], c_cb[:, k:k + 1],
                        op0=OP.mult, op1=OP.add,
                        waits=[("s", sv), ("v", hvv)], inc="v")
                for tap in range(1, 4):
                    cv = op(st, "vector", "scalar_tensor_tensor", xcb[:, :],
                            xpb[:, tap:tap + TC],
                            c_cw[:, tap * 32 + k:tap * 32 + k + 1],
                            xcb[:, :], op0=OP.mult, op1=OP.add, inc="v")
                sxv = op(st, "scalar", "activation", sxcb[:, :], xcb[:, :],
                         AF.Silu, waits=[("v", cv)], inc="s")
            else:
                zv = op(st, "scalar", "activation", szb[:, 0:NH], ps[pr[0]][:, :],
                        AF.Silu, waits=[("t", tv)], inc="s")
                zv = op(st, "scalar", "activation", szb[:, NH:TC], ps[pr[1]][:, :],
                        AF.Silu, inc="s")
                st.bset(pr[0], ("s", zv)); st.bset(pr[1], ("s", zv))
                y_v = op(st, "vector", "tensor_mul", Y[:, k * TC:(k + 1) * TC],
                         sxcb[:, :], szb[:, :], waits=[("s", zv)], inc="v")
                y_k[k] = y_v
            emit_gamma_piece()
        y_done = y_v
        if "y" in debug:
            op(st, "sync", "dma_start", dbg_outs["y"], Y[:, :],
               waits=[("v", y_done)], inc="wsc", incn=16)
        flush(nc, st)
        ph2.close()
        arH.__exit__(None, None, None)
        if stop == 2:
            _stopnow(); return nc

        arXM = nc.sbuf_tensor("arXM", [128, NCC * TC], F32, side="right")
        XM = arXM.__enter__()
        ph3 = ExitStack()
        sb3 = lambda n, sh, dt: ph3.enter_context(nc.sbuf_tensor(n, sh, dt, side="left"))
        pan2 = [sb3(f"pan2_{i}", [128, 2048], BF16) for i in range(2)]
        ubf = [sb3(f"ubf{i}", [128, TC], F32) for i in range(2)]
        xr3 = [sb3(f"xr3_{i}", [128, TC], F32) for i in range(2)]

        st.begin()
        pan_free = {0: ("t", 0), 1: ("t", 0)}
        u_sp = {}
        rm2_t = []
        xu_v = []
        for m in range(16):
            xld = op(st, "sync", "dma_start", xr3[m % 2][:, :],
                     d_x[m * 128:(m + 1) * 128, :],
                     waits=[("v", xu_v[m - 2])] if m >= 2 else [],
                     inc=f"wx{m % 2}", incn=16)
            tv = None
            pr = (0, 1) if (m % 2 == 0) else (2, 3)
            for hp in range(2):
                wgv = op(st, "gpsimd", "dma_start", pan2[hp][:, :],
                         d_wout[m, :, hp * 2048:(hp + 1) * 2048],
                         waits=[pan_free[hp]], inc=f"wg{hp}", incn=16)
                w0 = [(f"wg{hp}", wgv)] + (st.bw(*pr) if hp == 0 else [])
                if m == 0 and hp == 0:
                    w0.append(("v", y_done))
                for ci in range(16):
                    c = hp * 16 + ci
                    lhs = pan2[hp][:, ci * 128:(ci + 1) * 128]
                    first = (c == 0)
                    last = (c == 31)
                    op(st, "tensor", "matmul", ps[pr[0]][:, :], lhs,
                       Y[:, c * TC:c * TC + NH], start=first, stop=last,
                       waits=w0 if ci == 0 else [])
                    tv = op(st, "tensor", "matmul", ps[pr[1]][:, :], lhs,
                            Y[:, c * TC + NH:(c + 1) * TC], start=first,
                            stop=last, inc="t" if ci == 15 else None)
                pan_free[hp] = ("t", tv)
            wv = [("t", tv), (f"wx{m % 2}", xld)]
            if m >= 2:
                wv += [u_sp[m - 2]]
            vv = op(st, "vector", "tensor_add", XM[:, m * TC:m * TC + NH],
                    xr3[m % 2][:, 0:NH], ps[pr[0]][:, :], waits=wv, inc="v")
            vv = op(st, "vector", "tensor_add", XM[:, m * TC + NH:(m + 1) * TC],
                    xr3[m % 2][:, NH:TC], ps[pr[1]][:, :], inc="v")
            vv = op(st, "vector", "scalar_tensor_tensor", ubf[m % 2][:, 0:NH],
                    xr3[m % 2][:, 0:NH], 2.0, ps[pr[0]][:, :],
                    op0=OP.mult, op1=OP.add, inc="v")
            vv = op(st, "vector", "scalar_tensor_tensor", ubf[m % 2][:, NH:TC],
                    xr3[m % 2][:, NH:TC], 2.0, ps[pr[1]][:, :],
                    op0=OP.mult, op1=OP.add, inc="v")
            xu_v.append(vv)
            st.bset(pr[0], ("v", vv)); st.bset(pr[1], ("v", vv))
            sv = op(st, "scalar", "activation", sqs[m % 2][:, :],
                    XM[:, m * TC:(m + 1) * TC], AF.Square,
                    waits=[("v", vv)] + ([("t", rm2_t[m - 2])] if m >= 2 else []),
                    inc="s")
            op(st, "tensor", "matmul", pstat1[0:1, :], c_onb[:, 0:1],
               sqs[m % 2][:, 0:NH], start=(m == 0), stop=(m == 15),
               waits=[("s", sv)] + (st.bw("s1", "s2") if m == 0 else []))
            tv2 = op(st, "tensor", "matmul", pstat2[0:1, :], c_onb[:, 0:1],
                     sqs[m % 2][:, NH:TC], start=(m == 0), stop=(m == 15),
                     inc="t")
            rm2_t.append(tv2)
            u_sp[m] = (f"wu{m % 2}",
                       op(st, "sync", "dma_start", d_su[:, m * TC:(m + 1) * TC],
                          ubf[m % 2][:, :], waits=[("v", vv)],
                          inc=f"wu{m % 2}", incn=16))
            emit_gamma_piece()
            emit_gamma_piece()
        mm2_t = st.val("t")
        u_sp_all = [u_sp[14], u_sp[15]]
        if "xmid" in debug:
            op(st, "sync", "dma_start", dbg_outs["xmid"], XM[:, :],
               waits=[("v", st.val("v"))], inc="wsc", incn=16)
        if "u" in debug:
            op(st, "sync", "dma_start", dbg_outs["u"], d_su,
               waits=u_sp_all, inc="wsc", incn=16)
        flush(nc, st)

        ph3.close()
        if stop == 3:
            _stopnow(); return nc

        st.begin()
        while gp_idx[0] < len(gpieces):
            emit_gamma_piece()
        gv1 = opn(st, "vector", "reduce_sum", gtmp[:, 0:1], gcols[:, 0:64],
                  axis=AX.X,
                  waits=[g_abs[len(gpieces) - 1], ("v", g_last_add[0])], inc="v")
        gv1 = opn(st, "vector", "reduce_sum", gtmp[:, 1:2], gcols[:, 64:96],
                  axis=AX.X, inc="v")
        gt1 = op(st, "tensor", "matmul", pmisc[0:1, 0:2], c_onf[:, 0:1],
                 gtmp[:, 0:2], start=True, stop=True,
                 waits=[("v", gv1)] + st.bw("mi"), inc="t")
        gv2 = opn(st, "vector", "tensor_scalar", grow[0:1, 0:2], pmisc[0:1, 0:2],
                  1.0 / NW, 1e-5, op0=OP.mult, op1=OP.max,
                  waits=[("t", gt1)], inc="v")
        gv2 = opn(st, "vector", "tensor_scalar", grow[0:1, 2:3], grow[0:1, 0:1],
                  0.5, None, op0=OP.mult, inc="v")
        gv2 = opn(st, "vector", "tensor_scalar", grow[0:1, 3:4], grow[0:1, 1:2],
                  0.5, None, op0=OP.mult, inc="v")
        gv2 = opn(st, "vector", "tensor_scalar", grow[0:1, 4:5], grow[0:1, 2:3],
                  -1.0, None, op0=OP.mult, inc="v")
        gv2 = opn(st, "vector", "tensor_scalar", grow[0:1, 5:6], grow[0:1, 3:4],
                  -1.0, None, op0=OP.mult, inc="v")
        gv2 = opn(st, "vector", "tensor_scalar", grow[0:1, 6:7], grow[0:1, 0:1],
                  1.0 / (127.0 * SQ2048), None, op0=OP.mult, inc="v")
        gv2 = opn(st, "vector", "tensor_scalar", grow[0:1, 7:8], grow[0:1, 1:2],
                  1.0 / (127.0 * 64.0), None, op0=OP.mult, inc="v")
        gt2 = op(st, "tensor", "matmul", pmisc[:, 0:8], c_onf[0:1, :],
                 grow[0:1, 0:8], start=True, stop=True, waits=[("v", gv2)],
                 inc="t")
        gs2 = op(st, "scalar", "copy", gcol[:, 0:8], pmisc[:, 0:8],
                 waits=[("t", gt2)], inc="s")
        gamma_rdy = ("s", gs2)
        st.bset("mi", ("s", gs2))
        if "gam" in debug:
            op(st, "sync", "dma_start", dbg_outs["gam"], grow[0:1, :],
               waits=[("v", gv2)], inc="wsc", incn=16)
        v1 = opn(st, "vector", "tensor_scalar", rows[0:1, 0:NH], pstat1[0:1, :],
                1e-30, None, op0=OP.max, waits=[("t", mm2_t)], inc="v")
        v1 = opn(st, "vector", "tensor_scalar", rows[0:1, NH:TC], pstat2[0:1, :],
                1e-30, None, op0=OP.max, inc="v")
        v1 = opn(st, "vector", "reciprocal", rows2[0:1, 0:TC], rows[0:1, 0:TC],
                inc="v")
        s1 = op(st, "scalar", "activation", rows[0:1, 0:TC], rows2[0:1, 0:TC],
                AF.Sqrt, scale=2048.0, waits=[("v", v1)], inc="s")
        op(st, "tensor", "matmul", pstat1[:, :], c_onf[0:1, :], rows[0:1, 0:NH],
           start=True, stop=True, waits=[("s", s1)])
        tb = op(st, "tensor", "matmul", pstat2[:, :], c_onf[0:1, :],
                rows[0:1, NH:TC], start=True, stop=True, inc="t")
        sb = op(st, "scalar", "copy", bc[:, 0:NH], pstat1[:, :],
                waits=[("t", tb)], inc="s")
        sb = op(st, "scalar", "copy", bc[:, NH:TC], pstat2[:, :], inc="s")
        st.bset("s1", ("s", sb)); st.bset("s2", ("s", sb))
        flush(nc, st)
        phGam.close()

        phT = ExitStack()
        TDEP = 4
        twb = [phT.enter_context(nc.sbuf_tensor(f"twb{i}", [128, 512], F32,
                                                side="left"))
               for i in range(TDEP)]
        tgb = phT.enter_context(nc.sbuf_tensor("tgb", [128, 512], BF16,
                                               side="left"))
        tob = [phT.enter_context(nc.sbuf_tensor(f"tob{i}", [128, 512], BF16,
                                                side="left"))
               for i in range(TDEP)]
        TP = 512
        tpieces = []
        for m in range(32):
            for j in range(4):
                tpieces.append((d_wup[m, :, j * TP:(j + 1) * TP],
                                d_stup[:, m * 2048 + j * TP:m * 2048 + (j + 1) * TP],
                                True))
        for m in range(16):
            for j in range(8):
                tpieces.append((d_wdn[m, :, j * TP:(j + 1) * TP],
                                d_stdn[:, m * 4096 + j * TP:m * 4096 + (j + 1) * TP],
                                False))
        tp_idx = [0]
        tp_sp = {}
        tern_up_done = []
        tern_dn_done = []

        def emit_tern_piece():
            i = tp_idx[0]
            if i >= len(tpieces):
                return
            tp_idx[0] += 1
            src_, dst_, is_up = tpieces[i]
            b = i % TDEP
            sb_ = i % 2
            ld = op(st, "sync", "dma_start", twb[b][:, :], src_,
                    waits=[tp_sp.get(i - TDEP)], inc=f"wgg{sb_}", incn=16)
            wv = [(f"wgg{sb_}", ld)]
            if i == 0:
                wv.append(gamma_rdy)
            ng, pg = (4, 2) if is_up else (5, 3)
            op(st, "vector", "tensor_scalar", tgb[:, :], twb[b][:, :],
               gcol[:, ng:ng + 1], None, op0=OP.is_lt, waits=wv, inc="v")
            tv = op(st, "vector", "scalar_tensor_tensor", tob[b][:, :],
                    twb[b][:, :], gcol[:, pg:pg + 1], tgb[:, :],
                    op0=OP.is_gt, op1=OP.subtract,
                    waits=[tp_sp.get(i - TDEP)], inc="v")
            tp_sp[i] = (f"wt{sb_}",
                        op(st, "gpsimd", "dma_start", dst_, tob[b][:, :],
                           waits=[("v", tv)], inc=f"wt{sb_}", incn=16))
            if i == 127:
                tern_up_done.extend([tp_sp[126], tp_sp[127]])
            if i == 255:
                tern_dn_done.extend([tp_sp[254], tp_sp[255]])

        phB = ExitStack()
        B2 = phB.enter_context(nc.sbuf_tensor("arB2", [128, NCC * TC], BF16,
                                              side="left"))
        st.begin()
        h2_v = 0
        for cc in range(NCC):
            h2_v = op(st, "vector", "tensor_mul", B2[:, cc * TC:(cc + 1) * TC],
                      XM[:, cc * TC:(cc + 1) * TC], bc[:, :],
                      waits=[("s", sb)] if cc == 0 else [], inc="v")
        if "h2" in debug:
            op(st, "sync", "dma_start", dbg_outs["h2"], B2[:, :],
               waits=[("v", h2_v)], inc="wsc", incn=16)
        flush(nc, st)
        arXM.__exit__(None, None, None)
        arY.__exit__(None, None, None)
        if stop == 4:
            _stopnow(); phB.close(); return nc

        hid_cm = nc.sbuf_tensor("hidden", [128, 64 * TC], BF16, side="right")
        HID = hid_cm.__enter__()
        ph5 = ExitStack()
        sb5 = lambda n, sh, dt: ph5.enter_context(nc.sbuf_tensor(n, sh, dt, side="left"))
        pan5 = [sb5(f"pan5_{i}", [128, 1024], BF16) for i in range(2)]

        st.begin()
        pan_free = {0: ("t", 0), 1: ("t", 0)}
        hid_s = 0
        for m in range(64):
            tv = None
            pr = (0, 1) if (m % 2 == 0) else (2, 3)
            for hp in range(2):
                wgv = op(st, "gpsimd", "dma_start", pan5[hp][:, :],
                         d_wm1[m, :, hp * 1024:(hp + 1) * 1024],
                         waits=[pan_free[hp]], inc=f"wg{hp}", incn=16)
                w0 = [(f"wg{hp}", wgv)] + (st.bw(*pr) if hp == 0 else [])
                if m == 0 and hp == 0:
                    w0 += [("v", h2_v)]
                for ci in range(8):
                    c = hp * 8 + ci
                    lhs = pan5[hp][:, ci * 128:(ci + 1) * 128]
                    first = (c == 0)
                    last = (c == NCC - 1)
                    op(st, "tensor", "matmul", ps[pr[0]][:, :], lhs,
                       B2[:, c * TC:c * TC + NH], start=first, stop=last,
                       waits=w0 if ci == 0 else [])
                    tv = op(st, "tensor", "matmul", ps[pr[1]][:, :], lhs,
                            B2[:, c * TC + NH:(c + 1) * TC], start=first,
                            stop=last, inc="t" if ci == 7 else None)
                pan_free[hp] = ("t", tv)
            sv = op(st, "scalar", "activation", HID[:, m * TC:m * TC + NH],
                    ps[pr[0]][:, :], AF.Relu, waits=[("t", tv)], inc="s")
            sv = op(st, "scalar", "activation", HID[:, m * TC + NH:(m + 1) * TC],
                    ps[pr[1]][:, :], AF.Relu, inc="s")
            hid_s = sv
            st.bset(pr[0], ("s", sv)); st.bset(pr[1], ("s", sv))
            for _ in range(3):
                emit_tern_piece()
        flush(nc, st)
        ph5.close()
        phB.close()
        if stop == 5:
            _stopnow(); hid_cm.__exit__(None, None, None); return nc

        ph6 = ExitStack()
        sb6 = lambda n, sh, dt: ph6.enter_context(nc.sbuf_tensor(n, sh, dt, side="left"))
        pan6 = [sb6(f"pan6_{i}", [128, 4096], BF16) for i in range(2)]
        xq6 = [sb6(f"xq6_{i}", [128, TC], F32) for i in range(2)]
        ub6 = [sb6(f"ub6_{i}", [128, TC], F32) for i in range(2)]

        st.begin()
        pan_free = {0: ("t", 0), 1: ("t", 0)}
        ln1_t = []
        xp_sp = {}
        xpre_vs = []
        for m in range(16):
            url = op(st, "sync", "dma_start", ub6[m % 2][:, :],
                     d_su[:, m * TC:(m + 1) * TC],
                     waits=u_sp_all if m < 2 else
                     [("v", xpre_vs[m - 2])], inc=f"wu{m % 2}", incn=16)
            tv = None
            pr = (0, 1) if (m % 2 == 0) else (2, 3)
            for hp in range(2):
                wgv = op(st, "gpsimd", "dma_start", pan6[hp][:, :],
                         d_wm2[m, :, hp * 4096:(hp + 1) * 4096],
                         waits=[pan_free[hp]], inc=f"wg{hp}", incn=16)
                w0 = [(f"wg{hp}", wgv)] + (st.bw(*pr) if hp == 0 else [])
                if m == 0 and hp == 0:
                    w0 += [("s", hid_s)]
                for ci in range(32):
                    c = hp * 32 + ci
                    lhs = pan6[hp][:, ci * 128:(ci + 1) * 128]
                    first = (c == 0)
                    last = (c == 63)
                    op(st, "tensor", "matmul", ps[pr[0]][:, :], lhs,
                       HID[:, c * TC:c * TC + NH], start=first, stop=last,
                       waits=w0 if ci == 0 else [])
                    tv = op(st, "tensor", "matmul", ps[pr[1]][:, :], lhs,
                            HID[:, c * TC + NH:(c + 1) * TC], start=first,
                            stop=last, inc="t" if ci == 31 else None)
                pan_free[hp] = ("t", tv)
            wv = [("t", tv), (f"wu{m % 2}", url)]
            if m >= 2:
                wv.append(xp_sp[m - 2])
            vv = op(st, "vector", "tensor_add", xq6[m % 2][:, 0:NH],
                    ub6[m % 2][:, 0:NH], ps[pr[0]][:, :], waits=wv, inc="v")
            vv = op(st, "vector", "tensor_add", xq6[m % 2][:, NH:TC],
                    ub6[m % 2][:, NH:TC], ps[pr[1]][:, :], inc="v")
            xpre_vs.append(vv)
            st.bset(pr[0], ("v", vv)); st.bset(pr[1], ("v", vv))
            sv = op(st, "scalar", "copy", sqs[0][:, :], xq6[m % 2][:, :],
                    waits=[("v", vv)] + ([("t", ln1_t[m - 1])] if m >= 1 else []),
                    inc="s")
            sv = op(st, "scalar", "activation", sqs[1][:, :], xq6[m % 2][:, :],
                    AF.Square, inc="s")
            wstat = [("s", sv)]
            if m == 0:
                wstat += st.bw("s1", "s2", "mi", "ha")
            op(st, "tensor", "matmul", pstat1[0:1, :], c_onb[:, 0:1],
               sqs[0][:, 0:NH], start=(m == 0), stop=(m == 15), waits=wstat)
            op(st, "tensor", "matmul", pstat2[0:1, :], c_onb[:, 0:1],
               sqs[0][:, NH:TC], start=(m == 0), stop=(m == 15))
            op(st, "tensor", "matmul", pmisc[0:1, :], c_onb[:, 0:1],
               sqs[1][:, 0:NH], start=(m == 0), stop=(m == 15))
            tv2 = op(st, "tensor", "matmul", phalo[0:1, :], c_onb[:, 0:1],
                     sqs[1][:, NH:TC], start=(m == 0), stop=(m == 15), inc="t")
            ln1_t.append(tv2)
            xp_sp[m] = (f"wp{m % 2}",
                        op(st, "sync", "dma_start", d_sxp[:, m * TC:(m + 1) * TC],
                           xq6[m % 2][:, :], waits=[("v", vv)],
                           inc=f"wp{m % 2}", incn=16))
            for _ in range(4):
                emit_tern_piece()
        mm4_t = st.val("t")
        xp_sp_all = [xp_sp[14], xp_sp[15]]
        if "xpre" in debug:
            op(st, "sync", "dma_start", dbg_outs["xpre"], d_sxp,
               waits=xp_sp_all, inc="wsc", incn=16)
        flush(nc, st)
        hid_cm.__exit__(None, None, None)
        if stop == 6:
            _stopnow(); ph6.close(); return nc

        st.begin()
        v1 = opn(st, "vector", "tensor_scalar", rows3[0:1, 0:NH], pstat1[0:1, :],
                1.0 / 2048.0, None, op0=OP.mult, waits=[("t", mm4_t)], inc="v")
        v1 = opn(st, "vector", "tensor_scalar", rows3[0:1, NH:TC], pstat2[0:1, :],
                1.0 / 2048.0, None, op0=OP.mult, inc="v")
        v1 = opn(st, "vector", "tensor_scalar", rows[0:1, 0:NH], pmisc[0:1, :],
                1.0 / 2048.0, None, op0=OP.mult, inc="v")
        v1 = opn(st, "vector", "tensor_scalar", rows[0:1, NH:TC], phalo[0:1, :],
                1.0 / 2048.0, None, op0=OP.mult, inc="v")
        v1 = opn(st, "vector", "tensor_mul", rows2[0:1, 0:TC], rows3[0:1, 0:TC],
                rows3[0:1, 0:TC], inc="v")
        v1 = opn(st, "vector", "scalar_tensor_tensor", rows[0:1, 0:TC],
                rows[0:1, 0:TC], 1e-5, rows2[0:1, 0:TC],
                op0=OP.add, op1=OP.subtract, inc="v")
        v1 = opn(st, "vector", "reciprocal", rows2[0:1, 0:TC], rows[0:1, 0:TC],
                inc="v")
        s1 = op(st, "scalar", "activation", rows[0:1, 0:TC], rows2[0:1, 0:TC],
                AF.Sqrt, waits=[("v", v1)], inc="s")
        v1 = opn(st, "vector", "tensor_mul", rows2[0:1, 0:TC], rows3[0:1, 0:TC],
                rows[0:1, 0:TC], waits=[("s", s1)], inc="v")
        op(st, "tensor", "matmul", pstat1[:, :], c_onf[0:1, :], rows[0:1, 0:NH],
           start=True, stop=True, waits=[("v", v1)])
        op(st, "tensor", "matmul", pstat2[:, :], c_onf[0:1, :], rows[0:1, NH:TC],
           start=True, stop=True)
        op(st, "tensor", "matmul", pmisc[:, :], c_onf[0:1, :], rows2[0:1, 0:NH],
           start=True, stop=True)
        tb = op(st, "tensor", "matmul", phalo[:, :], c_onf[0:1, :],
                rows2[0:1, NH:TC], start=True, stop=True, inc="t")
        sb = op(st, "scalar", "copy", bc[:, 0:NH], pstat1[:, :], waits=[("t", tb)],
                inc="s")
        sb = op(st, "scalar", "copy", bc[:, NH:TC], pstat2[:, :], inc="s")
        sb = op(st, "scalar", "copy", bc2[:, 0:NH], pmisc[:, :], inc="s")
        sb = op(st, "scalar", "copy", bc2[:, NH:TC], phalo[:, :], inc="s")
        for _k in ("s1", "s2", "mi", "ha"):
            st.bset(_k, ("s", sb))
        x1_v = 0
        x1_sp = {}
        for cc in range(NCC):
            rl = op(st, "sync", "dma_start", xq6[cc % 2][:, :],
                    d_sxp[:, cc * TC:(cc + 1) * TC],
                    waits=(xp_sp_all if cc < 2 else []) +
                          ([("v", x1_v)] if cc >= 2 else []),
                    inc=f"wp{cc % 2}", incn=16)
            vv = op(st, "vector", "tensor_mul", ub6[cc % 2][:, :],
                    xq6[cc % 2][:, :], bc[:, :],
                    waits=[(f"wp{cc % 2}", rl)] + ([("s", sb)] if cc == 0 else []) +
                          ([x1_sp[cc - 2]] if cc >= 2 else []),
                    inc="v")
            x1_v = op(st, "vector", "tensor_sub", ub6[cc % 2][:, :],
                      ub6[cc % 2][:, :], bc2[:, :], inc="v")
            x1_sp[cc] = (f"wu{cc % 2}",
                         op(st, "gpsimd", "dma_start",
                            d_sx1[:, cc * TC:(cc + 1) * TC], ub6[cc % 2][:, :],
                            waits=[("v", x1_v)], inc=f"wu{cc % 2}", incn=16))
        x1_sp_all = [x1_sp[NCC - 2], x1_sp[NCC - 1]]
        if "x1" in debug:
            op(st, "sync", "dma_start", dbg_outs["x1"], d_sx1,
               waits=x1_sp_all, inc="wsc", incn=16)
        flush(nc, st)
        ph6.close()
        phT.close()
        if stop == 7:
            _stopnow(); return nc

        ph8 = ExitStack()
        sb8 = lambda n, sh, dt: ph8.enter_context(nc.sbuf_tensor(n, sh, dt, side="left"))
        QA = sb8("qarena", [128, 64 * NQ], BF16)
        X1Q = [sb8(f"x1q{i}", [128, NCC * NQ], BF16) for i in range(3)]
        C2 = [sb8(f"c2_{i}", [128, 32 * NQ], F32) for i in range(2)]
        V1B = sb8("v1b", [128, NCC * NQ], F32)
        QG = 4
        CHB = sb8("chb", [128, 4 * NQ], BF16)
        CLB = sb8("clb", [128, 4 * NQ], BF16)
        SCR1 = sb8("scr1", [128, 8 * NQ], F32)
        QI = sb8("qi8", [128, QG * NQ], I32)
        pant = [sb8(f"pant{i}", [128, 4096], BF16) for i in range(2)]
        rel8 = [sb8("rel8_0", [128, NQ], F32)] * 2

        def rep_ap(buf, off, n):
            rs = buf[:, :].ap[0][0]
            return bass.AP(buf[:, 0:NQ].tensor, off, [[rs, 128], [0, n], [1, NQ]])

        def sview(buf, off, nblk, blkstride, width):
            rs = buf[:, :].ap[0][0]
            return bass.AP(buf[:, 0:NQ].tensor, off,
                           [[rs, 128], [blkstride, nblk], [1, width]])

        scr1_t = [0]
        ld_q = {}
        h1_s = {}
        h3_s = {}
        t1_v = {}
        t3_v = {}
        x1_sb = {}
        vq1_d = {}
        vq2_d = {}
        m5_ev = {}
        mm5_t_d = {}
        mm6_t_d = {}
        m6_s = {}
        xa_v = {}
        out_edge = {}
        pan_free8 = {0: ("t", 0), 1: ("t", 0)}
        ln2_t = []
        d_sx1_r = d_sx1.rearrange("p (c t) -> p c t", c=NCC)
        MS5 = list(range(16, 32)) + list(range(16))

        def butterflies_slab(buf, nch, wait0):
            vlast = 0
            s = nch // 2
            first = True
            while s >= 1:
                nblk = nch // (2 * s)
                width = s * NQ
                av = sview(buf, 0, nblk, 2 * s * NQ, width)
                bv = sview(buf, width, nblk, 2 * s * NQ, width)
                vlast = op(st, "vector", "tensor_add", av, av, bv,
                           waits=list(wait0) if first else [], inc="v")
                vlast = op(st, "vector", "scalar_tensor_tensor", bv, bv,
                           -2.0, av, op0=OP.mult, op1=OP.add, inc="v")
                first = False
                s //= 2
            return vlast

        def max_to_col(wait_v, col_idx):
            tt = 0
            for b_ in range(NB):
                tt = op(st, "tensor", "transpose",
                        pmisc[:, b_ * 128:(b_ + 1) * 128],
                        SCR1[:, b_ * 128:(b_ + 1) * 128], c_idf[:, :],
                        waits=([("v", wait_v)] + st.bw("mi")) if b_ == 0 else [],
                        inc="t" if b_ == NB - 1 else None)
            scr1_t[0] = tt
            vv = 0
            for b_ in range(NB):
                vv = op(st, "vector", "reduce_max",
                        cols[:, col_idx + b_:col_idx + b_ + 1],
                        pmisc[:, b_ * 128:(b_ + 1) * 128], axis=AX.X,
                        waits=[("t", tt)] if b_ == 0 else [], inc="v")
            st.bset("mi", ("v", vv))
            return vv

        def col_to_bcast(col_idx, wait_v, dstbuf, dstoff, rtag=None):
            tt = 0
            for b_ in range(NB):
                tt = op(st, "tensor", "transpose",
                        pmisc[0:1, b_ * 128:(b_ + 1) * 128],
                        cols[:, col_idx + b_:col_idx + b_ + 1], c_idf[:, :],
                        waits=([("v", wait_v)] + st.bw("mi")) if b_ == 0 else [],
                        inc="t" if b_ == NB - 1 else None)
            vv = opn(st, "vector", "tensor_copy", rows[0:1, 0:NQ],
                     pmisc[0:1, 0:NQ], waits=[("t", tt)], inc="v")
            if rtag is not None and rtag in debug:
                op(st, "sync", "dma_start", dbg_outs[rtag], rows[0:1, 0:NQ],
                   waits=[("v", vv)], inc="wsc", incn=16)
            tb_ = op(st, "tensor", "matmul", pmisc[:, 0:NQ], c_onf[0:1, :],
                     rows[0:1, 0:NQ], start=True, stop=True, waits=[("v", vv)],
                     inc="t")
            sb_ = op(st, "scalar", "copy", dstbuf[:, dstoff:dstoff + NQ],
                     pmisc[:, 0:NQ], waits=[("t", tb_)], inc="s")
            st.bset("mi", ("s", sb_))
            return sb_

        def quant(buf, nch, q0col, wait_s, bcbuf, bcoff, extra=()):
            vq = 0
            W = QG * NQ
            bcrep = rep_ap(bcbuf, bcoff, QG)
            for g in range(nch // QG):
                vsl = buf[:, g * W:(g + 1) * W].rearrange(
                    "p (c t) -> p c t", c=QG)
                w = ([("s", wait_s)] + [e for e in extra if e and e[1] > 0]) \
                    if g == 0 else []
                op(st, "vector", "tensor_mul",
                   SCR1[:, 0:W].rearrange("p (c t) -> p c t", c=QG),
                   vsl, bcrep, waits=w, inc="v")
                op(st, "vector", "tensor_scalar", QI[:, 0:W], SCR1[:, 0:W],
                   0.5, None, op0=OP.add, inc="v")
                op(st, "vector", "scalar_tensor_tensor", SCR1[:, 0:W],
                   SCR1[:, 0:W], 0.5, QI[:, 0:W], op0=OP.add, op1=OP.is_lt,
                   inc="v")
                vq = op(st, "vector", "scalar_tensor_tensor",
                        QA[:, q0col + g * W:q0col + (g + 1) * W],
                        SCR1[:, 0:W], -1.0, QI[:, 0:W], op0=OP.mult,
                        op1=OP.add, inc="v")
            return vq

        def emit_LOAD(q):
            w = list(x1_sp_all)
            if q >= 3:
                w.append(("v", xa_v[q - 3]))
            ld_q[q] = op(st, "sync", "dma_start",
                         X1Q[q % 3].rearrange("p (c t) -> p c t", c=NCC),
                         d_sx1_r[:, :, q * NQ:(q + 1) * NQ],
                         waits=w, inc="wq", incn=16)

        def emit_H1(q):
            ev = 0
            for cc in range(NCC):
                bk = cc % 4
                w = st.bw(bk)
                if cc == 0:
                    w = w + [("wq", ld_q[q])]
                tv = op(st, "tensor", "matmul", ps[bk][:, 0:NQ], c_h128[:, :],
                        X1Q[q % 3][:, cc * NQ:(cc + 1) * NQ], start=True, stop=True,
                        waits=w, inc="t")
                sw = [("t", tv)]
                if cc == 0 and q >= 1:
                    sw.append(("v", vq1_d[q - 1]))
                ev = op(st, "scalar", "copy", V1B[:, cc * NQ:(cc + 1) * NQ],
                        ps[bk][:, 0:NQ], waits=sw, inc="s")
                st.bset(bk, ("s", ev))
            h1_s[q] = ev

        def emit_B1(q):
            butterflies_slab(V1B, NCC, [("s", h1_s[q])])

        def emit_T1(q):
            t1_v[q] = op(st, "vector", "tensor_reduce",
                         SCR1[:, 0:NQ].rearrange("p t -> p t ()"),
                         V1B[:, :].rearrange("p (c t) -> p t c", c=NCC),
                         AX.X, OP.max, apply_absolute_value=True,
                         waits=[("t", scr1_t[0])], inc="v")
            if "v1" in debug and q == 0:
                op(st, "sync", "dma_start", dbg_outs["v1"], V1B[:, :],
                   waits=[("v", t1_v[q])], inc="wsc", incn=16)

        def emit_X1(q):
            p = q % 2
            vmx = max_to_col(t1_v[q], 2 * p)
            vc = opn(st, "vector", "tensor_scalar", cols[:, 2 * p:2 * p + NB],
                     cols[:, 2 * p:2 * p + NB], SQ2048 * 1e-5, None,
                     op0=OP.max, waits=[("v", vmx)], inc="v")
            vc = opn(st, "vector", "reciprocal", cols[:, 8:8 + NB],
                     cols[:, 2 * p:2 * p + NB], inc="v")
            vc = opn(st, "vector", "tensor_scalar", cols[:, 8:8 + NB],
                     cols[:, 8:8 + NB], 127.0, None, op0=OP.mult, inc="v")
            x1_sb[q] = col_to_bcast(8, vc, bc, p * NQ,
                                    rtag="c1r" if q == 0 else None)

        def emit_Q1(q):
            p = q % 2
            vq1_d[q] = quant(V1B, NCC, (32 + p * 16) * NQ, x1_sb[q],
                             bc, p * NQ,
                             extra=[("t", mm5_t_d.get(q - 2, 0))])
            if "q1" in debug and q == 0:
                op(st, "sync", "dma_start", dbg_outs["q1"],
                   QA[:, (32 + p * 16) * NQ:(48 + p * 16) * NQ],
                   waits=[("v", vq1_d[q])], inc="wsc", incn=16)

        ha_prev = [("s", 0)]
        hch_e = {i: ("t", 0) for i in range(4)}

        def emit_M5(q, filler=(), inserts=None):
            p = q % 2
            C = C2[p]
            q1off = (32 + p * 16) * NQ
            inserts = inserts or {}
            fl = list(filler)
            fpos = [0]

            def emit_fill(n):
                while n > 0 and fpos[0] < len(fl):
                    fl[fpos[0]]()
                    fpos[0] += 1
                    n -= 1

            cl_v = {}
            LAG = 3
            h3_last = 0
            for idx in range(32 + LAG):
                if idx in inserts:
                    inserts[idx]()
                if idx < 32:
                    m = MS5[idx]
                    emit_fill(2 if idx < 8 else (1 if idx % 3 == 0 else 0))
                    b = idx % 2
                    wld = [pan_free8[b]]
                    if q == 0 and idx < 2:
                        wld += tern_up_done
                    dv = op(st, "gpsimd", "dma_start", pant[b][:, 0:2048],
                            d_stup[:, m * 2048:(m + 1) * 2048],
                            waits=wld, inc=f"wt{b}", incn=16)
                    bk = idx % 4
                    w0 = [(f"wt{b}", dv)] + st.bw(bk)
                    if idx == 0:
                        w0.append(("v", vq1_d[q]))
                    for c in range(NCC):
                        lhs = pant[b][:, c * 128:(c + 1) * 128]
                        tv = op(st, "tensor", "matmul", ps[bk][:, 0:NQ], lhs,
                                QA[:, q1off + c * NQ:q1off + (c + 1) * NQ],
                                start=(c == 0), stop=(c == NCC - 1),
                                waits=w0 if c == 0 else [],
                                inc="t" if c == NCC - 1 else None)
                    pan_free8[b] = ("t", tv)
                    sw = [("t", tv)]
                    if m == 0:
                        sw.append(out_edge.get(q - 2))
                    op(st, "scalar", "activation", rel8[idx % 2][:, :],
                       ps[bk][:, 0:NQ], AF.Relu, waits=sw, inc="s")
                    g2 = op(st, "scalar", "activation",
                            C[:, m * NQ:(m + 1) * NQ],
                            rel8[idx % 2][:, :], AF.Square, inc="s")
                    st.bset(bk, ("s", g2))
                    sl = idx % 4
                    ch = op(st, "scalar", "copy",
                            CHB[:, sl * NQ:(sl + 1) * NQ],
                            C[:, m * NQ:(m + 1) * NQ],
                            waits=[hch_e[sl]], inc="s")
                    cl_v[idx] = op(st, "vector", "tensor_sub",
                                   CLB[:, sl * NQ:(sl + 1) * NQ],
                                   C[:, m * NQ:(m + 1) * NQ],
                                   CHB[:, sl * NQ:(sl + 1) * NQ],
                                   waits=[("s", ch), hch_e[sl]], inc="v")
                if idx >= LAG:
                    jj = idx - LAG
                    mm = MS5[jj]
                    sl = jj % 4
                    hf = jj % 2
                    wtm = [("v", cl_v[jj])]
                    if ha_prev[0][1] > 0:
                        wtm.append(ha_prev[0])
                    op(st, "tensor", "matmul",
                       phalo[:, hf * NQ:(hf + 1) * NQ], c_h128[:, :],
                       CHB[:, sl * NQ:(sl + 1) * NQ], start=True, stop=False,
                       waits=wtm)
                    th = op(st, "tensor", "matmul",
                            phalo[:, hf * NQ:(hf + 1) * NQ], c_h128[:, :],
                            CLB[:, sl * NQ:(sl + 1) * NQ], start=False,
                            stop=True, inc="t")
                    hch_e[sl] = ("t", th)
                    h3_last = op(st, "scalar", "copy",
                                 C[:, mm * NQ:(mm + 1) * NQ],
                                 phalo[:, hf * NQ:(hf + 1) * NQ],
                                 waits=[("t", th)], inc="s")
                    ha_prev[0] = ("s", h3_last)
            emit_fill(len(fl))
            h3_s[q] = h3_last
            mm5_t_d[q] = st.val("t")

        def b3t3_thunks(q):
            p = q % 2
            buf = C2[p]
            thunks = []
            s = 16
            first = True
            while s >= 1:
                nblk = 32 // (2 * s)
                width = s * NQ
                av = sview(buf, 0, nblk, 2 * s * NQ, width)
                bv = sview(buf, width, nblk, 2 * s * NQ, width)
                def t1(av=av, bv=bv, first=first, q=q):
                    w0 = [("s", h3_s[q])] if first else []
                    op(st, "vector", "tensor_add", av, av, bv, waits=w0,
                       inc="v")

                def t2(av=av, bv=bv):
                    op(st, "vector", "scalar_tensor_tensor", bv, bv, -2.0, av,
                       op0=OP.mult, op1=OP.add, inc="v")

                thunks += [t1, t2]
                first = False
                s //= 2

            def t3(q=q, p=p):
                t3_v[q] = op(st, "vector", "tensor_reduce",
                             SCR1[:, 0:NQ].rearrange("p t -> p t ()"),
                             C2[p][:, :].rearrange("p (c t) -> p t c", c=32),
                             AX.X, OP.max, apply_absolute_value=True,
                             waits=[("t", scr1_t[0])], inc="v")

            thunks.append(t3)
            return thunks

        def emit_X3(q):
            p = q % 2
            vmx = max_to_col(t3_v[q], 12)
            vc = opn(st, "vector", "tensor_scalar", cols[:, 16:16 + NB],
                     cols[:, 2 * p:2 * p + NB], gcol[:, 6:7], None, op0=OP.mult,
                     waits=[("v", vmx)], inc="v")
            vc = opn(st, "vector", "tensor_mul", cols[:, 18:18 + NB],
                     cols[:, 16:16 + NB], cols[:, 16:16 + NB], inc="v")
            vc = opn(st, "vector", "reciprocal", cols[:, 20:20 + NB],
                     cols[:, 18:18 + NB], inc="v")
            vc = opn(st, "vector", "tensor_scalar", cols[:, 20:20 + NB],
                     cols[:, 20:20 + NB], 6.4e-4, None, op0=OP.mult, inc="v")
            vc = opn(st, "vector", "tensor_max", cols[:, 12:12 + NB],
                     cols[:, 12:12 + NB], cols[:, 20:20 + NB], inc="v")
            vc = opn(st, "vector", "reciprocal", cols[:, 20:20 + NB],
                     cols[:, 12:12 + NB], inc="v")
            vc2 = opn(st, "vector", "tensor_scalar", cols[:, 20:20 + NB],
                      cols[:, 20:20 + NB], 127.0, None, op0=OP.mult, inc="v")
            sb_c2 = col_to_bcast(20, vc2, bc2, 0)
            va = opn(st, "vector", "tensor_mul", cols[:, 16:16 + NB],
                     cols[:, 18:18 + NB], cols[:, 12:12 + NB], inc="v")
            va = opn(st, "vector", "tensor_scalar", cols[:, 16:16 + NB],
                     cols[:, 16:16 + NB], gcol[:, 7:8], None, op0=OP.mult,
                     inc="v")
            col_to_bcast(16, va, bc2, NQ, rtag="alr" if q == 0 else None)
            return sb_c2

        def emit_Q2(q, sb_c2):
            p = q % 2
            vq2_d[q] = quant(C2[p], 32, 0, sb_c2, bc2, 0,
                             extra=[("t", mm6_t_d.get(q - 1, 0))])
            if "q2" in debug and q == 0:
                op(st, "sync", "dma_start", dbg_outs["q2"], QA[:, 0:32 * NQ],
                   waits=[("v", vq2_d[q])], inc="wsc", incn=16)

        def emit_M6(q, inserts=None):
            p = q % 2
            C = C2[p]
            inserts = inserts or {}
            ev = 0
            pend = []
            for m in range(16):
                if m in inserts:
                    inserts[m]()
                b = m % 2
                wld = [pan_free8[b]]
                if q == 0 and m < 2:
                    wld += tern_dn_done
                dv = op(st, "gpsimd", "dma_start", pant[b][:, :],
                        d_stdn[:, m * 4096:(m + 1) * 4096],
                        waits=wld, inc=f"wt{b}", incn=16)
                bk = m % 4
                w0 = [(f"wt{b}", dv)] + st.bw(bk)
                if m == 0:
                    w0.append(("v", vq2_d[q]))
                for c in range(32):
                    lhs = pant[b][:, c * 128:(c + 1) * 128]
                    tv = op(st, "tensor", "matmul", ps[bk][:, 0:NQ], lhs,
                            QA[:, c * NQ:(c + 1) * NQ], start=(c == 0),
                            stop=(c == 31), waits=w0 if c == 0 else [],
                            inc="t" if c == 31 else None)
                pan_free8[b] = ("t", tv)
                sw = [("t", tv)]
                if m == 0:
                    sw.append(("v", vq2_d[q]))
                ev = op(st, "scalar", "copy", C[:, m * NQ:(m + 1) * NQ],
                        ps[bk][:, 0:NQ], waits=sw, inc="s")
                st.bset(bk, ("s", ev))
                op(st, "vector", "tensor_mul", C[:, m * NQ:(m + 1) * NQ],
                   C[:, m * NQ:(m + 1) * NQ], bc2[:, NQ:2 * NQ],
                   waits=[("s", ev)], inc="v")
                vv2 = op(st, "vector", "tensor_add",
                         C[:, m * NQ:(m + 1) * NQ],
                         C[:, m * NQ:(m + 1) * NQ],
                         X1Q[q % 3][:, m * NQ:(m + 1) * NQ], inc="v")
                while pend and (pend[0][0] + 4 <= m or m == 15):
                    pm, sof, sv = pend.pop(0)
                    wstat = [("s", sv)]
                    if pm == 1:
                        wstat += st.bw("s1", "s2")
                    for j in range(2):
                        op(st, "tensor", "matmul", pstat1[0:1, 0:NQ],
                           c_onb[:, 0:1],
                           sqs[0][:, sof + j * NQ:sof + (j + 1) * NQ],
                           start=(pm == 1 and j == 0),
                           stop=(pm == 15 and j == 1),
                           waits=wstat if j == 0 else [])
                        tv2 = op(st, "tensor", "matmul", pstat2[0:1, 0:NQ],
                                 c_onb[:, 0:1],
                                 sqs[1][:, sof + j * NQ:sof + (j + 1) * NQ],
                                 start=(pm == 1 and j == 0),
                                 stop=(pm == 15 and j == 1),
                                 inc="t" if j == 1 else None)
                    ln2_t.append(tv2)
                if m % 2 == 1:
                    sof = (m // 2 % 2) * 2 * NQ
                    gsl = C[:, (m - 1) * NQ:(m + 1) * NQ]
                    sv = op(st, "scalar", "copy", sqs[0][:, sof:sof + 2 * NQ],
                            gsl, waits=[("v", vv2)] +
                                  ([("t", ln2_t[-1])] if ln2_t else []),
                            inc="s")
                    sv = op(st, "scalar", "activation",
                            sqs[1][:, sof:sof + 2 * NQ],
                            gsl, AF.Square, inc="s")
                    pend.append((m, sof, sv))
                    if m == 15:
                        pm, sof, sv = pend.pop(0)
                        wstat = [("s", sv)]
                        for j in range(2):
                            op(st, "tensor", "matmul", pstat1[0:1, 0:NQ],
                               c_onb[:, 0:1],
                               sqs[0][:, sof + j * NQ:sof + (j + 1) * NQ],
                               start=False, stop=(j == 1),
                               waits=wstat if j == 0 else [])
                            tv2 = op(st, "tensor", "matmul",
                                     pstat2[0:1, 0:NQ], c_onb[:, 0:1],
                                     sqs[1][:, sof + j * NQ:sof + (j + 1) * NQ],
                                     start=False, stop=(j == 1),
                                     inc="t" if j == 1 else None)
                        ln2_t.append(tv2)
            mm6_t_d[q] = st.val("t")
            m6_s[q] = ev
            xa_v[q] = vv2
            if "x2pre" in debug and q == 0:
                op(st, "sync", "dma_start", dbg_outs["x2pre"],
                   C[:, 0:NCC * NQ], waits=[("v", vv2)], inc="wsc", incn=16)

        def emit_L2(q):
            p = q % 2
            C = C2[p]
            c3 = C[:, 0:NCC * NQ].rearrange("p (c t) -> p c t", c=NCC)
            v1 = opn(st, "vector", "tensor_scalar", rows3[0:1, 0:NQ],
                     pstat1[0:1, 0:NQ], 1.0 / 2048.0, None, op0=OP.mult,
                     waits=[("t", ln2_t[-1])], inc="v")
            v1 = opn(st, "vector", "tensor_scalar", rows[0:1, 0:NQ],
                     pstat2[0:1, 0:NQ], 1.0 / 2048.0, None, op0=OP.mult,
                     inc="v")
            v1 = opn(st, "vector", "tensor_mul", rows2[0:1, 0:NQ],
                     rows3[0:1, 0:NQ], rows3[0:1, 0:NQ], inc="v")
            v1 = opn(st, "vector", "scalar_tensor_tensor", rows[0:1, 0:NQ],
                     rows[0:1, 0:NQ], 1e-5, rows2[0:1, 0:NQ],
                     op0=OP.add, op1=OP.subtract, inc="v")
            v1 = opn(st, "vector", "reciprocal", rows2[0:1, 0:NQ],
                     rows[0:1, 0:NQ], inc="v")
            s1 = op(st, "scalar", "activation", rows[0:1, 0:NQ],
                    rows2[0:1, 0:NQ], AF.Sqrt, waits=[("v", v1)], inc="s")
            v1 = opn(st, "vector", "tensor_mul", rows2[0:1, 0:NQ],
                     rows3[0:1, 0:NQ], rows[0:1, 0:NQ], waits=[("s", s1)],
                     inc="v")
            op(st, "tensor", "matmul", pstat1[:, 0:NQ], c_onf[0:1, :],
               rows[0:1, 0:NQ], start=True, stop=True, waits=[("v", v1)])
            tb = op(st, "tensor", "matmul", pstat2[:, 0:NQ], c_onf[0:1, :],
                    rows2[0:1, 0:NQ], start=True, stop=True, inc="t")
            sb_ = op(st, "scalar", "copy", bc2[:, 2 * NQ:3 * NQ],
                     pstat1[:, 0:NQ], waits=[("t", tb)], inc="s")
            sb_ = op(st, "scalar", "copy", bc2[:, 3 * NQ:4 * NQ],
                     pstat2[:, 0:NQ], inc="s")
            st.bset("s1", ("s", sb_)); st.bset("s2", ("s", sb_))
            invrep = rep_ap(bc2, 2 * NQ, NCC)
            muirep = rep_ap(bc2, 3 * NQ, NCC)
            lv = op(st, "vector", "tensor_mul", c3, c3, invrep,
                    waits=[("s", sb_)], inc="v")
            lv = op(st, "vector", "tensor_tensor", c3, c3, muirep,
                    op=OP.subtract, inc="v")
            od = op(st, "sync", "dma_start",
                    d_out.rearrange("(c p) t -> p c t", p=128)[:, :, q * NQ:(q + 1) * NQ],
                    C[:, 0:NCC * NQ].rearrange("p (c t) -> p c t", c=NCC),
                    waits=[("v", lv)], inc="wo0", incn=16)
            out_edge[q] = ("wo0", od)

        st.begin()
        emit_LOAD(0); emit_H1(0); emit_B1(0); emit_T1(0); emit_X1(0); emit_Q1(0)
        sbc2_d = {}
        for k in range(5):
            A = k if k <= 3 else None
            B = k - 1 if k >= 1 else None
            if A is not None:
                ins = {}
                if B is not None and B >= 1:
                    def _l2(B=B):
                        emit_L2(B - 1)
                    ins[2] = _l2
                if A < 3:
                    def _lh(A=A):
                        emit_LOAD(A + 1)
                        emit_H1(A + 1)
                    ins[8] = _lh
                if B is not None:
                    def _x3(B=B):
                        sbc2_d[B] = emit_X3(B)
                    ins[22] = _x3
                emit_M5(A, filler=b3t3_thunks(B) if B is not None else (),
                        inserts=ins)
            elif B is not None:
                emit_L2(B - 1)
                for th in b3t3_thunks(B):
                    th()
                sbc2_d[B] = emit_X3(B)
            if B is not None:
                emit_Q2(B, sbc2_d[B])
                ins6 = {}
                if A is not None and A < 3:
                    def _bt(A=A):
                        emit_B1(A + 1)
                        emit_T1(A + 1)
                    def _x1(A=A):
                        emit_X1(A + 1)
                    ins6[2] = _bt
                    ins6[8] = _x1
                emit_M6(B, inserts=ins6)
                if A is not None and A < 3:
                    emit_Q1(A + 1)
                if B == 3:
                    emit_L2(B)
            elif A is not None and A < 3:
                emit_B1(A + 1)
                emit_T1(A + 1)
                emit_X1(A + 1)
                emit_Q1(A + 1)
        flush(nc, st)

        st.begin()
        wt(st, "sync", "wo0", out_edge[3][1])
        if debug:
            wt(st, "sync", "wsc", st.val("wsc"))
        flush(nc, st)

        ph8.close()

    return nc



def _hadamard128():
    h = np.ones((1, 1), dtype=np.float32)
    while h.shape[0] < 128:
        h = np.block([[h, h], [h, -h]])
    return h


def _prep_weights(W_in, conv_w, conv_b, W_out, W_mlp1, W_mlp2, w_up, w_down):
    c = np.ascontiguousarray

    def panels(WT, ncc, nmc):
        return c(WT.reshape(ncc, 128, nmc, 128).transpose(2, 1, 0, 3)
                 .reshape(nmc, 128, ncc * 128))

    win = panels(W_in.T, 16, 64)
    order = []
    for k in range(32):
        order += [k, 32 + k]
    win = c(win[order])
    wout = panels(W_out.T, 32, 16)
    wm1 = panels(W_mlp1.T, 16, 64)
    wm2 = panels(W_mlp2.T, 64, 16)
    wup = panels(w_up.T, 16, 32)
    wdn = panels(w_down.T, 32, 16)
    cw = c(conv_w[:, 0, :].reshape(32, 128, 4).transpose(1, 2, 0).reshape(128, 128))
    cb = c(conv_b.reshape(32, 128).T)
    h128 = _hadamard128().astype(ml_dtypes.bfloat16)
    identf = np.eye(128, dtype=np.float32)
    onesf = np.ones((128, 128), dtype=np.float32)
    onesb = np.ones((128, 8), dtype=ml_dtypes.bfloat16)
    return dict(win_p=win, wout_p=wout, wm1_p=wm1, wm2_p=wm2, wup_p=wup,
                wdn_p=wdn, c_convw=cw, c_convb=cb, c_h128=h128,
                c_identf=identf, c_onesf=onesf, c_onesb=onesb)


def _prep_core(x, b, h):
    c = np.ascontiguousarray
    xt = c(x[b, h * TC:(h + 1) * TC, :].T)
    if h == 0:
        halo = np.zeros((128, 48), dtype=np.float32)
    else:
        hs = x[b, h * TC - 3:h * TC, :]
        halo = c(hs.T.reshape(16, 128, 3).transpose(1, 0, 2).reshape(128, 48))
    return xt, halo


def kernel(L0_attn_x, L0_attn_W_in, L0_attn_conv_w, L0_attn_conv_b,
           L0_attn_W_out, L0_attn_W_mlp1, L0_attn_W_mlp2,
           L0_ffn_w_up, L0_ffn_w_down):
    x = np.ascontiguousarray(L0_attn_x, dtype=np.float32)
    shared = _prep_weights(
        np.asarray(L0_attn_W_in, np.float32), np.asarray(L0_attn_conv_w, np.float32),
        np.asarray(L0_attn_conv_b, np.float32), np.asarray(L0_attn_W_out, np.float32),
        np.asarray(L0_attn_W_mlp1, np.float32), np.asarray(L0_attn_W_mlp2, np.float32),
        np.asarray(L0_ffn_w_up, np.float32), np.asarray(L0_ffn_w_down, np.float32))

    nc = build(debug=tuple(DBG))
    in_maps = []
    for i in range(8):
        b, hh = i // 2, i % 2
        xt, halo = _prep_core(x, b, hh)
        m = dict(shared)
        m["x_t"] = xt
        m["halo_p"] = halo
        in_maps.append(m)

    res = run_bass_kernel_spmd(nc, in_maps, core_ids=list(range(8)))
    out = np.empty((4, 2048, 2048), dtype=np.float32)
    for i in range(8):
        b, hh = i // 2, i % 2
        out[b, hh * TC:(hh + 1) * TC, :] = res.results[i]["out"].T
    return out


# version 16
# speedup vs baseline: 1.0691x; 1.0395x over previous
import numpy as np
import ml_dtypes
from contextlib import ExitStack

import concourse.bass as bass
import concourse.mybir as mybir
from concourse.bass_utils import run_bass_kernel_spmd

F32 = mybir.dt.float32
BF16 = mybir.dt.bfloat16
I32 = mybir.dt.int32
OP = mybir.AluOpType
AF = mybir.ActivationFunctionType
AX = mybir.AxisListType

D = 2048
TC = 1024
NH = 512
NQ = 256
NB = NQ // 128
NCC = 16
SQ2048 = float(np.sqrt(2048.0))
NW = 8192.0 * 1024.0

DBG = []

ENGS = ("tensor", "vector", "scalar", "sync", "gpsimd")


class St:
    def __init__(self):
        self.c = {k: 0 for k in ("t", "v", "s", "wg", "ws")}
        self.ops = None
        self.sems = None
        self.bank = {k: [] for k in (0, 1, 2, 3, "s1", "s2", "mi", "ha")}

    def begin(self):
        self.ops = {e: [] for e in ENGS}

    def inc(self, k, n=1):
        self.c[k] += n
        return self.c[k]

    def val(self, k):
        return self.c[k]

    def bw(self, *keys):
        out = []
        for k in keys:
            out.extend(e for e in self.bank[k] if e[1] > 0)
        return out

    def bset(self, key, *edges):
        self.bank[key] = list(edges)


def flush(nc, st):
    ops = st.ops
    st.ops = None
    with nc.Block() as blk:
        for eng in ENGS:
            lst = ops[eng]
            if not lst:
                continue

            def mk(lst):
                def body(e):
                    for f in lst:
                        f(e)
                return body

            getattr(blk, eng)(mk(lst))


def op(st, eng, meth, *args, waits=(), inc=None, incn=1, **kw):
    sems = st.sems
    val = st.inc(inc, incn) if inc else None
    waits = [w for w in waits if w is not None and w[1] > 0]

    def f(e):
        for sname, v in waits:
            e.wait_ge(sems[sname], v)
        inst = getattr(e, meth)(*args, **kw)
        if inc:
            inst.then_inc(sems[inc], incn)

    st.ops[eng].append(f)
    return val


def opn(st, eng, meth, *args, waits=(), **kw):
    return op(st, eng, meth, *args,
              waits=list(waits) + [("v", st.val("v"))], **kw)


def wt(st, eng, sname, v):
    sems = st.sems

    def f(e):
        e.wait_ge(sems[sname], v)

    st.ops[eng].append(f)


def build(debug=(), stop=None):
    nc = bass.Bass("TRN2", target_bir_lowering=False,
                   detect_race_conditions=False)
    st = St()

    def _stopnow():
        st.begin()
        wt(st, "sync", "wsc", st.val("wsc"))
        flush(nc, st)


    d_x = nc.dram_tensor("x_t", [D, TC], F32, kind="ExternalInput").ap()
    d_halo = nc.dram_tensor("halo_p", [128, 48], F32, kind="ExternalInput").ap()
    d_win = nc.dram_tensor("win_p", [64, 128, 2048], F32, kind="ExternalInput").ap()
    d_wout = nc.dram_tensor("wout_p", [16, 128, 4096], F32, kind="ExternalInput").ap()
    d_wm1 = nc.dram_tensor("wm1_p", [64, 128, 2048], F32, kind="ExternalInput").ap()
    d_wm2 = nc.dram_tensor("wm2_p", [16, 128, 8192], F32, kind="ExternalInput").ap()
    d_wup = nc.dram_tensor("wup_p", [32, 128, 2048], F32, kind="ExternalInput").ap()
    d_wdn = nc.dram_tensor("wdn_p", [16, 128, 4096], F32, kind="ExternalInput").ap()
    d_h128 = nc.dram_tensor("c_h128", [128, 128], BF16, kind="ExternalInput").ap()
    d_idf = nc.dram_tensor("c_identf", [128, 128], F32, kind="ExternalInput").ap()
    d_onf = nc.dram_tensor("c_onesf", [128, 128], F32, kind="ExternalInput").ap()
    d_onb = nc.dram_tensor("c_onesb", [128, 8], BF16, kind="ExternalInput").ap()
    d_cw = nc.dram_tensor("c_convw", [128, 128], F32, kind="ExternalInput").ap()
    d_cb = nc.dram_tensor("c_convb", [128, 32], F32, kind="ExternalInput").ap()
    d_out = nc.dram_tensor("out", [D, TC], F32, kind="ExternalOutput").ap()
    d_su = nc.dram_tensor("sp_u", [128, NCC * TC], F32, kind="Internal").ap()
    d_sxm = nc.dram_tensor("sp_xmid", [128, NCC * TC], F32, kind="Internal").ap()
    d_sxp = nc.dram_tensor("sp_xpre", [128, NCC * TC], F32, kind="Internal").ap()
    d_sx1 = nc.dram_tensor("sp_x1", [128, NCC * TC], BF16, kind="Internal").ap()
    FP8 = mybir.dt.float8e4
    d_stup = nc.dram_tensor("sp_tup", [128, 32 * 2048], FP8, kind="Internal").ap()
    d_stdn = nc.dram_tensor("sp_tdn", [128, 16 * 4096], FP8, kind="Internal").ap()

    dbg_outs = {}
    for nm, shape, dt in [
        ("h", [128, NCC * TC], BF16), ("y", [128, 32 * TC], BF16),
        ("xmid", [128, NCC * TC], F32), ("u", [128, NCC * TC], F32),
        ("h2", [128, NCC * TC], BF16), ("xpre", [128, NCC * TC], F32),
        ("x1", [128, NCC * TC], BF16), ("v1", [128, NCC * NQ], F32),
        ("q1", [128, NCC * NQ], BF16), ("g", [128, 32 * NQ], F32),
        ("q2", [128, 32 * NQ], BF16), ("x2pre", [128, NCC * NQ], F32),
        ("gam", [1, 8], F32), ("c1r", [1, NQ], F32), ("alr", [1, NQ], F32),
        ("colz", [128, 24], F32),
    ]:
        if nm in debug:
            dbg_outs[nm] = nc.dram_tensor("dbg_" + nm, shape, dt,
                                          kind="ExternalOutput").ap()

    es = ExitStack()
    with es:
        semnames = ("t", "v", "s", "wsc", "wx0", "wx1", "wg0", "wg1",
                    "wm0", "wm1", "wu0", "wu1", "wgg0", "wgg1", "wp0", "wp1",
                    "wq", "wf0", "wf1", "wt0", "wt1", "wo0", "wo1")
        st.sems = {k: es.enter_context(nc.semaphore("sem_" + k))
                   for k in semnames}
        st.c = {k: 0 for k in semnames}

        sbL = lambda n, sh, dt: es.enter_context(nc.sbuf_tensor(n, sh, dt, side="left"))
        c_h128 = sbL("s_h128", [128, 128], BF16)
        c_idf = sbL("c_idf", [128, 128], F32)
        c_onf = sbL("c_onf", [128, 128], F32)
        c_onb = sbL("c_onb", [128, 8], BF16)
        c_cw = sbL("c_cw", [128, 128], F32)
        c_cb = sbL("c_cb", [128, 32], F32)
        halo_x = sbL("halo_x", [128, 48], F32)
        halo_h = sbL("halo_h", [128, 48], BF16)
        halo_sq = sbL("halo_sq", [128, 48], BF16)
        bc = sbL("bc", [128, TC], F32)
        bc2 = sbL("bc2", [128, TC], F32)
        rows = sbL("rows", [1, TC], F32)
        rows2 = sbL("rows2", [1, TC], F32)
        rows3 = sbL("rows3", [1, TC], F32)
        cols = sbL("cols", [128, 24], F32)
        gcols = sbL("gcols", [128, 96], F32)
        gtmp = sbL("gtmp", [128, 2], F32)
        grow = sbL("grow", [1, 8], F32)
        gcol = sbL("gcol", [128, 8], F32)
        sqs = [sbL(f"sqs{i}", [128, TC], BF16) for i in range(2)]

        ps = [es.enter_context(nc.psum_tensor(f"p{i}", [128, NH], F32))
              for i in range(4)]
        pstat1 = es.enter_context(nc.psum_tensor("pstat1", [128, NH], F32))
        pstat2 = es.enter_context(nc.psum_tensor("pstat2", [128, NH], F32))
        pmisc = es.enter_context(nc.psum_tensor("pmisc", [128, NH], F32))
        phalo = es.enter_context(nc.psum_tensor("phalo", [128, NH], F32))

        arY = nc.sbuf_tensor("arY", [128, 32 * TC], BF16, side="right")
        Y = arY.__enter__()
        arH = nc.sbuf_tensor("arH", [128, NCC * TC], BF16, side="right")
        H = arH.__enter__()
        ph1 = ExitStack()
        xrb = [ph1.enter_context(nc.sbuf_tensor(f"xrb{i}", [128, TC], F32,
                                                side="left")) for i in range(2)]

        st.begin()
        for dst, src in [(c_h128, d_h128), (c_idf, d_idf), (c_onf, d_onf),
                         (c_onb, d_onb), (c_cw, d_cw), (c_cb, d_cb),
                         (halo_x, d_halo)]:
            consts_done = op(st, "sync", "dma_start", dst[:, :], src,
                             inc="wsc", incn=16)
        sq_t = []
        sq_s = []
        for cc in range(NCC):
            xld = op(st, "sync", "dma_start", xrb[cc % 2][:, :],
                     d_x[cc * 128:(cc + 1) * 128, :],
                     waits=[("s", sq_s[cc - 2])] if cc >= 2 else [],
                     inc=f"wx{cc % 2}", incn=16)
            sv = op(st, "scalar", "activation", sqs[cc % 2][:, :],
                    xrb[cc % 2][:, :], AF.Square,
                    waits=[(f"wx{cc % 2}", xld)] +
                          ([("t", sq_t[cc - 2])] if cc >= 2 else []),
                    inc="s")
            sq_s.append(sv)
            op(st, "tensor", "matmul", pstat1[0:1, :], c_onb[:, 0:1],
               sqs[cc % 2][:, 0:NH], start=(cc == 0), stop=(cc == NCC - 1),
               waits=[("s", sv), ("wsc", consts_done)])
            tv = op(st, "tensor", "matmul", pstat2[0:1, :], c_onb[:, 0:1],
                    sqs[cc % 2][:, NH:TC], start=(cc == 0), stop=(cc == NCC - 1),
                    inc="t")
            sq_t.append(tv)
        svh = op(st, "scalar", "activation", halo_sq[:, :], halo_x[:, :],
                 AF.Square, waits=[("wsc", consts_done)], inc="s")
        for cc in range(NCC):
            op(st, "tensor", "matmul", pmisc[0:1, 0:3], c_onb[:, 0:1],
               halo_sq[:, cc * 3:(cc + 1) * 3], start=(cc == 0),
               stop=(cc == NCC - 1), waits=[("s", svh)] if cc == 0 else [],
               inc="t" if cc == NCC - 1 else None)

        v1 = opn(st, "vector", "tensor_scalar", rows[0:1, 0:NH], pstat1[0:1, :],
                1e-30, None, op0=OP.max, waits=[("t", st.val("t"))], inc="v")
        v1 = opn(st, "vector", "tensor_scalar", rows[0:1, NH:TC], pstat2[0:1, :],
                1e-30, None, op0=OP.max, inc="v")
        v1 = opn(st, "vector", "tensor_scalar", rows3[0:1, 0:3], pmisc[0:1, 0:3],
                1e-30, None, op0=OP.max, inc="v")
        v1 = opn(st, "vector", "reciprocal", rows2[0:1, 0:TC], rows[0:1, 0:TC],
                inc="v")
        v1 = opn(st, "vector", "reciprocal", rows3[0:1, 4:7], rows3[0:1, 0:3],
                inc="v")
        s1 = op(st, "scalar", "activation", rows[0:1, 0:TC], rows2[0:1, 0:TC],
                AF.Sqrt, scale=2048.0, waits=[("v", v1)], inc="s")
        s1 = op(st, "scalar", "activation", rows3[0:1, 0:3], rows3[0:1, 4:7],
                AF.Sqrt, scale=2048.0, inc="s")
        op(st, "tensor", "matmul", pstat1[:, :], c_onf[0:1, :], rows[0:1, 0:NH],
           start=True, stop=True, waits=[("s", s1)])
        op(st, "tensor", "matmul", pstat2[:, :], c_onf[0:1, :], rows[0:1, NH:TC],
           start=True, stop=True)
        tb = op(st, "tensor", "matmul", pmisc[:, 0:3], c_onf[0:1, :],
                rows3[0:1, 0:3], start=True, stop=True, inc="t")
        sb = op(st, "scalar", "copy", bc[:, 0:NH], pstat1[:, :],
                waits=[("t", tb)], inc="s")
        sb = op(st, "scalar", "copy", bc[:, NH:TC], pstat2[:, :], inc="s")
        sb = op(st, "scalar", "copy", bc2[:, 0:3], pmisc[:, 0:3], inc="s")
        st.bset("s1", ("s", sb)); st.bset("s2", ("s", sb)); st.bset("mi", ("s", sb))
        h_v = 0
        for cc in range(NCC):
            rl = op(st, "sync", "dma_start", xrb[cc % 2][:, :],
                    d_x[cc * 128:(cc + 1) * 128, :],
                    waits=([("s", sq_s[NCC - 1])] if cc < 2 else []) +
                          ([("v", h_v)] if cc >= 2 else []),
                    inc=f"wx{cc % 2}", incn=16)
            h_v = op(st, "vector", "tensor_mul", H[:, cc * TC:(cc + 1) * TC],
                     xrb[cc % 2][:, :], bc[:, :],
                     waits=[(f"wx{cc % 2}", rl)] + ([("s", sb)] if cc == 0 else []),
                     inc="v")
        for cc in range(NCC):
            h_v = op(st, "vector", "tensor_mul", halo_h[:, cc * 3:(cc + 1) * 3],
                     halo_x[:, cc * 3:(cc + 1) * 3], bc2[:, 0:3], inc="v")
        h_done = h_v
        if "h" in debug:
            op(st, "sync", "dma_start", dbg_outs["h"], H[:, :],
               waits=[("v", h_done)], inc="wsc", incn=16)
        flush(nc, st)
        ph1.close()
        if stop == 1:
            _stopnow(); return nc

        phGam = ExitStack()
        gwb5 = [phGam.enter_context(nc.sbuf_tensor(f"gwb5_{i}", [128, 1024], F32,
                                                   side="left")) for i in range(2)]
        gab5 = phGam.enter_context(nc.sbuf_tensor("gab5", [128, 1024], F32,
                                                  side="left"))
        gpieces = []
        for jj in range(64):
            gpieces.append((d_wup[jj // 2, :, (jj % 2) * 1024:(jj % 2 + 1) * 1024],
                            jj))
        for jj in range(32):
            gpieces.append((d_wdn[jj // 2, :, (jj % 2) * 2048:(jj % 2) * 2048 + 1024],
                            64 + jj))
            gpieces.append((d_wdn[jj // 2, :, (jj % 2) * 2048 + 1024:(jj % 2 + 1) * 2048],
                            64 + jj))
        gp_idx = [0]
        g_abs = {}
        g_last_add = [0]

        def emit_gamma_piece():
            i = gp_idx[0]
            if i >= len(gpieces):
                return
            gp_idx[0] += 1
            src_, col = gpieces[i]
            two_rounds = (i >= 64 and (i - 64) % 2 == 1)
            gv = op(st, "sync", "dma_start", gwb5[i % 2][:, :], src_,
                    waits=[g_abs.get(i - 2)], inc=f"wgg{i % 2}", incn=16)
            if not two_rounds:
                g_abs[i] = ("s", op(st, "scalar", "activation", gab5[:, :],
                                    gwb5[i % 2][:, :], AF.Abs,
                                    accum_out=gcols[:, col:col + 1],
                                    waits=[(f"wgg{i % 2}", gv)], inc="s"))
            else:
                sa = op(st, "scalar", "activation", gab5[:, :],
                        gwb5[i % 2][:, :], AF.Abs,
                        accum_out=gtmp[:, 1:2],
                        waits=[(f"wgg{i % 2}", gv), ("v", g_last_add[0])],
                        inc="s")
                g_abs[i] = ("s", sa)
                g_last_add[0] = op(st, "vector", "tensor_add",
                                   gcols[:, col:col + 1], gcols[:, col:col + 1],
                                   gtmp[:, 1:2], waits=[("s", sa)], inc="v")

        ph2 = ExitStack()
        sb2 = lambda n, sh, dt: ph2.enter_context(nc.sbuf_tensor(n, sh, dt, side="left"))
        pan1 = [sb2(f"pan1_{i}", [128, 2048], BF16) for i in range(2)]
        xpb = sb2("xpb", [128, TC + 3], F32)
        szb = sb2("szb", [128, TC], F32)
        xcb = sb2("xcb", [128, TC], F32)
        sxcb = sb2("sxcb", [128, TC], F32)

        st.begin()
        pan_free = {0: ("t", 0), 1: ("t", 0)}
        y_k = {}
        y_v = 0
        for j in range(64):
            k = j // 2
            is_xp = (j % 2 == 0)
            wgv = op(st, "gpsimd", "dma_start", pan1[j % 2][:, :], d_win[j, :, :],
                     waits=[pan_free[j % 2]], inc=f"wg{j % 2}", incn=16)
            pr = (0, 1) if is_xp else (2, 3)
            w0 = [(f"wg{j % 2}", wgv)] + st.bw(*pr)
            if j == 0:
                w0.append(("v", h_done))
            if is_xp:
                w0 += st.bw("ha")
            for c in range(NCC):
                lhs = pan1[j % 2][:, c * 128:(c + 1) * 128]
                last = (c == NCC - 1)
                op(st, "tensor", "matmul", ps[pr[0]][:, :], lhs,
                   H[:, c * TC:c * TC + NH], start=(c == 0), stop=last,
                   waits=w0 if c == 0 else [])
                if is_xp:
                    op(st, "tensor", "matmul", phalo[:, 0:3], lhs,
                       halo_h[:, c * 3:(c + 1) * 3], start=(c == 0), stop=last)
                tv = op(st, "tensor", "matmul", ps[pr[1]][:, :], lhs,
                        H[:, c * TC + NH:(c + 1) * TC], start=(c == 0), stop=last,
                        inc="t" if last else None)
            pan_free[j % 2] = ("t", tv)
            if is_xp:
                sv = op(st, "scalar", "copy", xpb[:, 3:3 + NH], ps[pr[0]][:, :],
                        waits=[("t", tv)] + ([("v", y_k[k - 1])] if k >= 1 else []),
                        inc="s")
                sv = op(st, "scalar", "copy", xpb[:, 3 + NH:3 + TC],
                        ps[pr[1]][:, :], inc="s")
                hvv = op(st, "vector", "tensor_copy", xpb[:, 0:3], phalo[:, 0:3],
                         waits=[("t", tv)] + ([("v", y_k[k - 1])] if k >= 1 else []),
                         inc="v")
                st.bset(pr[0], ("s", sv)); st.bset(pr[1], ("s", sv))
                st.bset("ha", ("v", hvv))
                cv = op(st, "vector", "tensor_scalar", xcb[:, :], xpb[:, 0:TC],
                        c_cw[:, k:k + 1], c_cb[:, k:k + 1],
                        op0=OP.mult, op1=OP.add,
                        waits=[("s", sv), ("v", hvv)], inc="v")
                for tap in range(1, 4):
                    cv = op(st, "vector", "scalar_tensor_tensor", xcb[:, :],
                            xpb[:, tap:tap + TC],
                            c_cw[:, tap * 32 + k:tap * 32 + k + 1],
                            xcb[:, :], op0=OP.mult, op1=OP.add, inc="v")
                sxv = op(st, "scalar", "activation", sxcb[:, :], xcb[:, :],
                         AF.Silu, waits=[("v", cv)], inc="s")
            else:
                zv = op(st, "scalar", "activation", szb[:, 0:NH], ps[pr[0]][:, :],
                        AF.Silu, waits=[("t", tv)], inc="s")
                zv = op(st, "scalar", "activation", szb[:, NH:TC], ps[pr[1]][:, :],
                        AF.Silu, inc="s")
                st.bset(pr[0], ("s", zv)); st.bset(pr[1], ("s", zv))
                y_v = op(st, "vector", "tensor_mul", Y[:, k * TC:(k + 1) * TC],
                         sxcb[:, :], szb[:, :], waits=[("s", zv)], inc="v")
                y_k[k] = y_v
            emit_gamma_piece()
        y_done = y_v
        if "y" in debug:
            op(st, "sync", "dma_start", dbg_outs["y"], Y[:, :],
               waits=[("v", y_done)], inc="wsc", incn=16)
        flush(nc, st)
        ph2.close()
        arH.__exit__(None, None, None)
        if stop == 2:
            _stopnow(); return nc

        arXM = nc.sbuf_tensor("arXM", [128, NCC * TC], F32, side="right")
        XM = arXM.__enter__()
        ph3 = ExitStack()
        sb3 = lambda n, sh, dt: ph3.enter_context(nc.sbuf_tensor(n, sh, dt, side="left"))
        pan2 = [sb3(f"pan2_{i}", [128, 2048], BF16) for i in range(2)]
        ubf = [sb3(f"ubf{i}", [128, TC], F32) for i in range(2)]
        xr3 = [sb3(f"xr3_{i}", [128, TC], F32) for i in range(2)]

        st.begin()
        pan_free = {0: ("t", 0), 1: ("t", 0)}
        u_sp = {}
        rm2_t = []
        xu_v = []
        for m in range(16):
            xld = op(st, "sync", "dma_start", xr3[m % 2][:, :],
                     d_x[m * 128:(m + 1) * 128, :],
                     waits=[("v", xu_v[m - 2])] if m >= 2 else [],
                     inc=f"wx{m % 2}", incn=16)
            tv = None
            pr = (0, 1) if (m % 2 == 0) else (2, 3)
            for hp in range(2):
                wgv = op(st, "gpsimd", "dma_start", pan2[hp][:, :],
                         d_wout[m, :, hp * 2048:(hp + 1) * 2048],
                         waits=[pan_free[hp]], inc=f"wg{hp}", incn=16)
                w0 = [(f"wg{hp}", wgv)] + (st.bw(*pr) if hp == 0 else [])
                if m == 0 and hp == 0:
                    w0.append(("v", y_done))
                for ci in range(16):
                    c = hp * 16 + ci
                    lhs = pan2[hp][:, ci * 128:(ci + 1) * 128]
                    first = (c == 0)
                    last = (c == 31)
                    op(st, "tensor", "matmul", ps[pr[0]][:, :], lhs,
                       Y[:, c * TC:c * TC + NH], start=first, stop=last,
                       waits=w0 if ci == 0 else [])
                    tv = op(st, "tensor", "matmul", ps[pr[1]][:, :], lhs,
                            Y[:, c * TC + NH:(c + 1) * TC], start=first,
                            stop=last, inc="t" if ci == 15 else None)
                pan_free[hp] = ("t", tv)
            wv = [("t", tv), (f"wx{m % 2}", xld)]
            if m >= 2:
                wv += [u_sp[m - 2]]
            vv = op(st, "vector", "tensor_add", XM[:, m * TC:m * TC + NH],
                    xr3[m % 2][:, 0:NH], ps[pr[0]][:, :], waits=wv, inc="v")
            vv = op(st, "vector", "tensor_add", XM[:, m * TC + NH:(m + 1) * TC],
                    xr3[m % 2][:, NH:TC], ps[pr[1]][:, :], inc="v")
            vv = op(st, "vector", "scalar_tensor_tensor", ubf[m % 2][:, 0:NH],
                    xr3[m % 2][:, 0:NH], 2.0, ps[pr[0]][:, :],
                    op0=OP.mult, op1=OP.add, inc="v")
            vv = op(st, "vector", "scalar_tensor_tensor", ubf[m % 2][:, NH:TC],
                    xr3[m % 2][:, NH:TC], 2.0, ps[pr[1]][:, :],
                    op0=OP.mult, op1=OP.add, inc="v")
            xu_v.append(vv)
            st.bset(pr[0], ("v", vv)); st.bset(pr[1], ("v", vv))
            sv = op(st, "scalar", "activation", sqs[m % 2][:, :],
                    XM[:, m * TC:(m + 1) * TC], AF.Square,
                    waits=[("v", vv)] + ([("t", rm2_t[m - 2])] if m >= 2 else []),
                    inc="s")
            op(st, "tensor", "matmul", pstat1[0:1, :], c_onb[:, 0:1],
               sqs[m % 2][:, 0:NH], start=(m == 0), stop=(m == 15),
               waits=[("s", sv)] + (st.bw("s1", "s2") if m == 0 else []))
            tv2 = op(st, "tensor", "matmul", pstat2[0:1, :], c_onb[:, 0:1],
                     sqs[m % 2][:, NH:TC], start=(m == 0), stop=(m == 15),
                     inc="t")
            rm2_t.append(tv2)
            u_sp[m] = (f"wu{m % 2}",
                       op(st, "sync", "dma_start", d_su[:, m * TC:(m + 1) * TC],
                          ubf[m % 2][:, :], waits=[("v", vv)],
                          inc=f"wu{m % 2}", incn=16))
            emit_gamma_piece()
            emit_gamma_piece()
        mm2_t = st.val("t")
        u_sp_all = [u_sp[14], u_sp[15]]
        if "xmid" in debug:
            op(st, "sync", "dma_start", dbg_outs["xmid"], XM[:, :],
               waits=[("v", st.val("v"))], inc="wsc", incn=16)
        if "u" in debug:
            op(st, "sync", "dma_start", dbg_outs["u"], d_su,
               waits=u_sp_all, inc="wsc", incn=16)
        flush(nc, st)

        ph3.close()
        if stop == 3:
            _stopnow(); return nc

        st.begin()
        while gp_idx[0] < len(gpieces):
            emit_gamma_piece()
        gv1 = opn(st, "vector", "reduce_sum", gtmp[:, 0:1], gcols[:, 0:64],
                  axis=AX.X,
                  waits=[g_abs[len(gpieces) - 1], ("v", g_last_add[0])], inc="v")
        gv1 = opn(st, "vector", "reduce_sum", gtmp[:, 1:2], gcols[:, 64:96],
                  axis=AX.X, inc="v")
        gt1 = op(st, "tensor", "matmul", pmisc[0:1, 0:2], c_onf[:, 0:1],
                 gtmp[:, 0:2], start=True, stop=True,
                 waits=[("v", gv1)] + st.bw("mi"), inc="t")
        gv2 = opn(st, "vector", "tensor_scalar", grow[0:1, 0:2], pmisc[0:1, 0:2],
                  1.0 / NW, 1e-5, op0=OP.mult, op1=OP.max,
                  waits=[("t", gt1)], inc="v")
        gv2 = opn(st, "vector", "tensor_scalar", grow[0:1, 2:3], grow[0:1, 0:1],
                  0.5, None, op0=OP.mult, inc="v")
        gv2 = opn(st, "vector", "tensor_scalar", grow[0:1, 3:4], grow[0:1, 1:2],
                  0.5, None, op0=OP.mult, inc="v")
        gv2 = opn(st, "vector", "tensor_scalar", grow[0:1, 4:5], grow[0:1, 2:3],
                  -1.0, None, op0=OP.mult, inc="v")
        gv2 = opn(st, "vector", "tensor_scalar", grow[0:1, 5:6], grow[0:1, 3:4],
                  -1.0, None, op0=OP.mult, inc="v")
        gv2 = opn(st, "vector", "tensor_scalar", grow[0:1, 6:7], grow[0:1, 0:1],
                  1.0 / (127.0 * SQ2048), None, op0=OP.mult, inc="v")
        gv2 = opn(st, "vector", "tensor_scalar", grow[0:1, 7:8], grow[0:1, 1:2],
                  1.0 / (127.0 * 64.0), None, op0=OP.mult, inc="v")
        gt2 = op(st, "tensor", "matmul", pmisc[:, 0:8], c_onf[0:1, :],
                 grow[0:1, 0:8], start=True, stop=True, waits=[("v", gv2)],
                 inc="t")
        gs2 = op(st, "scalar", "copy", gcol[:, 0:8], pmisc[:, 0:8],
                 waits=[("t", gt2)], inc="s")
        gamma_rdy = ("s", gs2)
        st.bset("mi", ("s", gs2))
        if "gam" in debug:
            op(st, "sync", "dma_start", dbg_outs["gam"], grow[0:1, :],
               waits=[("v", gv2)], inc="wsc", incn=16)
        v1 = opn(st, "vector", "tensor_scalar", rows[0:1, 0:NH], pstat1[0:1, :],
                1e-30, None, op0=OP.max, waits=[("t", mm2_t)], inc="v")
        v1 = opn(st, "vector", "tensor_scalar", rows[0:1, NH:TC], pstat2[0:1, :],
                1e-30, None, op0=OP.max, inc="v")
        v1 = opn(st, "vector", "reciprocal", rows2[0:1, 0:TC], rows[0:1, 0:TC],
                inc="v")
        s1 = op(st, "scalar", "activation", rows[0:1, 0:TC], rows2[0:1, 0:TC],
                AF.Sqrt, scale=2048.0, waits=[("v", v1)], inc="s")
        op(st, "tensor", "matmul", pstat1[:, :], c_onf[0:1, :], rows[0:1, 0:NH],
           start=True, stop=True, waits=[("s", s1)])
        tb = op(st, "tensor", "matmul", pstat2[:, :], c_onf[0:1, :],
                rows[0:1, NH:TC], start=True, stop=True, inc="t")
        sb = op(st, "scalar", "copy", bc[:, 0:NH], pstat1[:, :],
                waits=[("t", tb)], inc="s")
        sb = op(st, "scalar", "copy", bc[:, NH:TC], pstat2[:, :], inc="s")
        st.bset("s1", ("s", sb)); st.bset("s2", ("s", sb))
        flush(nc, st)
        phGam.close()

        phT = ExitStack()
        TDEP = 4
        twb = [phT.enter_context(nc.sbuf_tensor(f"twb{i}", [128, 512], F32,
                                                side="left"))
               for i in range(TDEP)]
        tgb = phT.enter_context(nc.sbuf_tensor("tgb", [128, 512], BF16,
                                               side="left"))
        tob = [phT.enter_context(nc.sbuf_tensor(f"tob{i}", [128, 512], BF16,
                                                side="left"))
               for i in range(TDEP)]
        TP = 512
        tpieces = []
        for m in range(32):
            for j in range(4):
                tpieces.append((d_wup[m, :, j * TP:(j + 1) * TP],
                                d_stup[:, m * 2048 + j * TP:m * 2048 + (j + 1) * TP],
                                True))
        for m in range(16):
            for j in range(8):
                tpieces.append((d_wdn[m, :, j * TP:(j + 1) * TP],
                                d_stdn[:, m * 4096 + j * TP:m * 4096 + (j + 1) * TP],
                                False))
        tp_idx = [0]
        tp_sp = {}
        tern_up_done = []
        tern_dn_done = []

        def emit_tern_piece():
            i = tp_idx[0]
            if i >= len(tpieces):
                return
            tp_idx[0] += 1
            src_, dst_, is_up = tpieces[i]
            b = i % TDEP
            lsem = ("wgg0", "wgg1", "wo1", "wf1")[b]
            ssem = ("wt0", "wt1", "wx0", "wx1")[b]
            ld = op(st, "sync", "dma_start", twb[b][:, :], src_,
                    waits=[tp_sp.get(i - TDEP)], inc=lsem, incn=16)
            wv = [(lsem, ld)]
            if i == 0:
                wv.append(gamma_rdy)
            ng, pg = (4, 2) if is_up else (5, 3)
            op(st, "vector", "tensor_scalar", tgb[:, :], twb[b][:, :],
               gcol[:, ng:ng + 1], None, op0=OP.is_lt, waits=wv, inc="v")
            tv = op(st, "vector", "scalar_tensor_tensor", tob[b][:, :],
                    twb[b][:, :], gcol[:, pg:pg + 1], tgb[:, :],
                    op0=OP.is_gt, op1=OP.subtract,
                    waits=[tp_sp.get(i - TDEP)], inc="v")
            tp_sp[i] = (ssem,
                        op(st, "gpsimd", "dma_start", dst_, tob[b][:, :],
                           waits=[("v", tv)], inc=ssem, incn=16))
            if i == 127:
                tern_up_done.extend([tp_sp[j] for j in range(124, 128)])
            if i == 255:
                tern_dn_done.extend([tp_sp[j] for j in range(252, 256)])

        phB = ExitStack()
        B2 = phB.enter_context(nc.sbuf_tensor("arB2", [128, NCC * TC], BF16,
                                              side="left"))
        st.begin()
        h2_v = 0
        for cc in range(NCC):
            h2_v = op(st, "vector", "tensor_mul", B2[:, cc * TC:(cc + 1) * TC],
                      XM[:, cc * TC:(cc + 1) * TC], bc[:, :],
                      waits=[("s", sb)] if cc == 0 else [], inc="v")
        if "h2" in debug:
            op(st, "sync", "dma_start", dbg_outs["h2"], B2[:, :],
               waits=[("v", h2_v)], inc="wsc", incn=16)
        flush(nc, st)
        arXM.__exit__(None, None, None)
        arY.__exit__(None, None, None)
        if stop == 4:
            _stopnow(); phB.close(); return nc

        hid_cm = nc.sbuf_tensor("hidden", [128, 64 * TC], BF16, side="right")
        HID = hid_cm.__enter__()
        ph5 = ExitStack()
        sb5 = lambda n, sh, dt: ph5.enter_context(nc.sbuf_tensor(n, sh, dt, side="left"))
        pan5 = [sb5(f"pan5_{i}", [128, 1024], BF16) for i in range(4)]

        st.begin()
        pan_free = {i: ("t", 0) for i in range(4)}
        hid_s = 0
        for m in range(64):
            tv = None
            pr = (0, 1) if (m % 2 == 0) else (2, 3)
            for hp in range(2):
                sl5 = (2 * m + hp) % 4
                sem5 = ("wg0", "wg1", "wm0", "wm1")[sl5]
                wgv = op(st, "gpsimd", "dma_start", pan5[sl5][:, :],
                         d_wm1[m, :, hp * 1024:(hp + 1) * 1024],
                         waits=[pan_free[sl5]], inc=sem5, incn=16)
                w0 = [(sem5, wgv)] + (st.bw(*pr) if hp == 0 else [])
                if m == 0 and hp == 0:
                    w0 += [("v", h2_v)]
                for ci in range(8):
                    c = hp * 8 + ci
                    lhs = pan5[sl5][:, ci * 128:(ci + 1) * 128]
                    first = (c == 0)
                    last = (c == NCC - 1)
                    op(st, "tensor", "matmul", ps[pr[0]][:, :], lhs,
                       B2[:, c * TC:c * TC + NH], start=first, stop=last,
                       waits=w0 if ci == 0 else [])
                    tv = op(st, "tensor", "matmul", ps[pr[1]][:, :], lhs,
                            B2[:, c * TC + NH:(c + 1) * TC], start=first,
                            stop=last, inc="t" if ci == 7 else None)
                pan_free[sl5] = ("t", tv)
            sv = op(st, "scalar", "activation", HID[:, m * TC:m * TC + NH],
                    ps[pr[0]][:, :], AF.Relu, waits=[("t", tv)], inc="s")
            sv = op(st, "scalar", "activation", HID[:, m * TC + NH:(m + 1) * TC],
                    ps[pr[1]][:, :], AF.Relu, inc="s")
            hid_s = sv
            st.bset(pr[0], ("s", sv)); st.bset(pr[1], ("s", sv))
            for _ in range(3):
                emit_tern_piece()
        flush(nc, st)
        ph5.close()
        phB.close()
        if stop == 5:
            _stopnow(); hid_cm.__exit__(None, None, None); return nc

        ph6 = ExitStack()
        sb6 = lambda n, sh, dt: ph6.enter_context(nc.sbuf_tensor(n, sh, dt, side="left"))
        pan6 = [sb6(f"pan6_{i}", [128, 4096], BF16) for i in range(3)]
        xq6 = [sb6(f"xq6_{i}", [128, TC], F32) for i in range(2)]
        ub6 = [sb6(f"ub6_{i}", [128, TC], F32) for i in range(2)]

        st.begin()
        pan_free = {i: ("t", 0) for i in range(3)}
        ln1_t = []
        xp_sp = {}
        xpre_vs = []
        for m in range(16):
            url = op(st, "sync", "dma_start", ub6[m % 2][:, :],
                     d_su[:, m * TC:(m + 1) * TC],
                     waits=u_sp_all if m < 2 else
                     [("v", xpre_vs[m - 2])], inc=f"wu{m % 2}", incn=16)
            tv = None
            pr = (0, 1) if (m % 2 == 0) else (2, 3)
            for hp in range(2):
                sl6 = (2 * m + hp) % 3
                sem6 = ("wg0", "wg1", "wf0")[sl6]
                wgv = op(st, "gpsimd", "dma_start", pan6[sl6][:, :],
                         d_wm2[m, :, hp * 4096:(hp + 1) * 4096],
                         waits=[pan_free[sl6]], inc=sem6, incn=16)
                w0 = [(sem6, wgv)] + (st.bw(*pr) if hp == 0 else [])
                if m == 0 and hp == 0:
                    w0 += [("s", hid_s)]
                for ci in range(32):
                    c = hp * 32 + ci
                    lhs = pan6[sl6][:, ci * 128:(ci + 1) * 128]
                    first = (c == 0)
                    last = (c == 63)
                    op(st, "tensor", "matmul", ps[pr[0]][:, :], lhs,
                       HID[:, c * TC:c * TC + NH], start=first, stop=last,
                       waits=w0 if ci == 0 else [])
                    tv = op(st, "tensor", "matmul", ps[pr[1]][:, :], lhs,
                            HID[:, c * TC + NH:(c + 1) * TC], start=first,
                            stop=last, inc="t" if ci == 31 else None)
                pan_free[sl6] = ("t", tv)
            wv = [("t", tv), (f"wu{m % 2}", url)]
            if m >= 2:
                wv.append(xp_sp[m - 2])
            vv = op(st, "vector", "tensor_add", xq6[m % 2][:, 0:NH],
                    ub6[m % 2][:, 0:NH], ps[pr[0]][:, :], waits=wv, inc="v")
            vv = op(st, "vector", "tensor_add", xq6[m % 2][:, NH:TC],
                    ub6[m % 2][:, NH:TC], ps[pr[1]][:, :], inc="v")
            xpre_vs.append(vv)
            st.bset(pr[0], ("v", vv)); st.bset(pr[1], ("v", vv))
            sv = op(st, "scalar", "copy", sqs[0][:, :], xq6[m % 2][:, :],
                    waits=[("v", vv)] + ([("t", ln1_t[m - 1])] if m >= 1 else []),
                    inc="s")
            sv = op(st, "scalar", "activation", sqs[1][:, :], xq6[m % 2][:, :],
                    AF.Square, inc="s")
            wstat = [("s", sv)]
            if m == 0:
                wstat += st.bw("s1", "s2", "mi", "ha")
            op(st, "tensor", "matmul", pstat1[0:1, :], c_onb[:, 0:1],
               sqs[0][:, 0:NH], start=(m == 0), stop=(m == 15), waits=wstat)
            op(st, "tensor", "matmul", pstat2[0:1, :], c_onb[:, 0:1],
               sqs[0][:, NH:TC], start=(m == 0), stop=(m == 15))
            op(st, "tensor", "matmul", pmisc[0:1, :], c_onb[:, 0:1],
               sqs[1][:, 0:NH], start=(m == 0), stop=(m == 15))
            tv2 = op(st, "tensor", "matmul", phalo[0:1, :], c_onb[:, 0:1],
                     sqs[1][:, NH:TC], start=(m == 0), stop=(m == 15), inc="t")
            ln1_t.append(tv2)
            xp_sp[m] = (f"wp{m % 2}",
                        op(st, "sync", "dma_start", d_sxp[:, m * TC:(m + 1) * TC],
                           xq6[m % 2][:, :], waits=[("v", vv)],
                           inc=f"wp{m % 2}", incn=16))
            for _ in range(4):
                emit_tern_piece()
        mm4_t = st.val("t")
        xp_sp_all = [xp_sp[14], xp_sp[15]]
        if "xpre" in debug:
            op(st, "sync", "dma_start", dbg_outs["xpre"], d_sxp,
               waits=xp_sp_all, inc="wsc", incn=16)
        flush(nc, st)
        hid_cm.__exit__(None, None, None)
        if stop == 6:
            _stopnow(); ph6.close(); return nc

        st.begin()
        v1 = opn(st, "vector", "tensor_scalar", rows3[0:1, 0:NH], pstat1[0:1, :],
                1.0 / 2048.0, None, op0=OP.mult, waits=[("t", mm4_t)], inc="v")
        v1 = opn(st, "vector", "tensor_scalar", rows3[0:1, NH:TC], pstat2[0:1, :],
                1.0 / 2048.0, None, op0=OP.mult, inc="v")
        v1 = opn(st, "vector", "tensor_scalar", rows[0:1, 0:NH], pmisc[0:1, :],
                1.0 / 2048.0, None, op0=OP.mult, inc="v")
        v1 = opn(st, "vector", "tensor_scalar", rows[0:1, NH:TC], phalo[0:1, :],
                1.0 / 2048.0, None, op0=OP.mult, inc="v")
        v1 = opn(st, "vector", "tensor_mul", rows2[0:1, 0:TC], rows3[0:1, 0:TC],
                rows3[0:1, 0:TC], inc="v")
        v1 = opn(st, "vector", "scalar_tensor_tensor", rows[0:1, 0:TC],
                rows[0:1, 0:TC], 1e-5, rows2[0:1, 0:TC],
                op0=OP.add, op1=OP.subtract, inc="v")
        v1 = opn(st, "vector", "reciprocal", rows2[0:1, 0:TC], rows[0:1, 0:TC],
                inc="v")
        s1 = op(st, "scalar", "activation", rows[0:1, 0:TC], rows2[0:1, 0:TC],
                AF.Sqrt, waits=[("v", v1)], inc="s")
        v1 = opn(st, "vector", "tensor_mul", rows2[0:1, 0:TC], rows3[0:1, 0:TC],
                rows[0:1, 0:TC], waits=[("s", s1)], inc="v")
        op(st, "tensor", "matmul", pstat1[:, :], c_onf[0:1, :], rows[0:1, 0:NH],
           start=True, stop=True, waits=[("v", v1)])
        op(st, "tensor", "matmul", pstat2[:, :], c_onf[0:1, :], rows[0:1, NH:TC],
           start=True, stop=True)
        op(st, "tensor", "matmul", pmisc[:, :], c_onf[0:1, :], rows2[0:1, 0:NH],
           start=True, stop=True)
        tb = op(st, "tensor", "matmul", phalo[:, :], c_onf[0:1, :],
                rows2[0:1, NH:TC], start=True, stop=True, inc="t")
        sb = op(st, "scalar", "copy", bc[:, 0:NH], pstat1[:, :], waits=[("t", tb)],
                inc="s")
        sb = op(st, "scalar", "copy", bc[:, NH:TC], pstat2[:, :], inc="s")
        sb = op(st, "scalar", "copy", bc2[:, 0:NH], pmisc[:, :], inc="s")
        sb = op(st, "scalar", "copy", bc2[:, NH:TC], phalo[:, :], inc="s")
        for _k in ("s1", "s2", "mi", "ha"):
            st.bset(_k, ("s", sb))
        x1_v = 0
        x1_sp = {}
        for cc in range(NCC):
            rl = op(st, "sync", "dma_start", xq6[cc % 2][:, :],
                    d_sxp[:, cc * TC:(cc + 1) * TC],
                    waits=(xp_sp_all if cc < 2 else []) +
                          ([("v", x1_v)] if cc >= 2 else []),
                    inc=f"wp{cc % 2}", incn=16)
            vv = op(st, "vector", "tensor_mul", ub6[cc % 2][:, :],
                    xq6[cc % 2][:, :], bc[:, :],
                    waits=[(f"wp{cc % 2}", rl)] + ([("s", sb)] if cc == 0 else []) +
                          ([x1_sp[cc - 2]] if cc >= 2 else []),
                    inc="v")
            x1_v = op(st, "vector", "tensor_sub", ub6[cc % 2][:, :],
                      ub6[cc % 2][:, :], bc2[:, :], inc="v")
            x1_sp[cc] = (f"wu{cc % 2}",
                         op(st, "gpsimd", "dma_start",
                            d_sx1[:, cc * TC:(cc + 1) * TC], ub6[cc % 2][:, :],
                            waits=[("v", x1_v)], inc=f"wu{cc % 2}", incn=16))
        x1_sp_all = [x1_sp[NCC - 2], x1_sp[NCC - 1]]
        if "x1" in debug:
            op(st, "sync", "dma_start", dbg_outs["x1"], d_sx1,
               waits=x1_sp_all, inc="wsc", incn=16)
        flush(nc, st)
        ph6.close()
        phT.close()
        if stop == 7:
            _stopnow(); return nc

        ph8 = ExitStack()
        sb8 = lambda n, sh, dt: ph8.enter_context(nc.sbuf_tensor(n, sh, dt, side="left"))
        QA = sb8("qarena", [128, 64 * NQ], BF16)
        X1Q = [sb8(f"x1q{i}", [128, NCC * NQ], BF16) for i in range(3)]
        C2 = [sb8(f"c2_{i}", [128, 32 * NQ], F32) for i in range(2)]
        V1B = sb8("v1b", [128, NCC * NQ], F32)
        QG = 4
        CHB = sb8("chb", [128, 4 * NQ], BF16)
        CLB = sb8("clb", [128, 4 * NQ], BF16)
        SCR1 = sb8("scr1", [128, 8 * NQ], F32)
        QI = sb8("qi8", [128, QG * NQ], I32)
        pant = [sb8(f"pant{i}", [128, 4096], BF16) for i in range(2)]
        rel8 = [sb8("rel8_0", [128, NQ], F32)] * 2

        def rep_ap(buf, off, n):
            rs = buf[:, :].ap[0][0]
            return bass.AP(buf[:, 0:NQ].tensor, off, [[rs, 128], [0, n], [1, NQ]])

        def sview(buf, off, nblk, blkstride, width):
            rs = buf[:, :].ap[0][0]
            return bass.AP(buf[:, 0:NQ].tensor, off,
                           [[rs, 128], [blkstride, nblk], [1, width]])

        scr1_t = [0]
        ld_q = {}
        h1_s = {}
        h3_s = {}
        t1_v = {}
        t3_v = {}
        x1_sb = {}
        vq1_d = {}
        vq2_d = {}
        m5_ev = {}
        mm5_t_d = {}
        mm6_t_d = {}
        m6_s = {}
        xa_v = {}
        out_edge = {}
        pan_free8 = {0: ("t", 0), 1: ("t", 0)}
        ln2_t = []
        d_sx1_r = d_sx1.rearrange("p (c t) -> p c t", c=NCC)
        MS5 = list(range(16, 32)) + list(range(16))

        def butterflies_slab(buf, nch, wait0):
            vlast = 0
            s = nch // 2
            first = True
            while s >= 1:
                nblk = nch // (2 * s)
                width = s * NQ
                av = sview(buf, 0, nblk, 2 * s * NQ, width)
                bv = sview(buf, width, nblk, 2 * s * NQ, width)
                vlast = op(st, "vector", "tensor_add", av, av, bv,
                           waits=list(wait0) if first else [], inc="v")
                vlast = op(st, "vector", "scalar_tensor_tensor", bv, bv,
                           -2.0, av, op0=OP.mult, op1=OP.add, inc="v")
                first = False
                s //= 2
            return vlast

        def max_to_col(wait_v, col_idx):
            tt = 0
            for b_ in range(NB):
                tt = op(st, "tensor", "transpose",
                        pmisc[:, b_ * 128:(b_ + 1) * 128],
                        SCR1[:, b_ * 128:(b_ + 1) * 128], c_idf[:, :],
                        waits=([("v", wait_v)] + st.bw("mi")) if b_ == 0 else [],
                        inc="t" if b_ == NB - 1 else None)
            scr1_t[0] = tt
            vv = 0
            for b_ in range(NB):
                vv = op(st, "vector", "reduce_max",
                        cols[:, col_idx + b_:col_idx + b_ + 1],
                        pmisc[:, b_ * 128:(b_ + 1) * 128], axis=AX.X,
                        waits=[("t", tt)] if b_ == 0 else [], inc="v")
            st.bset("mi", ("v", vv))
            return vv

        def col_to_bcast(col_idx, wait_v, dstbuf, dstoff, rtag=None):
            tt = 0
            for b_ in range(NB):
                tt = op(st, "tensor", "transpose",
                        pmisc[0:1, b_ * 128:(b_ + 1) * 128],
                        cols[:, col_idx + b_:col_idx + b_ + 1], c_idf[:, :],
                        waits=([("v", wait_v)] + st.bw("mi")) if b_ == 0 else [],
                        inc="t" if b_ == NB - 1 else None)
            vv = opn(st, "vector", "tensor_copy", rows[0:1, 0:NQ],
                     pmisc[0:1, 0:NQ], waits=[("t", tt)], inc="v")
            if rtag is not None and rtag in debug:
                op(st, "sync", "dma_start", dbg_outs[rtag], rows[0:1, 0:NQ],
                   waits=[("v", vv)], inc="wsc", incn=16)
            tb_ = op(st, "tensor", "matmul", pmisc[:, 0:NQ], c_onf[0:1, :],
                     rows[0:1, 0:NQ], start=True, stop=True, waits=[("v", vv)],
                     inc="t")
            sb_ = op(st, "scalar", "copy", dstbuf[:, dstoff:dstoff + NQ],
                     pmisc[:, 0:NQ], waits=[("t", tb_)], inc="s")
            st.bset("mi", ("s", sb_))
            return sb_

        def quant(buf, nch, q0col, wait_s, bcbuf, bcoff, extra=()):
            vq = 0
            W = QG * NQ
            bcrep = rep_ap(bcbuf, bcoff, QG)
            for g in range(nch // QG):
                vsl = buf[:, g * W:(g + 1) * W].rearrange(
                    "p (c t) -> p c t", c=QG)
                w = ([("s", wait_s)] + [e for e in extra if e and e[1] > 0]) \
                    if g == 0 else []
                op(st, "vector", "tensor_mul",
                   SCR1[:, 0:W].rearrange("p (c t) -> p c t", c=QG),
                   vsl, bcrep, waits=w, inc="v")
                op(st, "vector", "tensor_scalar", QI[:, 0:W], SCR1[:, 0:W],
                   0.5, None, op0=OP.add, inc="v")
                op(st, "vector", "scalar_tensor_tensor", SCR1[:, 0:W],
                   SCR1[:, 0:W], 0.5, QI[:, 0:W], op0=OP.add, op1=OP.is_lt,
                   inc="v")
                vq = op(st, "vector", "scalar_tensor_tensor",
                        QA[:, q0col + g * W:q0col + (g + 1) * W],
                        SCR1[:, 0:W], -1.0, QI[:, 0:W], op0=OP.mult,
                        op1=OP.add, inc="v")
            return vq

        def emit_LOAD(q):
            w = list(x1_sp_all)
            if q >= 3:
                w.append(("v", xa_v[q - 3]))
            ld_q[q] = op(st, "sync", "dma_start",
                         X1Q[q % 3].rearrange("p (c t) -> p c t", c=NCC),
                         d_sx1_r[:, :, q * NQ:(q + 1) * NQ],
                         waits=w, inc="wq", incn=16)

        def emit_H1(q):
            ev = 0
            for cc in range(NCC):
                bk = cc % 4
                w = st.bw(bk)
                if cc == 0:
                    w = w + [("wq", ld_q[q])]
                tv = op(st, "tensor", "matmul", ps[bk][:, 0:NQ], c_h128[:, :],
                        X1Q[q % 3][:, cc * NQ:(cc + 1) * NQ], start=True, stop=True,
                        waits=w, inc="t")
                sw = [("t", tv)]
                if cc == 0 and q >= 1:
                    sw.append(("v", vq1_d[q - 1]))
                ev = op(st, "scalar", "copy", V1B[:, cc * NQ:(cc + 1) * NQ],
                        ps[bk][:, 0:NQ], waits=sw, inc="s")
                st.bset(bk, ("s", ev))
            h1_s[q] = ev

        def emit_B1(q):
            butterflies_slab(V1B, NCC, [("s", h1_s[q])])

        def emit_T1(q):
            t1_v[q] = op(st, "vector", "tensor_reduce",
                         SCR1[:, 0:NQ].rearrange("p t -> p t ()"),
                         V1B[:, :].rearrange("p (c t) -> p t c", c=NCC),
                         AX.X, OP.max, apply_absolute_value=True,
                         waits=[("t", scr1_t[0])], inc="v")
            if "v1" in debug and q == 0:
                op(st, "sync", "dma_start", dbg_outs["v1"], V1B[:, :],
                   waits=[("v", t1_v[q])], inc="wsc", incn=16)

        def emit_X1(q):
            p = q % 2
            vmx = max_to_col(t1_v[q], 2 * p)
            vc = opn(st, "vector", "tensor_scalar", cols[:, 2 * p:2 * p + NB],
                     cols[:, 2 * p:2 * p + NB], SQ2048 * 1e-5, None,
                     op0=OP.max, waits=[("v", vmx)], inc="v")
            vc = opn(st, "vector", "reciprocal", cols[:, 8:8 + NB],
                     cols[:, 2 * p:2 * p + NB], inc="v")
            vc = opn(st, "vector", "tensor_scalar", cols[:, 8:8 + NB],
                     cols[:, 8:8 + NB], 127.0, None, op0=OP.mult, inc="v")
            x1_sb[q] = col_to_bcast(8, vc, bc, p * NQ,
                                    rtag="c1r" if q == 0 else None)

        def emit_Q1(q):
            p = q % 2
            vq1_d[q] = quant(V1B, NCC, (32 + p * 16) * NQ, x1_sb[q],
                             bc, p * NQ,
                             extra=[("t", mm5_t_d.get(q - 2, 0))])
            if "q1" in debug and q == 0:
                op(st, "sync", "dma_start", dbg_outs["q1"],
                   QA[:, (32 + p * 16) * NQ:(48 + p * 16) * NQ],
                   waits=[("v", vq1_d[q])], inc="wsc", incn=16)

        ha_prev = [("s", 0)]
        hch_e = {i: ("t", 0) for i in range(4)}

        def emit_M5(q, filler=(), inserts=None):
            p = q % 2
            C = C2[p]
            q1off = (32 + p * 16) * NQ
            inserts = inserts or {}
            fl = list(filler)
            fpos = [0]

            def emit_fill(n):
                while n > 0 and fpos[0] < len(fl):
                    fl[fpos[0]]()
                    fpos[0] += 1
                    n -= 1

            cl_v = {}
            LAG = 3
            h3_last = 0
            for idx in range(32 + LAG):
                if idx in inserts:
                    inserts[idx]()
                if idx < 32:
                    m = MS5[idx]
                    emit_fill(2 if idx < 8 else (1 if idx % 3 == 0 else 0))
                    b = idx % 2
                    wld = [pan_free8[b]]
                    if q == 0 and idx < 2:
                        wld += tern_up_done
                    dv = op(st, "gpsimd", "dma_start", pant[b][:, 0:2048],
                            d_stup[:, m * 2048:(m + 1) * 2048],
                            waits=wld, inc=f"wt{b}", incn=16)
                    bk = idx % 4
                    w0 = [(f"wt{b}", dv)] + st.bw(bk)
                    if idx == 0:
                        w0.append(("v", vq1_d[q]))
                    for c in range(NCC):
                        lhs = pant[b][:, c * 128:(c + 1) * 128]
                        tv = op(st, "tensor", "matmul", ps[bk][:, 0:NQ], lhs,
                                QA[:, q1off + c * NQ:q1off + (c + 1) * NQ],
                                start=(c == 0), stop=(c == NCC - 1),
                                waits=w0 if c == 0 else [],
                                inc="t" if c == NCC - 1 else None)
                    pan_free8[b] = ("t", tv)
                    sw = [("t", tv)]
                    if m == 0:
                        sw.append(out_edge.get(q - 2))
                    op(st, "scalar", "activation", rel8[idx % 2][:, :],
                       ps[bk][:, 0:NQ], AF.Relu, waits=sw, inc="s")
                    g2 = op(st, "scalar", "activation",
                            C[:, m * NQ:(m + 1) * NQ],
                            rel8[idx % 2][:, :], AF.Square, inc="s")
                    st.bset(bk, ("s", g2))
                    sl = idx % 4
                    ch = op(st, "scalar", "copy",
                            CHB[:, sl * NQ:(sl + 1) * NQ],
                            C[:, m * NQ:(m + 1) * NQ],
                            waits=[hch_e[sl]], inc="s")
                    cl_v[idx] = op(st, "vector", "tensor_sub",
                                   CLB[:, sl * NQ:(sl + 1) * NQ],
                                   C[:, m * NQ:(m + 1) * NQ],
                                   CHB[:, sl * NQ:(sl + 1) * NQ],
                                   waits=[("s", ch), hch_e[sl]], inc="v")
                if idx >= LAG:
                    jj = idx - LAG
                    mm = MS5[jj]
                    sl = jj % 4
                    hf = jj % 2
                    wtm = [("v", cl_v[jj])]
                    if ha_prev[0][1] > 0:
                        wtm.append(ha_prev[0])
                    op(st, "tensor", "matmul",
                       phalo[:, hf * NQ:(hf + 1) * NQ], c_h128[:, :],
                       CHB[:, sl * NQ:(sl + 1) * NQ], start=True, stop=False,
                       waits=wtm)
                    th = op(st, "tensor", "matmul",
                            phalo[:, hf * NQ:(hf + 1) * NQ], c_h128[:, :],
                            CLB[:, sl * NQ:(sl + 1) * NQ], start=False,
                            stop=True, inc="t")
                    hch_e[sl] = ("t", th)
                    h3_last = op(st, "scalar", "copy",
                                 C[:, mm * NQ:(mm + 1) * NQ],
                                 phalo[:, hf * NQ:(hf + 1) * NQ],
                                 waits=[("t", th)], inc="s")
                    ha_prev[0] = ("s", h3_last)
            emit_fill(len(fl))
            h3_s[q] = h3_last
            mm5_t_d[q] = st.val("t")

        def b3t3_thunks(q):
            p = q % 2
            buf = C2[p]
            thunks = []
            s = 16
            first = True
            while s >= 1:
                nblk = 32 // (2 * s)
                width = s * NQ
                av = sview(buf, 0, nblk, 2 * s * NQ, width)
                bv = sview(buf, width, nblk, 2 * s * NQ, width)
                def t1(av=av, bv=bv, first=first, q=q):
                    w0 = [("s", h3_s[q])] if first else []
                    op(st, "vector", "tensor_add", av, av, bv, waits=w0,
                       inc="v")

                def t2(av=av, bv=bv):
                    op(st, "vector", "scalar_tensor_tensor", bv, bv, -2.0, av,
                       op0=OP.mult, op1=OP.add, inc="v")

                thunks += [t1, t2]
                first = False
                s //= 2

            def t3(q=q, p=p):
                t3_v[q] = op(st, "vector", "tensor_reduce",
                             SCR1[:, 0:NQ].rearrange("p t -> p t ()"),
                             C2[p][:, :].rearrange("p (c t) -> p t c", c=32),
                             AX.X, OP.max, apply_absolute_value=True,
                             waits=[("t", scr1_t[0])], inc="v")

            thunks.append(t3)
            return thunks

        def emit_X3(q):
            p = q % 2
            vmx = max_to_col(t3_v[q], 12)
            vc = opn(st, "vector", "tensor_scalar", cols[:, 16:16 + NB],
                     cols[:, 2 * p:2 * p + NB], gcol[:, 6:7], None, op0=OP.mult,
                     waits=[("v", vmx)], inc="v")
            vc = opn(st, "vector", "tensor_mul", cols[:, 18:18 + NB],
                     cols[:, 16:16 + NB], cols[:, 16:16 + NB], inc="v")
            vc = opn(st, "vector", "reciprocal", cols[:, 20:20 + NB],
                     cols[:, 18:18 + NB], inc="v")
            vc = opn(st, "vector", "tensor_scalar", cols[:, 20:20 + NB],
                     cols[:, 20:20 + NB], 6.4e-4, None, op0=OP.mult, inc="v")
            vc = opn(st, "vector", "tensor_max", cols[:, 12:12 + NB],
                     cols[:, 12:12 + NB], cols[:, 20:20 + NB], inc="v")
            vc = opn(st, "vector", "reciprocal", cols[:, 20:20 + NB],
                     cols[:, 12:12 + NB], inc="v")
            vc2 = opn(st, "vector", "tensor_scalar", cols[:, 20:20 + NB],
                      cols[:, 20:20 + NB], 127.0, None, op0=OP.mult, inc="v")
            sb_c2 = col_to_bcast(20, vc2, bc2, 0)
            va = opn(st, "vector", "tensor_mul", cols[:, 16:16 + NB],
                     cols[:, 18:18 + NB], cols[:, 12:12 + NB], inc="v")
            va = opn(st, "vector", "tensor_scalar", cols[:, 16:16 + NB],
                     cols[:, 16:16 + NB], gcol[:, 7:8], None, op0=OP.mult,
                     inc="v")
            col_to_bcast(16, va, bc2, NQ, rtag="alr" if q == 0 else None)
            return sb_c2

        def emit_Q2(q, sb_c2):
            p = q % 2
            vq2_d[q] = quant(C2[p], 32, 0, sb_c2, bc2, 0,
                             extra=[("t", mm6_t_d.get(q - 1, 0))])
            if "q2" in debug and q == 0:
                op(st, "sync", "dma_start", dbg_outs["q2"], QA[:, 0:32 * NQ],
                   waits=[("v", vq2_d[q])], inc="wsc", incn=16)

        def emit_M6(q, inserts=None):
            p = q % 2
            C = C2[p]
            inserts = inserts or {}
            ev = 0
            pend = []
            for m in range(16):
                if m in inserts:
                    inserts[m]()
                b = m % 2
                wld = [pan_free8[b]]
                if q == 0 and m < 2:
                    wld += tern_dn_done
                dv = op(st, "gpsimd", "dma_start", pant[b][:, :],
                        d_stdn[:, m * 4096:(m + 1) * 4096],
                        waits=wld, inc=f"wt{b}", incn=16)
                bk = m % 4
                w0 = [(f"wt{b}", dv)] + st.bw(bk)
                if m == 0:
                    w0.append(("v", vq2_d[q]))
                for c in range(32):
                    lhs = pant[b][:, c * 128:(c + 1) * 128]
                    tv = op(st, "tensor", "matmul", ps[bk][:, 0:NQ], lhs,
                            QA[:, c * NQ:(c + 1) * NQ], start=(c == 0),
                            stop=(c == 31), waits=w0 if c == 0 else [],
                            inc="t" if c == 31 else None)
                pan_free8[b] = ("t", tv)
                sw = [("t", tv)]
                if m == 0:
                    sw.append(("v", vq2_d[q]))
                ev = op(st, "scalar", "copy", C[:, m * NQ:(m + 1) * NQ],
                        ps[bk][:, 0:NQ], waits=sw, inc="s")
                st.bset(bk, ("s", ev))
                op(st, "vector", "tensor_mul", C[:, m * NQ:(m + 1) * NQ],
                   C[:, m * NQ:(m + 1) * NQ], bc2[:, NQ:2 * NQ],
                   waits=[("s", ev)], inc="v")
                vv2 = op(st, "vector", "tensor_add",
                         C[:, m * NQ:(m + 1) * NQ],
                         C[:, m * NQ:(m + 1) * NQ],
                         X1Q[q % 3][:, m * NQ:(m + 1) * NQ], inc="v")
                while pend and (pend[0][0] + 4 <= m or m == 15):
                    pm, sof, sv = pend.pop(0)
                    wstat = [("s", sv)]
                    if pm == 1:
                        wstat += st.bw("s1", "s2")
                    for j in range(2):
                        op(st, "tensor", "matmul", pstat1[0:1, 0:NQ],
                           c_onb[:, 0:1],
                           sqs[0][:, sof + j * NQ:sof + (j + 1) * NQ],
                           start=(pm == 1 and j == 0),
                           stop=(pm == 15 and j == 1),
                           waits=wstat if j == 0 else [])
                        tv2 = op(st, "tensor", "matmul", pstat2[0:1, 0:NQ],
                                 c_onb[:, 0:1],
                                 sqs[1][:, sof + j * NQ:sof + (j + 1) * NQ],
                                 start=(pm == 1 and j == 0),
                                 stop=(pm == 15 and j == 1),
                                 inc="t" if j == 1 else None)
                    ln2_t.append(tv2)
                if m % 2 == 1:
                    sof = (m // 2 % 2) * 2 * NQ
                    gsl = C[:, (m - 1) * NQ:(m + 1) * NQ]
                    sv = op(st, "scalar", "copy", sqs[0][:, sof:sof + 2 * NQ],
                            gsl, waits=[("v", vv2)] +
                                  ([("t", ln2_t[-1])] if ln2_t else []),
                            inc="s")
                    sv = op(st, "scalar", "activation",
                            sqs[1][:, sof:sof + 2 * NQ],
                            gsl, AF.Square, inc="s")
                    pend.append((m, sof, sv))
                    if m == 15:
                        pm, sof, sv = pend.pop(0)
                        wstat = [("s", sv)]
                        for j in range(2):
                            op(st, "tensor", "matmul", pstat1[0:1, 0:NQ],
                               c_onb[:, 0:1],
                               sqs[0][:, sof + j * NQ:sof + (j + 1) * NQ],
                               start=False, stop=(j == 1),
                               waits=wstat if j == 0 else [])
                            tv2 = op(st, "tensor", "matmul",
                                     pstat2[0:1, 0:NQ], c_onb[:, 0:1],
                                     sqs[1][:, sof + j * NQ:sof + (j + 1) * NQ],
                                     start=False, stop=(j == 1),
                                     inc="t" if j == 1 else None)
                        ln2_t.append(tv2)
            mm6_t_d[q] = st.val("t")
            m6_s[q] = ev
            xa_v[q] = vv2
            if "x2pre" in debug and q == 0:
                op(st, "sync", "dma_start", dbg_outs["x2pre"],
                   C[:, 0:NCC * NQ], waits=[("v", vv2)], inc="wsc", incn=16)

        def emit_L2(q):
            p = q % 2
            C = C2[p]
            c3 = C[:, 0:NCC * NQ].rearrange("p (c t) -> p c t", c=NCC)
            v1 = opn(st, "vector", "tensor_scalar", rows3[0:1, 0:NQ],
                     pstat1[0:1, 0:NQ], 1.0 / 2048.0, None, op0=OP.mult,
                     waits=[("t", ln2_t[-1])], inc="v")
            v1 = opn(st, "vector", "tensor_scalar", rows[0:1, 0:NQ],
                     pstat2[0:1, 0:NQ], 1.0 / 2048.0, None, op0=OP.mult,
                     inc="v")
            v1 = opn(st, "vector", "tensor_mul", rows2[0:1, 0:NQ],
                     rows3[0:1, 0:NQ], rows3[0:1, 0:NQ], inc="v")
            v1 = opn(st, "vector", "scalar_tensor_tensor", rows[0:1, 0:NQ],
                     rows[0:1, 0:NQ], 1e-5, rows2[0:1, 0:NQ],
                     op0=OP.add, op1=OP.subtract, inc="v")
            v1 = opn(st, "vector", "reciprocal", rows2[0:1, 0:NQ],
                     rows[0:1, 0:NQ], inc="v")
            s1 = op(st, "scalar", "activation", rows[0:1, 0:NQ],
                    rows2[0:1, 0:NQ], AF.Sqrt, waits=[("v", v1)], inc="s")
            v1 = opn(st, "vector", "tensor_mul", rows2[0:1, 0:NQ],
                     rows3[0:1, 0:NQ], rows[0:1, 0:NQ], waits=[("s", s1)],
                     inc="v")
            op(st, "tensor", "matmul", pstat1[:, 0:NQ], c_onf[0:1, :],
               rows[0:1, 0:NQ], start=True, stop=True, waits=[("v", v1)])
            tb = op(st, "tensor", "matmul", pstat2[:, 0:NQ], c_onf[0:1, :],
                    rows2[0:1, 0:NQ], start=True, stop=True, inc="t")
            sb_ = op(st, "scalar", "copy", bc2[:, 2 * NQ:3 * NQ],
                     pstat1[:, 0:NQ], waits=[("t", tb)], inc="s")
            sb_ = op(st, "scalar", "copy", bc2[:, 3 * NQ:4 * NQ],
                     pstat2[:, 0:NQ], inc="s")
            st.bset("s1", ("s", sb_)); st.bset("s2", ("s", sb_))
            invrep = rep_ap(bc2, 2 * NQ, NCC)
            muirep = rep_ap(bc2, 3 * NQ, NCC)
            lv = op(st, "vector", "tensor_mul", c3, c3, invrep,
                    waits=[("s", sb_)], inc="v")
            lv = op(st, "vector", "tensor_tensor", c3, c3, muirep,
                    op=OP.subtract, inc="v")
            od = op(st, "sync", "dma_start",
                    d_out.rearrange("(c p) t -> p c t", p=128)[:, :, q * NQ:(q + 1) * NQ],
                    C[:, 0:NCC * NQ].rearrange("p (c t) -> p c t", c=NCC),
                    waits=[("v", lv)], inc="wo0", incn=16)
            out_edge[q] = ("wo0", od)

        st.begin()
        emit_LOAD(0); emit_H1(0); emit_B1(0); emit_T1(0); emit_X1(0); emit_Q1(0)
        sbc2_d = {}
        for k in range(5):
            A = k if k <= 3 else None
            B = k - 1 if k >= 1 else None
            if A is not None:
                ins = {}
                if B is not None and B >= 1:
                    def _l2(B=B):
                        emit_L2(B - 1)
                    ins[2] = _l2
                if A < 3:
                    def _lh(A=A):
                        emit_LOAD(A + 1)
                        emit_H1(A + 1)
                    ins[8] = _lh
                if B is not None:
                    def _x3(B=B):
                        sbc2_d[B] = emit_X3(B)
                    ins[22] = _x3
                emit_M5(A, filler=b3t3_thunks(B) if B is not None else (),
                        inserts=ins)
            elif B is not None:
                emit_L2(B - 1)
                for th in b3t3_thunks(B):
                    th()
                sbc2_d[B] = emit_X3(B)
            if B is not None:
                emit_Q2(B, sbc2_d[B])
                ins6 = {}
                if A is not None and A < 3:
                    def _bt(A=A):
                        emit_B1(A + 1)
                        emit_T1(A + 1)
                    def _x1(A=A):
                        emit_X1(A + 1)
                    ins6[2] = _bt
                    ins6[8] = _x1
                emit_M6(B, inserts=ins6)
                if A is not None and A < 3:
                    emit_Q1(A + 1)
                if B == 3:
                    emit_L2(B)
            elif A is not None and A < 3:
                emit_B1(A + 1)
                emit_T1(A + 1)
                emit_X1(A + 1)
                emit_Q1(A + 1)
        flush(nc, st)

        st.begin()
        wt(st, "sync", "wo0", out_edge[3][1])
        if debug:
            wt(st, "sync", "wsc", st.val("wsc"))
        flush(nc, st)

        ph8.close()

    return nc



def _hadamard128():
    h = np.ones((1, 1), dtype=np.float32)
    while h.shape[0] < 128:
        h = np.block([[h, h], [h, -h]])
    return h


def _prep_weights(W_in, conv_w, conv_b, W_out, W_mlp1, W_mlp2, w_up, w_down):
    c = np.ascontiguousarray

    def panels(WT, ncc, nmc):
        return c(WT.reshape(ncc, 128, nmc, 128).transpose(2, 1, 0, 3)
                 .reshape(nmc, 128, ncc * 128))

    win = panels(W_in.T, 16, 64)
    order = []
    for k in range(32):
        order += [k, 32 + k]
    win = c(win[order])
    wout = panels(W_out.T, 32, 16)
    wm1 = panels(W_mlp1.T, 16, 64)
    wm2 = panels(W_mlp2.T, 64, 16)
    wup = panels(w_up.T, 16, 32)
    wdn = panels(w_down.T, 32, 16)
    cw = c(conv_w[:, 0, :].reshape(32, 128, 4).transpose(1, 2, 0).reshape(128, 128))
    cb = c(conv_b.reshape(32, 128).T)
    h128 = _hadamard128().astype(ml_dtypes.bfloat16)
    identf = np.eye(128, dtype=np.float32)
    onesf = np.ones((128, 128), dtype=np.float32)
    onesb = np.ones((128, 8), dtype=ml_dtypes.bfloat16)
    return dict(win_p=win, wout_p=wout, wm1_p=wm1, wm2_p=wm2, wup_p=wup,
                wdn_p=wdn, c_convw=cw, c_convb=cb, c_h128=h128,
                c_identf=identf, c_onesf=onesf, c_onesb=onesb)


def _prep_core(x, b, h):
    c = np.ascontiguousarray
    xt = c(x[b, h * TC:(h + 1) * TC, :].T)
    if h == 0:
        halo = np.zeros((128, 48), dtype=np.float32)
    else:
        hs = x[b, h * TC - 3:h * TC, :]
        halo = c(hs.T.reshape(16, 128, 3).transpose(1, 0, 2).reshape(128, 48))
    return xt, halo


def kernel(L0_attn_x, L0_attn_W_in, L0_attn_conv_w, L0_attn_conv_b,
           L0_attn_W_out, L0_attn_W_mlp1, L0_attn_W_mlp2,
           L0_ffn_w_up, L0_ffn_w_down):
    x = np.ascontiguousarray(L0_attn_x, dtype=np.float32)
    shared = _prep_weights(
        np.asarray(L0_attn_W_in, np.float32), np.asarray(L0_attn_conv_w, np.float32),
        np.asarray(L0_attn_conv_b, np.float32), np.asarray(L0_attn_W_out, np.float32),
        np.asarray(L0_attn_W_mlp1, np.float32), np.asarray(L0_attn_W_mlp2, np.float32),
        np.asarray(L0_ffn_w_up, np.float32), np.asarray(L0_ffn_w_down, np.float32))

    nc = build(debug=tuple(DBG))
    in_maps = []
    for i in range(8):
        b, hh = i // 2, i % 2
        xt, halo = _prep_core(x, b, hh)
        m = dict(shared)
        m["x_t"] = xt
        m["halo_p"] = halo
        in_maps.append(m)

    res = run_bass_kernel_spmd(nc, in_maps, core_ids=list(range(8)))
    out = np.empty((4, 2048, 2048), dtype=np.float32)
    for i in range(8):
        b, hh = i // 2, i % 2
        out[b, hh * TC:(hh + 1) * TC, :] = res.results[i]["out"].T
    return out
